# version 05
# baseline (speedup 1.0000x reference)
import math
import numpy as np
from contextlib import ExitStack

import concourse.bass as bass
import concourse.mybir as mybir
from concourse.bass_utils import run_bass_kernel_spmd
from concourse.bass import IndirectOffsetOnAxis

F32 = mybir.dt.float32
BF16 = mybir.dt.bfloat16
F8 = mybir.dt.float8e4
I32 = mybir.dt.int32
AF = mybir.ActivationFunctionType
ALU = mybir.AluOpType
AX = mybir.AxisListType

S = 2048
D = 2048
TT = 16
IC = 16
IFF = 5632
NJT = 44
NH = 16
HD = 128
NEG = -1.0e9


class Env:
    def __init__(self, nc, block, sems):
        self.nc = nc
        self.block = block
        self.s = sems
        self.c = {k: 0 for k in sems}

    def inc(self, inst, k, by=1):
        inst.then_inc(self.s[k], by)
        self.c[k] += by
        return self.c[k]

    def barrier(self):
        self.c['PH'] += 5
        tgt = self.c['PH']
        for attr in ('tensor', 'vector', 'scalar', 'gpsimd', 'sync'):
            def mk(eng, tgt=tgt):
                eng.nop().then_inc(self.s['PH'], 1)
                eng.wait_ge(self.s['PH'], tgt)
            getattr(self.block, attr)(mk)


_DEBUG_SPILL = [False]


def build_nc():
    nc = bass.Bass("TRN2", target_bir_lowering=False, debug=False)
    d = {}
    def inp(name, shape, dt=F32):
        d[name] = nc.dram_tensor(name, shape, dt, kind="ExternalInput").ap()
    inp("x0", [S, D])
    inp("l0wq", [D, D]); inp("l0wk", [D, D]); inp("l0wv", [D, D]); inp("l0wo", [D, D])
    inp("wg", [IFF, D]); inp("wu", [IFF, D]); inp("wd", [D, IFF])
    inp("dnidx", [IFF, 1], I32)
    inp("l1wq", [D, D]); inp("l1wk", [D, 512]); inp("l1wv", [D, 512]); inp("l1wo", [D, D])
    inp("lam", [1, NH]); inp("wr", [8, D]); inp("wex", [8 * D, D])
    inp("cosT", [S, HD]); inp("sinF", [S, HD])
    out_ap = nc.dram_tensor("out", [S, D], F32, kind="ExternalOutput").ap()
    def scr(name, shape, dt):
        if _DEBUG_SPILL[0]:
            d[name] = nc.dram_tensor(name, shape, dt, kind="ExternalOutput").ap()
        else:
            d[name] = nc.dram_tensor(name, shape, dt).ap()
    scr("sp_q", [D, S], BF16)
    scr("sp_k", [D, S], BF16)
    scr("sp_v", [S, D], BF16)
    scr("sp_ao", [D, S], BF16)
    scr("sp_s", [S, D], BF16)
    scr("sp_T", [S, IFF], F32)
    scr("sp_qtT", [IFF, S], BF16)
    scr("sp_x1", [S, D], F32)
    if _DEBUG_SPILL[0]:
        dbg_a = nc.dram_tensor("dbg_a", [128, 32768], BF16, kind="ExternalOutput").ap()
        dbg_c = nc.dram_tensor("dbg_c", [128, 32768], BF16, kind="ExternalOutput").ap()
        dbg_q = nc.dram_tensor("dbg_q", [D, S], BF16, kind="ExternalOutput").ap()
        dbg_ao = nc.dram_tensor("dbg_ao", [D, S], BF16, kind="ExternalOutput").ap()
        dbg_s = nc.dram_tensor("dbg_s", [S, D], BF16, kind="ExternalOutput").ap()

    with ExitStack() as ctx:
        def sb(n, s_, dt):
            return ctx.enter_context(nc.sbuf_tensor(n, s_, dt))
        bigA = sb("bigA", [128, 32768], BF16)
        bigC = sb("bigC", [128, 32768], BF16)
        wstage = sb("wstage", [128, 5632], F32)
        wtrans = sb("wtrans", [128, 16384], BF16)
        f8s = sb("f8s", [128, IFF], F8)
        evb = sb("evb", [128, 512], BF16)
        evb2 = sb("evb2", [128, 512], BF16)
        sg0 = sb("sg0", [128, 512], F32)
        sg1 = sb("sg1", [128, 512], F32)
        sx0 = sb("sx0", [128, 512], F32)
        ssb = sb("ssb", [128, 512], BF16)
        stats = sb("stats", [128, 64], F32)
        selm = sb("selm", [128, 128], F32)
        tmaxs = sb("tmaxs", [128, 16], F32)
        ident = sb("ident", [128, 128], BF16)
        maskc = sb("maskc", [128, 128], F32)
        maska = sb("maska", [128, 128], F32)
        lamb = sb("lamb", [128, NH], F32)
        onesc = sb("onesc", [1, 128], F32)
        lams = sb("lams", [1, NH], F32)
        idxs = sb("idxs", [128, NJT], I32)

        stgb = wstage[:, 4096:5120].bitcast(BF16)
        stg5 = bigC[:, 0:5632]

        psA = ctx.enter_context(nc.psum_tensor("psA", [128, 2048], F32))
        psB = ctx.enter_context(nc.psum_tensor("psB", [128, 512], F32))
        psC = ctx.enter_context(nc.psum_tensor("psC", [128, 512], F32))
        ptA = ctx.enter_context(nc.psum_tensor("ptA", [128, 1024], BF16))
        ptB = ctx.enter_context(nc.psum_tensor("ptB", [128, 1024], BF16))

        sems = {}
        for name in ('LD', 'LG', 'PE', 'DV', 'AC', 'GP', 'PH', 'OT'):
            sems[name] = ctx.enter_context(nc.semaphore(name))
        block = ctx.enter_context(nc.Block())
        env = Env(nc, block, sems)
        B = env.block

        def transpose_blocks(src_ap_fn, n128, dst_fn, wait_dv=None, wait_extra=None):
            for g0 in range(0, n128, 16):
                n = min(16, n128 - g0)
                @B.tensor
                def _(t, g0=g0, n=n):
                    if g0 == 0:
                        if wait_dv is not None:
                            t.wait_ge(env.s['DV'], wait_dv)
                        if wait_extra is not None:
                            wait_extra(t)
                    t.wait_ge(env.s['AC'], env.c['AC'])
                    for j in range(n):
                        i = g0 + j
                        pt = ptA if j < 8 else ptB
                        env.inc(t.transpose(pt[:, (j % 8) * 128:(j % 8 + 1) * 128],
                                            src_ap_fn(i), ident[:]), 'PE', 1)
                pet = env.c['PE']
                @B.scalar
                def _(sc, g0=g0, n=n, pet=pet):
                    sc.wait_ge(env.s['PE'], pet)
                    for j in range(n):
                        i = g0 + j
                        pt = ptA if j < 8 else ptB
                        env.inc(sc.copy(dst_fn(i), pt[:, (j % 8) * 128:(j % 8 + 1) * 128]), 'AC', 1)
            return env.c['AC']

        @B.sync
        def _(sy):
            env.inc(sy.dma_start(out=idxs[:], in_=d["dnidx"].rearrange("(a b) c -> b (a c)", b=128),
                                 allow_slow_non_contiguous=True), 'LD', 16)
            env.inc(sy.dma_start(out=lams[:], in_=d["lam"]), 'LD', 16)
        ld_init = env.c['LD']

        @B.gpsimd
        def _(gp):
            gp.memset(ident[:], 1.0)
            gp.affine_select(ident[:], ident[:], [[-1, 128]], ALU.is_equal, 0.0,
                             base=0, channel_multiplier=1)
            gp.memset(maskc[:], 0.0)
            gp.affine_select(maskc[:], maskc[:], [[-1, 128]], ALU.is_ge, NEG,
                             base=0, channel_multiplier=1)
            gp.memset(maska[:], 0.0)
            env.inc(gp.affine_select(maska[:], maska[:], [[1, 128]], ALU.is_ge, NEG,
                                     base=-1, channel_multiplier=-1), 'GP', 1)

        @B.vector
        def _(v):
            v.memset(onesc[:], 1.0)
            env.inc(v.memset(stats[:, 5:6], 1e-5), 'DV', 1)


        @B.tensor
        def _(t):
            t.wait_ge(env.s['LD'], ld_init)
            t.wait_ge(env.s['DV'], env.c['DV'])
            env.inc(t.matmul(psB[:, 0:NH], onesc[:], lams[:], start=True, stop=True), 'PE', 1)

        @B.vector
        def _(v):
            v.wait_ge(env.s['PE'], env.c['PE'])
            env.inc(v.tensor_copy(lamb[:], psB[:, 0:NH]), 'DV', 1)

        env.barrier()

        x0_ap = d["x0"]
        wr2t = wtrans[:, 16128:16384]

        def ln_quant_phase(x_src, layer):
            for tt in range(TT):
                xw = wstage[:, 0:2048]
                xnf = wstage[:, 2048:4096]
                mu = stats[:, 0:1]; ss = stats[:, 1:2]; var = stats[:, 2:3]
                sd = stats[:, 3:4]; r0 = stats[:, 4:5]
                t1 = stats[:, 6:7]; t2 = stats[:, 7:8]
                rmax = stats[:, 8:9]; q1 = stats[:, 9:10]; q2 = stats[:, 10:11]

                @B.sync
                def _(sy, tt=tt):
                    if tt >= 1:
                        sy.wait_ge(env.s['DV'], env.c['DV'])
                        sy.wait_ge(env.s['AC'], env.c['AC'])
                    if layer == 1:
                        sy.wait_ge(env.s['LG'], env.c['LG'])
                    env.inc(sy.dma_start(out=xw, in_=x_src[tt * 128:(tt + 1) * 128, :]), 'LD', 16)
                ldx = env.c['LD']

                @B.vector
                def _(v):
                    v.wait_ge(env.s['LD'], ldx)
                    env.inc(v.reduce_sum(mu, xw, axis=AX.X), 'DV', 1)

                @B.scalar
                def _(sc):
                    sc.wait_ge(env.s['LD'], ldx)
                    env.inc(sc.activation(wtrans[:, 0:2048], xw, AF.Square, bias=0.0, scale=1.0,
                                          accum_out=ss), 'AC', 1)
                acs = env.c['AC']

                @B.vector
                def _(v):
                    v.wait_ge(env.s['AC'], acs)
                    v.tensor_scalar_mul(mu, mu, 1.0 / 2048.0)
                    v.drain()
                    v.tensor_scalar_mul(var, ss, 1.0 / 2048.0)
                    v.tensor_mul(t1, mu, mu)
                    v.drain()
                    env.inc(v.tensor_sub(var, var, t1), 'DV', 1)
                    v.drain()
                dvv = env.c['DV']

                @B.scalar
                def _(sc):
                    sc.wait_ge(env.s['DV'], dvv)
                    env.inc(sc.activation(sd, var, AF.Sqrt, bias=stats[:, 5:6], scale=1.0), 'AC', 1)
                acsd = env.c['AC']

                @B.vector
                def _(v):
                    v.wait_ge(env.s['AC'], acsd)
                    v.reciprocal(r0, sd)
                    v.drain()
                    v.tensor_scalar_add(t2, var, 1e-5)
                    v.drain()
                    for _i in range(2):
                        v.tensor_mul(t1, r0, r0)
                        v.drain()
                        v.tensor_mul(t1, t1, t2)
                        v.drain()
                        v.tensor_scalar(t1, t1, -0.5, 1.5, op0=ALU.mult, op1=ALU.add)
                        v.drain()
                        v.tensor_mul(r0, r0, t1)
                        v.drain()
                    v.tensor_scalar(xnf, xw, mu, r0, op0=ALU.subtract, op1=ALU.mult)
                    v.drain()
                    env.inc(v.tensor_copy(stgb[:], xnf), 'DV', 1)
                dvxn = env.c['DV']

                transpose_blocks(lambda i: stgb[:, i * 128:(i + 1) * 128], IC,
                                 lambda i, tt=tt: bigA[:, i * 2048 + tt * 128: i * 2048 + tt * 128 + 128],
                                 wait_dv=dvxn)
                acev = env.c['AC']

                if layer == 0:
                    @B.vector
                    def _(v):
                        v.wait_ge(env.s['DV'], dvxn)
                        v.wait_ge(env.s['AC'], acev)
                        env.inc(v.reduce_max(rmax, xnf, axis=AX.X, apply_absolute_value=True), 'DV', 1)
                        v.drain()
                        v.reciprocal(q1, rmax)
                        v.drain()
                        v.tensor_scalar_mul(q1, q1, 224.0)
                        v.drain()
                        v.tensor_scalar_mul(q2, rmax, 1.0 / 224.0)
                        v.drain()
                        env.inc(v.tensor_scalar_mul(f8s[:, 0:2048], xnf, q1), 'DV', 1)
                        v.drain()
                        env.inc(v.tensor_scalar_mul(stgb[:], f8s[:, 0:2048], q2), 'DV', 1)
                    dvq = env.c['DV']
                    transpose_blocks(lambda i: stgb[:, i * 128:(i + 1) * 128], IC,
                                     lambda i, tt=tt: bigC[:, i * 2048 + tt * 128: i * 2048 + tt * 128 + 128],
                                     wait_dv=dvq)
                else:
                    lob = stg5[:, 0:2048]
                    @B.vector
                    def _(v):
                        v.wait_ge(env.s['AC'], acev)
                        v.drain()
                        env.inc(v.tensor_tensor(lob, xnf, stgb[:], op=ALU.subtract), 'DV', 1)
                        v.drain()
                    dvlo = env.c['DV']
                    transpose_blocks(lambda i: lob[:, i * 128:(i + 1) * 128], IC,
                                     lambda i: stg5[:, 2048 + i * 128: 2048 + (i + 1) * 128],
                                     wait_dv=dvlo)
                    aclo = env.c['AC']

                    @B.tensor
                    def _(t, tt=tt):
                        t.wait_ge(env.s['AC'], aclo)
                        for i in range(IC):
                            t.matmul(psB[:, 0:16], bigA[:, i * 2048 + tt * 128: i * 2048 + tt * 128 + 128],
                                     wr2t[:, i * 16:(i + 1) * 16], start=(i == 0), stop=False)
                        for i in range(IC):
                            env.inc(t.matmul(psB[:, 0:16], stg5[:, 2048 + i * 128:2048 + (i + 1) * 128],
                                             wr2t[:, i * 16:(i + 1) * 16],
                                             start=False, stop=(i == IC - 1)), 'PE', 1)
                    perl = env.c['PE']

                    @B.vector
                    def _(v, tt=tt):
                        v.wait_ge(env.s['PE'], perl)
                        lg = stats[:, 16:24]
                        m8 = stats[:, 24:32]
                        v.tensor_copy(lg, psB[:, 0:8])
                        v.drain()
                        v.tensor_tensor(lg, lg, psB[:, 8:16], op=ALU.add)
                        v.drain()
                        v.max(m8, lg)
                        v.drain()
                        nt = stats[:, 32:33]
                        v.memset(nt, 1.0)
                        v.drain()
                        for e in range(8):
                            he = stats[:, 33:34]
                            v.tensor_tensor(he, lg[:, e:e + 1], m8[:, 0:1], op=ALU.is_equal)
                            v.drain()
                            v.tensor_mul(selm[:, tt * 8 + e: tt * 8 + e + 1], he, nt)
                            v.drain()
                            v.tensor_sub(nt, nt, selm[:, tt * 8 + e: tt * 8 + e + 1])
                            v.drain()
                        env.inc(v.engine_nop(), 'DV', 1)

        ln_quant_phase(x0_ap, 0)
        env.barrier()
        if _DEBUG_SPILL[0]:
            @B.gpsimd
            def _(gp):
                env.inc(gp.dma_start(out=dbg_a, in_=bigA[:]), 'LG', 16)
                env.inc(gp.dma_start(out=dbg_c, in_=bigC[:]), 'LG', 16)
                gp.wait_ge(env.s['LG'], env.c['LG'])
            env.barrier()

        def l0_proj_fm(w_ap, sp_dst):
            for oc in range(16):
                buf = oc % 2
                ws = wstage[:, buf * 2048:(buf + 1) * 2048]
                st0 = 8192 * buf
                @B.sync
                def _(sy, oc=oc):
                    sy.wait_ge(env.s['DV'], env.c['DV'])
                    env.inc(sy.dma_start(out=ws, in_=w_ap[oc * 128:(oc + 1) * 128, :]), 'LD', 16)
                ldw = env.c['LD']
                @B.vector
                def _(v, oc=oc):
                    v.wait_ge(env.s['LD'], ldw)
                    v.wait_ge(env.s['PE'], env.c['PE'])
                    env.inc(v.tensor_copy(stgb[:], ws), 'DV', 1)
                dvc = env.c['DV']
                def wx(t):
                    pass
                transpose_blocks(lambda i: stgb[:, i * 128:(i + 1) * 128], IC,
                                 lambda i, st0=st0: wtrans[:, st0 + i * 128: st0 + (i + 1) * 128],
                                 wait_dv=dvc)
                acst = env.c['AC']
                for tc in range(4):
                    ps = psB if tc % 2 == 0 else psC
                    @B.tensor
                    def _(t, tc=tc, oc=oc, st0=st0):
                        t.wait_ge(env.s['AC'], acst)
                        if tc >= 2 or oc > 0:
                            t.wait_ge(env.s['AC'], env.c['AC'])
                        for i in range(IC):
                            env.inc(t.matmul(ps[:], wtrans[:, st0 + i * 128: st0 + (i + 1) * 128],
                                             bigA[:, i * 2048 + tc * 512: i * 2048 + (tc + 1) * 512],
                                             start=(i == 0), stop=(i == IC - 1)), 'PE', 1)
                    pemm = env.c['PE']
                    eb = evb if tc % 2 == 0 else evb2
                    @B.scalar
                    def _(sc):
                        sc.wait_ge(env.s['PE'], pemm)
                        sc.wait_ge(env.s['LG'], env.c['LG'])
                        env.inc(sc.copy(eb[:, 0:512], ps[:]), 'AC', 1)
                    acev2 = env.c['AC']
                    @B.gpsimd
                    def _(gp, oc=oc, tc=tc, eb=eb):
                        gp.wait_ge(env.s['AC'], acev2)
                        env.inc(gp.dma_start(out=sp_dst[oc * 128:(oc + 1) * 128, tc * 512:(tc + 1) * 512],
                                             in_=eb[:, 0:512]), 'LG', 16)

        l0_proj_fm(d["l0wq"], d["sp_q"])
        l0_proj_fm(d["l0wk"], d["sp_k"])

        for oc4 in range(4):
            for rt in range(4):
                ws = wstage[:, (rt % 2) * 2048:(rt % 2 + 1) * 2048]
                @B.sync
                def _(sy):
                    sy.wait_ge(env.s['DV'], env.c['DV'])
                    env.inc(sy.dma_start(out=ws, in_=d["l0wv"][(oc4 * 4 + rt) * 128:(oc4 * 4 + rt + 1) * 128, :]), 'LD', 16)
                ldw = env.c['LD']
                @B.vector
                def _(v):
                    v.wait_ge(env.s['LD'], ldw)
                    v.wait_ge(env.s['PE'], env.c['PE'])
                    env.inc(v.tensor_copy(stgb[:], ws), 'DV', 1)
                dvc = env.c['DV']
                transpose_blocks(lambda i: stgb[:, i * 128:(i + 1) * 128], IC,
                                 lambda i, rt=rt: wtrans[:, i * 512 + rt * 128: i * 512 + rt * 128 + 128],
                                 wait_dv=dvc)
            acst = env.c['AC']
            for tt in range(TT):
                ps = psB if tt % 2 == 0 else psC
                @B.tensor
                def _(t, tt=tt, oc4=oc4):
                    t.wait_ge(env.s['AC'], acst)
                    if tt >= 2 or oc4 > 0:
                        t.wait_ge(env.s['AC'], env.c['AC'])
                    for i in range(IC):
                        env.inc(t.matmul(ps[:], bigA[:, i * 2048 + tt * 128: i * 2048 + tt * 128 + 128],
                                         wtrans[:, i * 512:(i + 1) * 512],
                                         start=(i == 0), stop=(i == IC - 1)), 'PE', 1)
                pemm = env.c['PE']
                eb = evb if tt % 2 == 0 else evb2
                @B.scalar
                def _(sc):
                    sc.wait_ge(env.s['PE'], pemm)
                    sc.wait_ge(env.s['LG'], env.c['LG'])
                    env.inc(sc.copy(eb[:, 0:512], ps[:]), 'AC', 1)
                acev2 = env.c['AC']
                @B.gpsimd
                def _(gp, tt=tt, oc4=oc4, eb=eb):
                    gp.wait_ge(env.s['AC'], acev2)
                    env.inc(gp.dma_start(out=d["sp_v"][tt * 128:(tt + 1) * 128, oc4 * 512:(oc4 + 1) * 512],
                                         in_=eb[:, 0:512]), 'LG', 16)

        env.barrier()

        qh = bigA[:, 0:2048]
        kh = bigA[:, 2048:4096]
        vh = bigA[:, 4096:6144]
        ptl = bigA[:, 6144:8192]
        pTs = bigA[:, 8192:16384]
        for h in range(NH):
            @B.sync
            def _(sy, h=h):
                if h >= 1:
                    sy.wait_ge(env.s['PE'], env.c['PE'])
                sy.wait_ge(env.s['LG'], env.c['LG'])
                env.inc(sy.dma_start(out=qh, in_=d["sp_q"][h * 128:(h + 1) * 128, :]), 'LD', 16)
                env.inc(sy.dma_start(out=kh, in_=d["sp_k"][h * 128:(h + 1) * 128, :]), 'LD', 16)
                env.inc(sy.dma_start(
                    out=vh.rearrange("p (a c) -> p a c", a=TT),
                    in_=d["sp_v"].rearrange("(a p) o -> p a o", p=128)[:, :, h * 128:(h + 1) * 128]), 'LD', 16)
            ldh = env.c['LD']

            for qg in range(4):
                for qi in range(4):
                    qt = qg * 4 + qi
                    @B.tensor
                    def _(t, qt=qt, h=h):
                        t.wait_ge(env.s['LD'], ldh)
                        if not (h == 0 and qt == 0):
                            t.wait_ge(env.s['DV'], env.c['DV'])
                        for kc in range(4):
                            env.inc(t.matmul(psA[:, kc * 512:(kc + 1) * 512],
                                             qh[:, qt * 128:(qt + 1) * 128],
                                             kh[:, kc * 512:(kc + 1) * 512],
                                             start=True, stop=True), 'PE', 1)
                    pesc = env.c['PE']

                    m0 = stats[:, 40:41]; nm = stats[:, 41:42]
                    zz = stats[:, 42:43]; zi = stats[:, 43:44]
                    @B.vector
                    def _(v):
                        v.wait_ge(env.s['PE'], pesc)
                        v.reduce_max(m0, psA[:], axis=AX.X)
                        v.drain()
                        env.inc(v.tensor_scalar_mul(nm, m0, -1.0 / math.sqrt(128.0)), 'DV', 1)
                        v.drain()
                    dvm = env.c['DV']

                    @B.scalar
                    def _(sc):
                        sc.wait_ge(env.s['DV'], dvm)
                        env.inc(sc.activation(ptl[:], psA[:], AF.Exp,
                                              bias=nm, scale=1.0 / math.sqrt(128.0),
                                              accum_out=zz), 'AC', 1)
                    acex = env.c['AC']

                    @B.vector
                    def _(v):
                        v.wait_ge(env.s['AC'], acex)
                        v.reciprocal(zi, zz)
                        v.drain()
                        env.inc(v.tensor_scalar_mul(ptl[:], ptl[:], zi), 'DV', 1)
                    dvp = env.c['DV']

                    transpose_blocks(lambda i: ptl[:, i * 128:(i + 1) * 128], 16,
                                     lambda i, qi=qi: pTs[:, i * 512 + qi * 128: i * 512 + qi * 128 + 128],
                                     wait_dv=dvp)
                acpt = env.c['AC']

                ps = psB if qg % 2 == 0 else psC
                @B.tensor
                def _(t, qg=qg, h=h):
                    t.wait_ge(env.s['AC'], acpt)
                    if qg >= 2 or h > 0:
                        t.wait_ge(env.s['AC'], env.c['AC'])
                    for kt in range(16):
                        env.inc(t.matmul(ps[:], vh[:, kt * 128:(kt + 1) * 128],
                                         pTs[:, kt * 512:(kt + 1) * 512],
                                         start=(kt == 0), stop=(kt == 15)), 'PE', 1)
                pepv = env.c['PE']

                eb = evb if qg % 2 == 0 else evb2
                @B.scalar
                def _(sc):
                    sc.wait_ge(env.s['PE'], pepv)
                    sc.wait_ge(env.s['LG'], env.c['LG'])
                    env.inc(sc.copy(eb[:, 0:512], ps[:]), 'AC', 1)
                acao = env.c['AC']
                @B.gpsimd
                def _(gp, h=h, qg=qg, eb=eb):
                    gp.wait_ge(env.s['AC'], acao)
                    env.inc(gp.dma_start(out=d["sp_ao"][h * 128:(h + 1) * 128, qg * 512:(qg + 1) * 512],
                                         in_=eb[:, 0:512]), 'LG', 16)

        env.barrier()

        for ob in range(16):
            buf = ob % 2
            ws = wstage[:, buf * 2048:(buf + 1) * 2048]
            @B.sync
            def _(sy, ob=ob):
                sy.wait_ge(env.s['DV'], env.c['DV'])
                env.inc(sy.dma_start(out=ws, in_=d["l0wo"][ob * 128:(ob + 1) * 128, :]), 'LD', 16)
            ldw = env.c['LD']
            @B.vector
            def _(v, ob=ob):
                v.wait_ge(env.s['LD'], ldw)
                v.wait_ge(env.s['PE'], env.c['PE'])
                env.inc(v.tensor_copy(stgb[:], ws), 'DV', 1)
            dvc = env.c['DV']
            transpose_blocks(lambda i: stgb[:, i * 128:(i + 1) * 128], IC,
                             lambda i, ob=ob: bigA[:, i * 2048 + ob * 128: i * 2048 + ob * 128 + 128],
                             wait_dv=dvc)
        acwo = env.c['AC']

        for tt in range(TT):
            aosl = stgb
            @B.sync
            def _(sy, tt=tt):
                sy.wait_ge(env.s['PE'], env.c['PE'])
                sy.wait_ge(env.s['LG'], env.c['LG'])
                env.inc(sy.dma_start(
                    out=aosl.rearrange("p (a c) -> p a c", a=16),
                    in_=d["sp_ao"].rearrange("(a p) t -> p a t", p=128)[:, :, tt * 128:(tt + 1) * 128]), 'LD', 16)
            ldao = env.c['LD']
            for oc4 in range(4):
                ps = psB if oc4 % 2 == 0 else psC
                @B.tensor
                def _(t, tt=tt, oc4=oc4):
                    t.wait_ge(env.s['LD'], ldao)
                    t.wait_ge(env.s['AC'], acwo)
                    if oc4 >= 2 or tt > 0:
                        t.wait_ge(env.s['AC'], env.c['AC'])
                    for hh in range(16):
                        env.inc(t.matmul(ps[:], aosl[:, hh * 128:(hh + 1) * 128],
                                         bigA[:, hh * 2048 + oc4 * 512: hh * 2048 + (oc4 + 1) * 512],
                                         start=(hh == 0), stop=(hh == 15)), 'PE', 1)
                pemm = env.c['PE']
                eb = evb if oc4 % 2 == 0 else evb2
                @B.scalar
                def _(sc):
                    sc.wait_ge(env.s['PE'], pemm)
                    sc.wait_ge(env.s['LG'], env.c['LG'])
                    env.inc(sc.copy(eb[:, 0:512], ps[:]), 'AC', 1)
                acs2 = env.c['AC']
                @B.gpsimd
                def _(gp, tt=tt, oc4=oc4, eb=eb):
                    gp.wait_ge(env.s['AC'], acs2)
                    env.inc(gp.dma_start(out=d["sp_s"][tt * 128:(tt + 1) * 128, oc4 * 512:(oc4 + 1) * 512],
                                         in_=eb[:, 0:512]), 'LG', 16)

        env.barrier()
        if _DEBUG_SPILL[0]:
            @B.gpsimd
            def _(gp):
                env.inc(gp.dma_start(out=dbg_q, in_=d["sp_q"]), 'LG', 16)
                env.inc(gp.dma_start(out=dbg_ao, in_=d["sp_ao"]), 'LG', 16)
                env.inc(gp.dma_start(out=dbg_s, in_=d["sp_s"]), 'LG', 16)
                gp.wait_ge(env.s['LG'], env.c['LG'])
            env.barrier()

        @B.vector
        def _(v):
            env.inc(v.memset(tmaxs[:], 0.0), 'DV', 1)

        for jc in range(11):
            buf = jc % 2
            for gi, w_ap in ((0, d["wg"]), (1, d["wu"])):
                strip0 = (gi * 2 + buf) * 8192
                for rt in range(4):
                    ws = wstage[:, (rt % 2) * 2048:(rt % 2 + 1) * 2048]
                    @B.gpsimd
                    def _(gp, jc=jc, rt=rt, w_ap=w_ap):
                        gp.wait_ge(env.s['DV'], env.c['DV'])
                        env.inc(gp.indirect_dma_start(
                            ws, None, w_ap,
                            IndirectOffsetOnAxis(idxs[:, jc * 4 + rt: jc * 4 + rt + 1], 0)), 'LG', 16)
                    ldw = env.c['LG']
                    wmx = stats[:, 44:45]; s1 = stats[:, 45:46]; s2 = stats[:, 46:47]
                    @B.vector
                    def _(v):
                        v.wait_ge(env.s['LG'], ldw)
                        v.wait_ge(env.s['PE'], env.c['PE'])
                        v.reduce_max(wmx, ws, axis=AX.X, apply_absolute_value=True)
                        v.drain()
                        v.reciprocal(s1, wmx)
                        v.drain()
                        v.tensor_scalar_mul(s1, s1, 224.0)
                        v.drain()
                        v.tensor_scalar_mul(s2, wmx, 1.0 / 224.0)
                        v.drain()
                        v.tensor_scalar_mul(f8s[:, 0:2048], ws, s1)
                        v.drain()
                        env.inc(v.tensor_scalar_mul(stgb[:], f8s[:, 0:2048], s2), 'DV', 1)
                    dvq = env.c['DV']
                    transpose_blocks(lambda i: stgb[:, i * 128:(i + 1) * 128], IC,
                                     lambda i, strip0=strip0, rt=rt:
                                         bigA[:, strip0 + i * 512 + rt * 128: strip0 + i * 512 + rt * 128 + 128],
                                     wait_dv=dvq)
            acst = env.c['AC']

            for tt in range(TT):
                pg = psA[:, (tt % 2) * 1024:(tt % 2) * 1024 + 512]
                pu = psA[:, (tt % 2) * 1024 + 512:(tt % 2) * 1024 + 1024]
                @B.tensor
                def _(t, tt=tt, jc=jc, buf=buf):
                    t.wait_ge(env.s['AC'], acst)
                    if tt >= 2 or jc > 0:
                        t.wait_ge(env.s['DV'], env.c['DV'])
                    for i in range(IC):
                        t.matmul(pg, bigC[:, i * 2048 + tt * 128: i * 2048 + tt * 128 + 128],
                                 bigA[:, (0 * 2 + buf) * 8192 + i * 512:(0 * 2 + buf) * 8192 + (i + 1) * 512],
                                 start=(i == 0), stop=(i == IC - 1))
                        env.inc(t.matmul(pu, bigC[:, i * 2048 + tt * 128: i * 2048 + tt * 128 + 128],
                                         bigA[:, (1 * 2 + buf) * 8192 + i * 512:(1 * 2 + buf) * 8192 + (i + 1) * 512],
                                         start=(i == 0), stop=(i == IC - 1)), 'PE', 1)
                pemm = env.c['PE']
                gsb = sg0 if tt % 2 == 0 else sg1
                @B.scalar
                def _(sc, pg=pg):
                    sc.wait_ge(env.s['PE'], pemm)
                    env.inc(sc.activation(gsb[:], pg, AF.Silu, bias=0.0, scale=1.0), 'AC', 1)
                acsl = env.c['AC']
                tch = stats[:, 48:49]
                @B.vector
                def _(v, tt=tt, jc=jc, pu=pu, gsb=gsb):
                    v.wait_ge(env.s['AC'], acsl)
                    if tt >= 2 or jc > 0:
                        v.wait_ge(env.s['LG'], env.c['LG'])
                    v.tensor_tensor(gsb[:], gsb[:], pu, op=ALU.mult)
                    v.drain()
                    v.reduce_max(tch, gsb[:], axis=AX.X, apply_absolute_value=True)
                    v.drain()
                    env.inc(v.tensor_tensor(tmaxs[:, tt:tt + 1], tmaxs[:, tt:tt + 1], tch, op=ALU.max), 'DV', 1)
                    v.drain()
                dvt = env.c['DV']
                @B.gpsimd
                def _(gp, tt=tt, jc=jc, gsb=gsb):
                    gp.wait_ge(env.s['DV'], dvt)
                    env.inc(gp.dma_start(out=d["sp_T"][tt * 128:(tt + 1) * 128, jc * 512:(jc + 1) * 512],
                                         in_=gsb[:]), 'LG', 16)

        env.barrier()

        for tt in range(TT):
            Tw = wstage[:, 0:5632]
            @B.sync
            def _(sy, tt=tt):
                if tt >= 1:
                    sy.wait_ge(env.s['DV'], env.c['DV'])
                sy.wait_ge(env.s['LG'], env.c['LG'])
                env.inc(sy.dma_start(out=Tw, in_=d["sp_T"][tt * 128:(tt + 1) * 128, :]), 'LD', 16)
            ldt = env.c['LD']
            q1 = stats[:, 49:50]; q2 = stats[:, 50:51]
            @B.vector
            def _(v, tt=tt):
                v.wait_ge(env.s['LD'], ldt)
                if tt >= 1:
                    v.wait_ge(env.s['LG'], env.c['LG'])
                v.reciprocal(q1, tmaxs[:, tt:tt + 1])
                v.drain()
                v.tensor_scalar_mul(q1, q1, 224.0)
                v.drain()
                v.tensor_scalar_mul(q2, tmaxs[:, tt:tt + 1], 1.0 / 224.0)
                v.drain()
                v.tensor_scalar_mul(f8s[:], Tw, q1)
                v.drain()
                env.inc(v.tensor_scalar_mul(stg5[:], f8s[:], q2), 'DV', 1)
            dvq = env.c['DV']
            transpose_blocks(lambda i: stg5[:, i * 128:(i + 1) * 128], NJT,
                             lambda i: bigA[:, i * 128:(i + 1) * 128],
                             wait_dv=dvq)
            acq = env.c['AC']
            @B.gpsimd
            def _(gp, tt=tt):
                gp.wait_ge(env.s['AC'], acq)
                env.inc(gp.dma_start(
                    out=d["sp_qtT"].rearrange("(a p) t -> p a t", p=128)[:, :, tt * 128:(tt + 1) * 128],
                    in_=bigA[:, 0:5632].rearrange("p (a c) -> p a c", a=NJT)), 'LG', 16)

        env.barrier()

        for hc in range(4):
            for rt in range(4):
                ws = wstage[:, 0:5632]
                @B.sync
                def _(sy, hc=hc, rt=rt):
                    sy.wait_ge(env.s['DV'], env.c['DV'])
                    env.inc(sy.dma_start(out=ws, in_=d["wd"][(hc * 4 + rt) * 128:(hc * 4 + rt + 1) * 128, :]), 'LD', 16)
                ldw = env.c['LD']
                wmx = stats[:, 51:52]; s1 = stats[:, 52:53]; s2 = stats[:, 53:54]
                @B.vector
                def _(v):
                    v.wait_ge(env.s['LD'], ldw)
                    v.wait_ge(env.s['PE'], env.c['PE'])
                    v.reduce_max(wmx, ws, axis=AX.X, apply_absolute_value=True)
                    v.drain()
                    v.reciprocal(s1, wmx)
                    v.drain()
                    v.tensor_scalar_mul(s1, s1, 224.0)
                    v.drain()
                    v.tensor_scalar_mul(s2, wmx, 1.0 / 224.0)
                    v.drain()
                    v.tensor_scalar_mul(f8s[:], ws, s1)
                    v.drain()
                    env.inc(v.tensor_scalar_mul(stg5[:], f8s[:], s2), 'DV', 1)
                dvq = env.c['DV']
                transpose_blocks(lambda i: stg5[:, i * 128:(i + 1) * 128], NJT,
                                 lambda i, rt=rt: bigA[:, i * 512 + rt * 128: i * 512 + rt * 128 + 128],
                                 wait_dv=dvq)
            acst = env.c['AC']
            for tt in range(TT):
                @B.sync
                def _(sy, hc=hc, tt=tt):
                    sy.wait_ge(env.s['PE'], env.c['PE'])
                    sy.wait_ge(env.s['LG'], env.c['LG'])
                    env.inc(sy.dma_start(
                        out=stg5[:].rearrange("p (a c) -> p a c", a=NJT),
                        in_=d["sp_qtT"].rearrange("(a p) t -> p a t", p=128)[:, :, tt * 128:(tt + 1) * 128]), 'LD', 16)
                    env.inc(sy.dma_start(out=sx0[:], in_=x0_ap[tt * 128:(tt + 1) * 128, hc * 512:(hc + 1) * 512]), 'LD', 16)
                    env.inc(sy.dma_start(out=ssb[:], in_=d["sp_s"][tt * 128:(tt + 1) * 128, hc * 512:(hc + 1) * 512]), 'LD', 16)
                ldq = env.c['LD']
                ps = psB if tt % 2 == 0 else psC
                @B.tensor
                def _(t, hc=hc, tt=tt):
                    t.wait_ge(env.s['LD'], ldq)
                    t.wait_ge(env.s['AC'], acst)
                    if tt >= 2 or hc > 0:
                        t.wait_ge(env.s['DV'], env.c['DV'])
                    for i in range(NJT):
                        env.inc(t.matmul(ps[:], stg5[:, i * 128:(i + 1) * 128],
                                         bigA[:, i * 512:(i + 1) * 512],
                                         start=(i == 0), stop=(i == NJT - 1)), 'PE', 1)
                pemm = env.c['PE']
                gsb = sg0 if tt % 2 == 0 else sg1
                @B.scalar
                def _(sc, ps=ps, gsb=gsb):
                    sc.wait_ge(env.s['PE'], pemm)
                    env.inc(sc.activation(gsb[:], ps[:], AF.Silu, bias=0.0, scale=1.0), 'AC', 1)
                acg = env.c['AC']
                @B.vector
                def _(v, hc=hc, tt=tt, gsb=gsb):
                    v.wait_ge(env.s['AC'], acg)
                    v.wait_ge(env.s['LD'], ldq)
                    if tt >= 2 or hc > 0:
                        v.wait_ge(env.s['LG'], env.c['LG'])
                    v.tensor_tensor(gsb[:], gsb[:], ssb[:], op=ALU.mult)
                    v.drain()
                    env.inc(v.tensor_tensor(sx0[:], sx0[:], gsb[:], op=ALU.add), 'DV', 1)
                    v.drain()
                dvx = env.c['DV']
                @B.gpsimd
                def _(gp, hc=hc, tt=tt):
                    gp.wait_ge(env.s['DV'], dvx)
                    env.inc(gp.dma_start(out=d["sp_x1"][tt * 128:(tt + 1) * 128, hc * 512:(hc + 1) * 512],
                                         in_=sx0[:]), 'LG', 16)

        env.barrier()

        @B.sync
        def _(sy):
            env.inc(sy.dma_start(out=wstage[0:8, 0:2048], in_=d["wr"]), 'LD', 16)
        ldwr = env.c['LD']
        @B.vector
        def _(v):
            v.wait_ge(env.s['LD'], ldwr)
            v.tensor_copy(stgb[0:8, 0:2048], wstage[0:8, 0:2048])
            v.drain()
            env.inc(v.tensor_tensor(stg5[0:8, 0:2048], wstage[0:8, 0:2048], stgb[0:8, 0:2048],
                                    op=ALU.subtract), 'DV', 1)
            v.drain()
        dvwr = env.c['DV']
        @B.tensor
        def _(t):
            t.wait_ge(env.s['DV'], dvwr)
            for i in range(IC):
                pt = ptA if i < 8 else ptB
                t.transpose(pt[:, (i % 8) * 128:(i % 8) * 128 + 8],
                            stgb[0:8, i * 128:(i + 1) * 128], ident[0:8, 0:8])
                env.inc(t.transpose(pt[:, (i % 8) * 128 + 8:(i % 8) * 128 + 16],
                                    stg5[0:8, i * 128:(i + 1) * 128], ident[0:8, 0:8]), 'PE', 1)
        pewr = env.c['PE']
        @B.scalar
        def _(sc):
            sc.wait_ge(env.s['PE'], pewr)
            for i in range(IC):
                pt = ptA if i < 8 else ptB
                env.inc(sc.copy(wr2t[:, i * 16:(i + 1) * 16],
                                pt[:, (i % 8) * 128:(i % 8) * 128 + 16]), 'AC', 1)

        env.barrier()
        ln_quant_phase(d["sp_x1"], 1)
        env.barrier()

        cosfull = bigC[:, 16384:20480].bitcast(F32)
        sinfull = bigC[:, 20480:24576].bitcast(F32)
        @B.sync
        def _(sy):
            env.inc(sy.dma_start(out=cosfull.rearrange("p (a c) -> p a c", a=TT),
                                 in_=d["cosT"].rearrange("(a p) c -> p a c", p=128)), 'LD', 16)
            env.inc(sy.dma_start(out=sinfull.rearrange("p (a c) -> p a c", a=TT),
                                 in_=d["sinF"].rearrange("(a p) c -> p a c", p=128)), 'LD', 16)
        ldcs = env.c['LD']

        def rope_apply(v, ps, eb, t1, tt):
            cos4 = cosfull[:, tt * 128:(tt + 1) * 128].rearrange("p (o c) -> p o c", o=1).broadcast_to([128, 4, 128])
            v.tensor_tensor(t1[:].rearrange("p (h c) -> p h c", h=4), ps[:].rearrange("p (h c) -> p h c", h=4),
                            cos4, op=ALU.mult)
            pr = ps[:].rearrange("p (h t c) -> p h t c", h=4, t=2, c=64)
            ebr = eb[:, 0:512].rearrange("p (h t c) -> p h t c", h=4, t=2, c=64)
            sin4 = sinfull[:, tt * 128:(tt + 1) * 128].rearrange("p (o t c) -> p o t c", o=1, t=2)\
                .broadcast_to([128, 4, 2, 64])
            v.tensor_tensor(ebr[:, :, 0, :], pr[:, :, 1, :], sin4[:, :, 0, :], op=ALU.mult)
            v.tensor_tensor(ebr[:, :, 1, :], pr[:, :, 0, :], sin4[:, :, 1, :], op=ALU.mult)
            v.drain()
            inst = v.tensor_tensor(eb[:, 0:512], t1[:], eb[:, 0:512], op=ALU.add)
            v.drain()
            return inst

        for oc4 in range(4):
            buf = oc4 % 2
            for i in range(IC):
                ws = wstage[:, (i % 2) * 2048:(i % 2) * 2048 + 512]
                @B.sync
                def _(sy, i=i, oc4=oc4):
                    sy.wait_ge(env.s['DV'], env.c['DV'])
                    env.inc(sy.dma_start(out=ws, in_=d["l1wq"][i * 128:(i + 1) * 128, oc4 * 512:(oc4 + 1) * 512]), 'LD', 16)
                ldw = env.c['LD']
                @B.vector
                def _(v, i=i, buf=buf, oc4=oc4):
                    v.wait_ge(env.s['LD'], ldw)
                    if oc4 >= 2:
                        v.wait_ge(env.s['PE'], env.c['PE'])
                    env.inc(v.tensor_copy(wtrans[:, buf * 8192 + i * 512: buf * 8192 + (i + 1) * 512], ws), 'DV', 1)
            dvst = env.c['DV']
            for tt in range(TT):
                ps = psB if tt % 2 == 0 else psC
                @B.tensor
                def _(t, tt=tt, oc4=oc4, buf=buf):
                    t.wait_ge(env.s['DV'], dvst)
                    if tt >= 2 or oc4 > 0:
                        t.wait_ge(env.s['DV'], env.c['DV'])
                    for i in range(IC):
                        env.inc(t.matmul(ps[:], bigA[:, i * 2048 + tt * 128: i * 2048 + tt * 128 + 128],
                                         wtrans[:, buf * 8192 + i * 512: buf * 8192 + (i + 1) * 512],
                                         start=(i == 0), stop=(i == IC - 1)), 'PE', 1)
                pemm = env.c['PE']
                eb = evb if tt % 2 == 0 else evb2
                t1 = sg0 if tt % 2 == 0 else sg1
                @B.vector
                def _(v, tt=tt, ps=ps, eb=eb, t1=t1):
                    v.wait_ge(env.s['PE'], pemm)
                    v.wait_ge(env.s['LD'], ldcs)
                    v.wait_ge(env.s['LG'], env.c['LG'])
                    env.inc(rope_apply(v, ps, eb, t1, tt), 'DV', 1)
                dvr = env.c['DV']
                @B.gpsimd
                def _(gp, tt=tt, oc4=oc4, eb=eb):
                    gp.wait_ge(env.s['DV'], dvr)
                    env.inc(gp.dma_start(out=d["sp_q"][tt * 128:(tt + 1) * 128, oc4 * 512:(oc4 + 1) * 512],
                                         in_=eb[:, 0:512]), 'LG', 16)

        for which, w_ap in (("k", d["l1wk"]), ("v", d["l1wv"])):
            for i in range(IC):
                ws = wstage[:, (i % 2) * 2048:(i % 2) * 2048 + 512]
                @B.sync
                def _(sy, i=i, w_ap=w_ap):
                    sy.wait_ge(env.s['DV'], env.c['DV'])
                    env.inc(sy.dma_start(out=ws, in_=w_ap[i * 128:(i + 1) * 128, :]), 'LD', 16)
                ldw = env.c['LD']
                @B.vector
                def _(v, i=i):
                    v.wait_ge(env.s['LD'], ldw)
                    v.wait_ge(env.s['PE'], env.c['PE'])
                    env.inc(v.tensor_copy(wtrans[:, i * 512:(i + 1) * 512], ws), 'DV', 1)
            dvst = env.c['DV']
            for tt in range(TT):
                ps = psB if tt % 2 == 0 else psC
                @B.tensor
                def _(t, tt=tt):
                    t.wait_ge(env.s['DV'], dvst)
                    if tt >= 2:
                        t.wait_ge(env.s['DV'], env.c['DV'])
                        t.wait_ge(env.s['AC'], env.c['AC'])
                    for i in range(IC):
                        env.inc(t.matmul(ps[:], bigA[:, i * 2048 + tt * 128: i * 2048 + tt * 128 + 128],
                                         wtrans[:, i * 512:(i + 1) * 512],
                                         start=(i == 0), stop=(i == IC - 1)), 'PE', 1)
                pemm = env.c['PE']
                if which == "v":
                    @B.scalar
                    def _(sc, tt=tt, ps=ps):
                        sc.wait_ge(env.s['PE'], pemm)
                        env.inc(sc.copy(bigC[:, 8192 + tt * 512: 8192 + (tt + 1) * 512], ps[:]), 'AC', 1)
                else:
                    eb = evb if tt % 2 == 0 else evb2
                    t1 = sg0 if tt % 2 == 0 else sg1
                    @B.vector
                    def _(v, tt=tt, ps=ps, eb=eb, t1=t1):
                        v.wait_ge(env.s['PE'], pemm)
                        v.wait_ge(env.s['LD'], ldcs)
                        v.wait_ge(env.s['AC'], env.c['AC'])
                        env.inc(rope_apply(v, ps, eb, t1, tt), 'DV', 1)
                    dvr = env.c['DV']
                    @B.tensor
                    def _(t, tt=tt, eb=eb):
                        t.wait_ge(env.s['DV'], dvr)
                        t.wait_ge(env.s['AC'], env.c['AC'])
                        for kv in range(4):
                            pt = ptA if kv < 2 else ptB
                            env.inc(t.transpose(pt[:, (kv % 2) * 128:(kv % 2 + 1) * 128],
                                                eb[:, kv * 128:(kv + 1) * 128], ident[:]), 'PE', 1)
                    pek = env.c['PE']
                    @B.scalar
                    def _(sc, tt=tt):
                        sc.wait_ge(env.s['PE'], pek)
                        for kv in range(4):
                            pt = ptA if kv < 2 else ptB
                            env.inc(sc.copy(bigC[:, kv * 2048 + tt * 128: kv * 2048 + tt * 128 + 128],
                                            pt[:, (kv % 2) * 128:(kv % 2 + 1) * 128]), 'AC', 1)

        env.barrier()

        qfm = bigC[:, 24576:26624]
        ptl1 = bigC[:, 26624:27392]
        ptl2 = bigC[:, 27392:28160]
        rT = bigC[:, 28160:28928]
        qtm = bigC[:, 28928:30976]
        for h in range(NH):
            kv = h // 4
            lamh = lamb[:, h:h + 1]
            @B.sync
            def _(sy, h=h):
                if h >= 1:
                    sy.wait_ge(env.s['PE'], env.c['PE'])
                sy.wait_ge(env.s['LG'], env.c['LG'])
                env.inc(sy.dma_start(
                    out=qtm.rearrange("p (a c) -> p a c", a=TT),
                    in_=d["sp_q"].rearrange("(a p) o -> p a o", p=128)[:, :, h * 128:(h + 1) * 128]), 'LD', 16)
            ldq = env.c['LD']
            @B.tensor
            def _(t):
                t.wait_ge(env.s['LD'], ldq)
            transpose_blocks(lambda i: qtm[:, i * 128:(i + 1) * 128], TT,
                             lambda i: qfm[:, i * 128:(i + 1) * 128])
            acq = env.c['AC']

            for qt in range(TT):
                j0 = max(0, qt - 4)
                nkt = qt - j0 + 1
                w = nkt * 128
                @B.tensor
                def _(t, qt=qt, h=h, j0=j0, nkt=nkt):
                    t.wait_ge(env.s['AC'], acq)
                    t.wait_ge(env.s['DV'], env.c['DV'])
                    for half, base in ((0, 0), (1, 1024)):
                        lhs = qfm[half * 64:half * 64 + 64, qt * 128:(qt + 1) * 128]
                        done = 0
                        while done < nkt:
                            take = min(4, nkt - done)
                            rhs = bigC[half * 64:half * 64 + 64,
                                       kv * 2048 + (j0 + done) * 128: kv * 2048 + (j0 + done + take) * 128]
                            env.inc(t.matmul(psA[:, base + done * 128: base + (done + take) * 128],
                                             lhs, rhs, start=True, stop=True,
                                             tile_position=(half * 64, 0)), 'PE', 1)
                            done += take
                pes = env.c['PE']
                m1 = stats[:, 54:55]; m2 = stats[:, 55:56]
                @B.vector
                def _(v, qt=qt, nkt=nkt, w=w):
                    v.wait_ge(env.s['PE'], pes)
                    for base in (0, 1024):
                        v.tensor_tensor(psA[:, base + (nkt - 1) * 128: base + nkt * 128],
                                        psA[:, base + (nkt - 1) * 128: base + nkt * 128],
                                        maskc[:], op=ALU.add)
                        if qt >= 4:
                            v.tensor_tensor(psA[:, base:base + 128],
                                            psA[:, base:base + 128],
                                            maska[:], op=ALU.add)
                    v.reduce_max(m1, psA[:, 0:w], axis=AX.X)
                    v.reduce_max(m2, psA[:, 1024:1024 + w], axis=AX.X)
                    v.tensor_scalar_mul(stats[:, 56:57], m1, -0.125)
                    env.inc(v.tensor_scalar_mul(stats[:, 57:58], m2, -0.125), 'DV', 1)
                dvmk = env.c['DV']
                z1 = stats[:, 58:59]; z2 = stats[:, 59:60]
                @B.scalar
                def _(sc, w=w):
                    sc.wait_ge(env.s['DV'], dvmk)
                    sc.activation(ptl1[:, 0:w], psA[:, 0:w], AF.Exp,
                                  bias=stats[:, 56:57], scale=0.125, accum_out=z1)
                    env.inc(sc.activation(ptl2[:, 0:w], psA[:, 1024:1024 + w], AF.Exp,
                                          bias=stats[:, 57:58], scale=0.125, accum_out=z2), 'AC', 1)
                acex = env.c['AC']
                @B.vector
                def _(v, w=w, lamh=lamh):
                    v.wait_ge(env.s['AC'], acex)
                    zi1 = stats[:, 60:61]; zi2 = stats[:, 61:62]
                    v.reciprocal(zi1, z1)
                    v.drain()
                    v.reciprocal(zi2, z2)
                    v.drain()
                    v.tensor_scalar_mul(ptl1[:, 0:w], ptl1[:, 0:w], zi1)
                    v.tensor_mul(zi2, zi2, lamh)
                    v.drain()
                    v.tensor_scalar_mul(zi2, zi2, -1.0)
                    v.drain()
                    v.scalar_tensor_tensor(ptl1[:, 0:w], ptl2[:, 0:w], zi2, ptl1[:, 0:w],
                                           op0=ALU.mult, op1=ALU.add)
                    v.drain()
                    ds = stats[:, 62:63]
                    v.tensor_scalar(ptl1[:, 0:w], ptl1[:, 0:w], 0.0, 0.0,
                                    op0=ALU.max, op1=ALU.add, accum_out=ds)
                    v.drain()
                    v.tensor_scalar_add(ds, ds, 1e-6)
                    v.drain()
                    ri = stats[:, 63:64]
                    v.reciprocal(ri, ds)
                    v.drain()
                    env.inc(v.tensor_scalar_mul(ptl1[:, 0:w], ptl1[:, 0:w], ri), 'DV', 1)
                    v.drain()
                dvr = env.c['DV']
                @B.tensor
                def _(t, nkt=nkt):
                    t.wait_ge(env.s['DV'], dvr)
                    t.wait_ge(env.s['AC'], env.c['AC'])
                    for j in range(nkt):
                        pt = ptA if j < 4 else ptB
                        env.inc(t.transpose(pt[:, (j % 4) * 128:(j % 4 + 1) * 128],
                                            ptl1[:, j * 128:(j + 1) * 128], ident[:]), 'PE', 1)
                per = env.c['PE']
                @B.scalar
                def _(sc, nkt=nkt):
                    sc.wait_ge(env.s['PE'], per)
                    for j in range(nkt):
                        pt = ptA if j < 4 else ptB
                        env.inc(sc.copy(rT[:, j * 128:(j + 1) * 128],
                                        pt[:, (j % 4) * 128:(j % 4 + 1) * 128]), 'AC', 1)
                acr = env.c['AC']
                ps = psB if qt % 2 == 0 else psC
                @B.tensor
                def _(t, qt=qt, h=h, j0=j0, nkt=nkt, kv=kv, ps=ps):
                    t.wait_ge(env.s['AC'], acr)
                    if qt >= 2 or h > 0:
                        t.wait_ge(env.s['AC'], env.c['AC'])
                    for j in range(nkt):
                        kt = j0 + j
                        env.inc(t.matmul(ps[:, 0:128],
                                         bigC[:, 8192 + kt * 512 + kv * 128: 8192 + kt * 512 + (kv + 1) * 128],
                                         rT[:, j * 128:(j + 1) * 128],
                                         start=(j == 0), stop=(j == nkt - 1)), 'PE', 1)
                pepv = env.c['PE']
                eb = evb if qt % 2 == 0 else evb2
                @B.scalar
                def _(sc, ps=ps, eb=eb):
                    sc.wait_ge(env.s['PE'], pepv)
                    sc.wait_ge(env.s['LG'], env.c['LG'])
                    env.inc(sc.copy(eb[:, 0:128], ps[:, 0:128]), 'AC', 1)
                acact = env.c['AC']
                @B.gpsimd
                def _(gp, h=h, qt=qt, eb=eb):
                    gp.wait_ge(env.s['AC'], acact)
                    env.inc(gp.dma_start(out=d["sp_ao"][h * 128:(h + 1) * 128, qt * 128:(qt + 1) * 128],
                                         in_=eb[:, 0:128]), 'LG', 16)

        env.barrier()

        for i in range(IC):
            ws = wstage[:, (i % 2) * 2048:(i % 2 + 1) * 2048]
            @B.sync
            def _(sy, i=i):
                if i >= 2:
                    sy.wait_ge(env.s['DV'], env.c['DV'])
                env.inc(sy.dma_start(out=ws, in_=d["l1wo"][i * 128:(i + 1) * 128, :]), 'LD', 16)
            ldw = env.c['LD']
            @B.vector
            def _(v, i=i):
                v.wait_ge(env.s['LD'], ldw)
                env.inc(v.tensor_copy(bigC[:, i * 2048:(i + 1) * 2048], ws), 'DV', 1)
        dvwo = env.c['DV']
        for tt in range(TT):
            aosl = stgb
            @B.sync
            def _(sy, tt=tt):
                sy.wait_ge(env.s['PE'], env.c['PE'])
                sy.wait_ge(env.s['LG'], env.c['LG'])
                env.inc(sy.dma_start(
                    out=aosl.rearrange("p (a c) -> p a c", a=16),
                    in_=d["sp_ao"].rearrange("(a p) t -> p a t", p=128)[:, :, tt * 128:(tt + 1) * 128]), 'LD', 16)
            ldao = env.c['LD']
            for oc4 in range(4):
                ps = psB if oc4 % 2 == 0 else psC
                @B.tensor
                def _(t, tt=tt, oc4=oc4, ps=ps):
                    t.wait_ge(env.s['LD'], ldao)
                    t.wait_ge(env.s['DV'], dvwo)
                    if oc4 >= 2 or tt > 0:
                        t.wait_ge(env.s['AC'], env.c['AC'])
                    for hh in range(16):
                        env.inc(t.matmul(ps[:], aosl[:, hh * 128:(hh + 1) * 128],
                                         bigC[:, hh * 2048 + oc4 * 512: hh * 2048 + (oc4 + 1) * 512],
                                         start=(hh == 0), stop=(hh == 15)), 'PE', 1)
                pemm = env.c['PE']
                eb = evb if oc4 % 2 == 0 else evb2
                @B.scalar
                def _(sc, ps=ps, eb=eb):
                    sc.wait_ge(env.s['PE'], pemm)
                    sc.wait_ge(env.s['LG'], env.c['LG'])
                    env.inc(sc.copy(eb[:, 0:512], ps[:]), 'AC', 1)
                acs2 = env.c['AC']
                @B.gpsimd
                def _(gp, tt=tt, oc4=oc4, eb=eb):
                    gp.wait_ge(env.s['AC'], acs2)
                    env.inc(gp.dma_start(out=d["sp_s"][tt * 128:(tt + 1) * 128, oc4 * 512:(oc4 + 1) * 512],
                                         in_=eb[:, 0:512]), 'LG', 16)

        env.barrier()

        @B.vector
        def _(v):
            env.inc(v.memset(bigC[:], 0.0), 'DV', 1)
            v.drain()
        for e in range(8):
            for fc in range(4):
                buf = fc % 2
                for rt in range(4):
                    ws = wstage[:, (rt % 2) * 2048:(rt % 2 + 1) * 2048]
                    @B.sync
                    def _(sy, e=e, fc=fc, rt=rt):
                        sy.wait_ge(env.s['DV'], env.c['DV'])
                        env.inc(sy.dma_start(out=ws, in_=d["wex"][e * 2048 + fc * 512 + rt * 128: e * 2048 + fc * 512 + (rt + 1) * 128, :]), 'LD', 16)
                    ldw = env.c['LD']
                    @B.vector
                    def _(v):
                        v.wait_ge(env.s['LD'], ldw)
                        v.wait_ge(env.s['PE'], env.c['PE'])
                        env.inc(v.tensor_copy(stgb[:], ws), 'DV', 1)
                    dvc = env.c['DV']
                    transpose_blocks(lambda i: stgb[:, i * 128:(i + 1) * 128], IC,
                                     lambda i, buf=buf, rt=rt:
                                         wtrans[:, buf * 8192 + i * 512 + rt * 128: buf * 8192 + i * 512 + rt * 128 + 128],
                                     wait_dv=dvc)
                acst = env.c['AC']
                for tt in range(TT):
                    ps = psB if tt % 2 == 0 else psC
                    @B.tensor
                    def _(t, tt=tt, fc=fc, e=e, buf=buf, ps=ps):
                        t.wait_ge(env.s['AC'], acst)
                        if tt >= 2 or fc > 0 or e > 0:
                            t.wait_ge(env.s['DV'], env.c['DV'])
                        for i in range(IC):
                            env.inc(t.matmul(ps[:], bigA[:, i * 2048 + tt * 128: i * 2048 + tt * 128 + 128],
                                             wtrans[:, buf * 8192 + i * 512: buf * 8192 + (i + 1) * 512],
                                             start=(i == 0), stop=(i == IC - 1)), 'PE', 1)
                    pemm = env.c['PE']
                    @B.vector
                    def _(v, tt=tt, fc=fc, e=e, ps=ps):
                        v.wait_ge(env.s['PE'], pemm)
                        env.inc(v.scalar_tensor_tensor(
                            bigC[:, tt * 2048 + fc * 512: tt * 2048 + (fc + 1) * 512],
                            ps[:], selm[:, tt * 8 + e: tt * 8 + e + 1],
                            bigC[:, tt * 2048 + fc * 512: tt * 2048 + (fc + 1) * 512],
                            op0=ALU.mult, op1=ALU.add), 'DV', 1)

        env.barrier()

        for tt in range(TT):
            for oc4 in range(4):
                k = tt * 4 + oc4
                gsb = sg0 if k % 2 == 0 else sg1
                @B.sync
                def _(sy, tt=tt, oc4=oc4, k=k):
                    if k > 0:
                        sy.wait_ge(env.s['OT'], env.c['OT'])
                        sy.wait_ge(env.s['DV'], env.c['DV'])
                    sy.wait_ge(env.s['LG'], env.c['LG'])
                    env.inc(sy.dma_start(out=sx0[:], in_=d["sp_x1"][tt * 128:(tt + 1) * 128, oc4 * 512:(oc4 + 1) * 512]), 'LD', 16)
                    env.inc(sy.dma_start(out=ssb[:], in_=d["sp_s"][tt * 128:(tt + 1) * 128, oc4 * 512:(oc4 + 1) * 512]), 'LD', 16)
                ldxs = env.c['LD']
                @B.scalar
                def _(sc, tt=tt, oc4=oc4, gsb=gsb):
                    sc.wait_ge(env.s['LD'], ldxs)
                    sc.wait_ge(env.s['DV'], env.c['DV'])
                    env.inc(sc.activation(gsb[:], bigC[:, tt * 2048 + oc4 * 512: tt * 2048 + (oc4 + 1) * 512],
                                          AF.Silu, bias=0.0, scale=1.0), 'AC', 1)
                acg = env.c['AC']
                @B.vector
                def _(v, gsb=gsb):
                    v.wait_ge(env.s['AC'], acg)
                    v.wait_ge(env.s['LD'], ldxs)
                    v.tensor_tensor(gsb[:], gsb[:], ssb[:], op=ALU.mult)
                    v.drain()
                    env.inc(v.tensor_tensor(sx0[:], sx0[:], gsb[:], op=ALU.add), 'DV', 1)
                    v.drain()
                dvx = env.c['DV']
                @B.gpsimd
                def _(gp, tt=tt, oc4=oc4):
                    gp.wait_ge(env.s['DV'], dvx)
                    env.inc(gp.dma_start(out=out_ap[tt * 128:(tt + 1) * 128, oc4 * 512:(oc4 + 1) * 512],
                                         in_=sx0[:]), 'OT', 16)

        @B.gpsimd
        def _(gp):
            gp.wait_ge(env.s['OT'], env.c['OT'])

    return nc


_NC_CACHE = [None]


def kernel(**inputs):
    f32 = lambda a: np.ascontiguousarray(np.asarray(a, dtype=np.float32))
    x = f32(inputs["L0_attn_x"])
    hd = 128
    inv_freq = (1.0 / (10000.0 ** (np.arange(0, hd, 2, dtype=np.float32) / np.float32(hd)))).astype(np.float32)
    t = np.arange(S, dtype=np.float32)
    freqs = np.outer(t, inv_freq).astype(np.float32)
    emb = np.concatenate([freqs, freqs], axis=-1)
    cosT = np.ascontiguousarray(np.cos(emb).astype(np.float32))
    sinT = np.sin(emb).astype(np.float32)
    sinF = sinT.copy()
    sinF[:, :64] = -sinF[:, :64]
    sinF = np.ascontiguousarray(sinF)

    dn = np.ascontiguousarray(np.asarray(inputs["L0_ffn_down_reorder_index"], dtype=np.int32).reshape(IFF, 1))

    common = {
        "l0wq": f32(inputs["L0_attn_Wq"]), "l0wk": f32(inputs["L0_attn_Wk"]),
        "l0wv": f32(inputs["L0_attn_Wv"]), "l0wo": f32(inputs["L0_attn_Wo"]),
        "wg": f32(inputs["L0_ffn_w_gate"]), "wu": f32(inputs["L0_ffn_w_up"]),
        "wd": f32(inputs["L0_ffn_w_down"]), "dnidx": dn,
        "l1wq": f32(inputs["L1_attn_Wq"]), "l1wk": f32(inputs["L1_attn_Wk"]),
        "l1wv": f32(inputs["L1_attn_Wv"]), "l1wo": f32(inputs["L1_attn_Wo"]),
        "lam": f32(inputs["L1_attn_lam"]).reshape(1, NH),
        "wr": f32(inputs["L1_ffn_w_router"]),
        "wex": f32(inputs["L1_ffn_w_experts"]).reshape(8 * D, D),
        "cosT": cosT, "sinF": sinF,
    }
    if _NC_CACHE[0] is None:
        _NC_CACHE[0] = build_nc()
    nc = _NC_CACHE[0]
    in_maps = [dict(common, x0=np.ascontiguousarray(x[c])) for c in range(8)]
    res = run_bass_kernel_spmd(nc, in_maps, core_ids=list(range(8)))
    return np.stack([r["out"] for r in res.results], axis=0)


# version 06
# speedup vs baseline: 1.0428x; 1.0428x over previous
import math
import numpy as np
from contextlib import ExitStack

import concourse.bass as bass
import concourse.mybir as mybir
from concourse.bass_utils import run_bass_kernel_spmd
from concourse.bass import IndirectOffsetOnAxis

F32 = mybir.dt.float32
BF16 = mybir.dt.bfloat16
F8 = mybir.dt.float8e4
I32 = mybir.dt.int32
AF = mybir.ActivationFunctionType
ALU = mybir.AluOpType
AX = mybir.AxisListType

S = 2048
D = 2048
TT = 16
IC = 16
IFF = 5632
NJT = 44
NH = 16
HD = 128
NEG = -1.0e9


class Env:
    def __init__(self, nc, block, sems):
        self.nc = nc
        self.block = block
        self.s = sems
        self.c = {k: 0 for k in sems}

    def inc(self, inst, k, by=1):
        inst.then_inc(self.s[k], by)
        self.c[k] += by
        return self.c[k]

    def barrier(self):
        self.c['PH'] += 5
        tgt = self.c['PH']
        for attr in ('tensor', 'vector', 'scalar', 'gpsimd', 'sync'):
            def mk(eng, tgt=tgt):
                eng.nop().then_inc(self.s['PH'], 1)
                eng.wait_ge(self.s['PH'], tgt)
            getattr(self.block, attr)(mk)


_DEBUG_SPILL = [False]


def build_nc():
    nc = bass.Bass("TRN2", target_bir_lowering=False, debug=False)
    d = {}
    def inp(name, shape, dt=F32):
        d[name] = nc.dram_tensor(name, shape, dt, kind="ExternalInput").ap()
    inp("x0", [S, D])
    inp("l0wq", [D, D]); inp("l0wk", [D, D]); inp("l0wv", [D, D]); inp("l0wo", [D, D])
    inp("wg", [IFF, D]); inp("wu", [IFF, D]); inp("wd", [D, IFF])
    inp("dnidx", [IFF, 1], I32)
    inp("l1wq", [D, D]); inp("l1wk", [D, 512]); inp("l1wv", [D, 512]); inp("l1wo", [D, D])
    inp("lam", [1, NH]); inp("wr", [8, D]); inp("wex", [8 * D, D])
    inp("cosT", [S, HD]); inp("sinF", [S, HD])
    out_ap = nc.dram_tensor("out", [S, D], F32, kind="ExternalOutput").ap()
    def scr(name, shape, dt):
        if _DEBUG_SPILL[0]:
            d[name] = nc.dram_tensor(name, shape, dt, kind="ExternalOutput").ap()
        else:
            d[name] = nc.dram_tensor(name, shape, dt).ap()
    scr("sp_q", [D, S], BF16)
    scr("sp_k", [D, S], BF16)
    scr("sp_v", [S, D], BF16)
    scr("sp_ao", [D, S], BF16)
    scr("sp_s", [S, D], BF16)
    scr("sp_T", [S, IFF], F32)
    scr("sp_qtT", [IFF, S], BF16)
    scr("sp_x1", [S, D], F32)
    if _DEBUG_SPILL[0]:
        dbg_a = nc.dram_tensor("dbg_a", [128, 32768], BF16, kind="ExternalOutput").ap()
        dbg_c = nc.dram_tensor("dbg_c", [128, 32768], BF16, kind="ExternalOutput").ap()
        dbg_q = nc.dram_tensor("dbg_q", [D, S], BF16, kind="ExternalOutput").ap()
        dbg_ao = nc.dram_tensor("dbg_ao", [D, S], BF16, kind="ExternalOutput").ap()
        dbg_s = nc.dram_tensor("dbg_s", [S, D], BF16, kind="ExternalOutput").ap()

    with ExitStack() as ctx:
        def sb(n, s_, dt):
            return ctx.enter_context(nc.sbuf_tensor(n, s_, dt))
        bigA = sb("bigA", [128, 32768], BF16)
        bigC = sb("bigC", [128, 32768], BF16)
        wstage = sb("wstage", [128, 5632], F32)
        wtrans = sb("wtrans", [128, 16384], BF16)
        f8s = sb("f8s", [128, IFF], F8)
        evb = sb("evb", [128, 512], BF16)
        evb2 = sb("evb2", [128, 512], BF16)
        sg0 = sb("sg0", [128, 512], F32)
        sg1 = sb("sg1", [128, 512], F32)
        sx0 = sb("sx0", [128, 512], F32)
        ssb = sb("ssb", [128, 512], BF16)
        stats = sb("stats", [128, 64], F32)
        selm = sb("selm", [128, 128], F32)
        tmaxs = sb("tmaxs", [128, 16], F32)
        ident = sb("ident", [128, 128], BF16)
        maskc = sb("maskc", [128, 128], F32)
        maska = sb("maska", [128, 128], F32)
        lamb = sb("lamb", [128, NH], F32)
        onesc = sb("onesc", [1, 128], F32)
        lams = sb("lams", [1, NH], F32)
        idxs = sb("idxs", [128, NJT], I32)

        stgb = wstage[:, 4096:5120].bitcast(BF16)
        stg5 = bigC[:, 0:5632]

        psA = ctx.enter_context(nc.psum_tensor("psA", [128, 2048], F32))
        psB = ctx.enter_context(nc.psum_tensor("psB", [128, 512], F32))
        psC = ctx.enter_context(nc.psum_tensor("psC", [128, 512], F32))
        ptA = ctx.enter_context(nc.psum_tensor("ptA", [128, 1024], BF16))
        ptB = ctx.enter_context(nc.psum_tensor("ptB", [128, 1024], BF16))

        sems = {}
        for name in ('LD', 'LG', 'PE', 'DV', 'AC', 'GP', 'PH', 'OT'):
            sems[name] = ctx.enter_context(nc.semaphore(name))
        block = ctx.enter_context(nc.Block())
        env = Env(nc, block, sems)
        B = env.block

        env.marks = {'pt_free': 0}

        def transpose_blocks(src_ap_fn, n128, dst_fn, wait_dv=None, wait_extra=None,
                             dst_batch=None, evac_eng='scalar'):
            for g0 in range(0, n128, 16):
                n = min(16, n128 - g0)
                @B.tensor
                def _(t, g0=g0, n=n):
                    if g0 == 0:
                        if wait_dv is not None:
                            t.wait_ge(env.s['DV'], wait_dv)
                        if wait_extra is not None:
                            wait_extra(t)
                    t.wait_ge(env.s['AC'], env.marks['pt_free'])
                    for j in range(n):
                        i = g0 + j
                        pt = ptA if j < 8 else ptB
                        env.inc(t.transpose(pt[:, (j % 8) * 128:(j % 8 + 1) * 128],
                                            src_ap_fn(i), ident[:]), 'PE', 1)
                pet = env.c['PE']
                eng_dec = getattr(B, evac_eng)
                @eng_dec
                def _(sc, g0=g0, n=n, pet=pet):
                    sc.wait_ge(env.s['PE'], pet)
                    if dst_batch is not None:
                        tgt, base_fn, step = dst_batch
                        W = tgt.shape[1]
                        for h0 in range(0, n, 8):
                            m = min(8, n - h0)
                            pt = ptA if h0 == 0 else ptB
                            b0 = base_fn(g0 + h0)
                            if step == 128:
                                env.inc(sc.copy(tgt[:, b0:b0 + m * 128],
                                                pt[:, 0:m * 128]), 'AC', 1)
                                continue
                            mm_ = m
                            if b0 + m * step > W:
                                mm_ = m - 1
                                env.inc(sc.copy(tgt[:, base_fn(g0 + h0 + mm_):base_fn(g0 + h0 + mm_) + 128],
                                                pt[:, mm_ * 128:(mm_ + 1) * 128]), 'AC', 1)
                            if mm_ == 1:
                                env.inc(sc.copy(tgt[:, b0:b0 + 128], pt[:, 0:128]), 'AC', 1)
                            elif mm_ > 0:
                                dst3 = tgt[:, b0:b0 + mm_ * step].rearrange(
                                    "p (a c) -> p a c", a=mm_)[:, :, 0:128]
                                src3 = pt[:, 0:mm_ * 128].rearrange("p (a c) -> p a c", a=mm_)
                                env.inc(sc.copy(dst3, src3), 'AC', 1)
                    else:
                        for j in range(n):
                            i = g0 + j
                            pt = ptA if j < 8 else ptB
                            env.inc(sc.copy(dst_fn(i), pt[:, (j % 8) * 128:(j % 8 + 1) * 128]), 'AC', 1)
                env.marks['pt_free'] = env.c['AC']
            return env.c['AC']

        @B.sync
        def _(sy):
            env.inc(sy.dma_start(out=idxs[:], in_=d["dnidx"].rearrange("(a b) c -> b (a c)", b=128),
                                 allow_slow_non_contiguous=True), 'LD', 16)
            env.inc(sy.dma_start(out=lams[:], in_=d["lam"]), 'LD', 16)
        ld_init = env.c['LD']

        @B.gpsimd
        def _(gp):
            gp.memset(ident[:], 1.0)
            gp.affine_select(ident[:], ident[:], [[-1, 128]], ALU.is_equal, 0.0,
                             base=0, channel_multiplier=1)
            gp.memset(maskc[:], 0.0)
            gp.affine_select(maskc[:], maskc[:], [[-1, 128]], ALU.is_ge, NEG,
                             base=0, channel_multiplier=1)
            gp.memset(maska[:], 0.0)
            env.inc(gp.affine_select(maska[:], maska[:], [[1, 128]], ALU.is_ge, NEG,
                                     base=-1, channel_multiplier=-1), 'GP', 1)

        @B.vector
        def _(v):
            v.memset(onesc[:], 1.0)
            env.inc(v.memset(stats[:, 5:6], 1e-5), 'DV', 1)


        @B.tensor
        def _(t):
            t.wait_ge(env.s['LD'], ld_init)
            t.wait_ge(env.s['DV'], env.c['DV'])
            env.inc(t.matmul(psB[:, 0:NH], onesc[:], lams[:], start=True, stop=True), 'PE', 1)

        @B.vector
        def _(v):
            v.wait_ge(env.s['PE'], env.c['PE'])
            env.inc(v.tensor_copy(lamb[:], psB[:, 0:NH]), 'DV', 1)

        env.barrier()

        x0_ap = d["x0"]
        wr2t = wtrans[:, 16128:16384]

        def ln_quant_phase(x_src, layer):
            for tt in range(TT):
                xw = wstage[:, 0:2048]
                xnf = wstage[:, 2048:4096]
                mu = stats[:, 0:1]; ss = stats[:, 1:2]; var = stats[:, 2:3]
                sd = stats[:, 3:4]; r0 = stats[:, 4:5]
                t1 = stats[:, 6:7]; t2 = stats[:, 7:8]
                rmax = stats[:, 8:9]; q1 = stats[:, 9:10]; q2 = stats[:, 10:11]

                @B.sync
                def _(sy, tt=tt):
                    if tt >= 1:
                        sy.wait_ge(env.s['DV'], env.c['DV'])
                        sy.wait_ge(env.s['AC'], env.c['AC'])
                    if layer == 1:
                        sy.wait_ge(env.s['LG'], env.c['LG'])
                    env.inc(sy.dma_start(out=xw, in_=x_src[tt * 128:(tt + 1) * 128, :]), 'LD', 16)
                ldx = env.c['LD']

                @B.vector
                def _(v):
                    v.wait_ge(env.s['LD'], ldx)
                    env.inc(v.reduce_sum(mu, xw, axis=AX.X), 'DV', 1)

                @B.scalar
                def _(sc):
                    sc.wait_ge(env.s['LD'], ldx)
                    env.inc(sc.activation(wtrans[:, 0:2048], xw, AF.Square, bias=0.0, scale=1.0,
                                          accum_out=ss), 'AC', 1)
                acs = env.c['AC']

                @B.vector
                def _(v):
                    v.wait_ge(env.s['AC'], acs)
                    v.tensor_scalar_mul(mu, mu, 1.0 / 2048.0)
                    v.drain()
                    v.tensor_scalar_mul(var, ss, 1.0 / 2048.0)
                    v.tensor_mul(t1, mu, mu)
                    v.drain()
                    env.inc(v.tensor_sub(var, var, t1), 'DV', 1)
                    v.drain()
                dvv = env.c['DV']

                @B.scalar
                def _(sc):
                    sc.wait_ge(env.s['DV'], dvv)
                    env.inc(sc.activation(sd, var, AF.Sqrt, bias=stats[:, 5:6], scale=1.0), 'AC', 1)
                acsd = env.c['AC']

                @B.vector
                def _(v):
                    v.wait_ge(env.s['AC'], acsd)
                    v.reciprocal(r0, sd)
                    v.drain()
                    v.tensor_scalar_add(t2, var, 1e-5)
                    v.drain()
                    for _i in range(2):
                        v.tensor_mul(t1, r0, r0)
                        v.drain()
                        v.tensor_mul(t1, t1, t2)
                        v.drain()
                        v.tensor_scalar(t1, t1, -0.5, 1.5, op0=ALU.mult, op1=ALU.add)
                        v.drain()
                        v.tensor_mul(r0, r0, t1)
                        v.drain()
                    v.tensor_scalar(xnf, xw, mu, r0, op0=ALU.subtract, op1=ALU.mult)
                    v.drain()
                    env.inc(v.tensor_copy(stgb[:], xnf), 'DV', 1)
                dvxn = env.c['DV']

                transpose_blocks(lambda i: stgb[:, i * 128:(i + 1) * 128], IC, None,
                                 wait_dv=dvxn,
                                 dst_batch=(bigA, lambda i0, tt=tt: i0 * 2048 + tt * 128, 2048))
                acev = env.c['AC']

                if layer == 0:
                    @B.vector
                    def _(v):
                        v.wait_ge(env.s['DV'], dvxn)
                        v.wait_ge(env.s['AC'], acev)
                        env.inc(v.reduce_max(rmax, xnf, axis=AX.X, apply_absolute_value=True), 'DV', 1)
                        v.drain()
                        v.reciprocal(q1, rmax)
                        v.drain()
                        v.tensor_scalar_mul(q1, q1, 224.0)
                        v.drain()
                        v.tensor_scalar_mul(q2, rmax, 1.0 / 224.0)
                        v.drain()
                        env.inc(v.tensor_scalar_mul(f8s[:, 0:2048], xnf, q1), 'DV', 1)
                        v.drain()
                        env.inc(v.tensor_scalar_mul(stgb[:], f8s[:, 0:2048], q2), 'DV', 1)
                    dvq = env.c['DV']
                    transpose_blocks(lambda i: stgb[:, i * 128:(i + 1) * 128], IC, None,
                                     wait_dv=dvq,
                                     dst_batch=(bigC, lambda i0, tt=tt: i0 * 2048 + tt * 128, 2048))
                else:
                    lob = stg5[:, 0:2048]
                    @B.vector
                    def _(v):
                        v.wait_ge(env.s['AC'], acev)
                        v.drain()
                        env.inc(v.tensor_tensor(lob, xnf, stgb[:], op=ALU.subtract), 'DV', 1)
                        v.drain()
                    dvlo = env.c['DV']
                    transpose_blocks(lambda i: lob[:, i * 128:(i + 1) * 128], IC, None,
                                     wait_dv=dvlo,
                                     dst_batch=(stg5, lambda i0: 2048 + i0 * 128, 128))
                    aclo = env.c['AC']

                    @B.tensor
                    def _(t, tt=tt):
                        t.wait_ge(env.s['AC'], aclo)
                        for i in range(IC):
                            t.matmul(psB[:, 0:16], bigA[:, i * 2048 + tt * 128: i * 2048 + tt * 128 + 128],
                                     wr2t[:, i * 16:(i + 1) * 16], start=(i == 0), stop=False)
                        for i in range(IC):
                            env.inc(t.matmul(psB[:, 0:16], stg5[:, 2048 + i * 128:2048 + (i + 1) * 128],
                                             wr2t[:, i * 16:(i + 1) * 16],
                                             start=False, stop=(i == IC - 1)), 'PE', 1)
                    perl = env.c['PE']

                    @B.vector
                    def _(v, tt=tt):
                        v.wait_ge(env.s['PE'], perl)
                        lg = stats[:, 16:24]
                        m8 = stats[:, 24:32]
                        v.tensor_copy(lg, psB[:, 0:8])
                        v.drain()
                        v.tensor_tensor(lg, lg, psB[:, 8:16], op=ALU.add)
                        v.drain()
                        v.max(m8, lg)
                        v.drain()
                        nt = stats[:, 32:33]
                        v.memset(nt, 1.0)
                        v.drain()
                        for e in range(8):
                            he = stats[:, 33:34]
                            v.tensor_tensor(he, lg[:, e:e + 1], m8[:, 0:1], op=ALU.is_equal)
                            v.drain()
                            v.tensor_mul(selm[:, tt * 8 + e: tt * 8 + e + 1], he, nt)
                            v.drain()
                            v.tensor_sub(nt, nt, selm[:, tt * 8 + e: tt * 8 + e + 1])
                            v.drain()
                        env.inc(v.engine_nop(), 'DV', 1)

        ln_quant_phase(x0_ap, 0)
        env.barrier()
        if _DEBUG_SPILL[0]:
            @B.gpsimd
            def _(gp):
                env.inc(gp.dma_start(out=dbg_a, in_=bigA[:]), 'LG', 16)
                env.inc(gp.dma_start(out=dbg_c, in_=bigC[:]), 'LG', 16)
                gp.wait_ge(env.s['LG'], env.c['LG'])
            env.barrier()

        def l0_proj_fm(w_ap, sp_dst):
            for oc in range(16):
                buf = oc % 2
                ws = wstage[:, buf * 2048:(buf + 1) * 2048]
                st0 = 8192 * buf
                @B.sync
                def _(sy, oc=oc):
                    sy.wait_ge(env.s['DV'], env.c['DV'])
                    env.inc(sy.dma_start(out=ws, in_=w_ap[oc * 128:(oc + 1) * 128, :]), 'LD', 16)
                ldw = env.c['LD']
                @B.vector
                def _(v, oc=oc):
                    v.wait_ge(env.s['LD'], ldw)
                    v.wait_ge(env.s['PE'], env.c['PE'])
                    env.inc(v.tensor_copy(stgb[:], ws), 'DV', 1)
                dvc = env.c['DV']
                def wx(t):
                    pass
                transpose_blocks(lambda i: stgb[:, i * 128:(i + 1) * 128], IC, None,
                                 wait_dv=dvc,
                                 dst_batch=(wtrans, lambda i0, st0=st0: st0 + i0 * 128, 128))
                acst = env.c['AC']
                for tc in range(4):
                    ps = psB if tc % 2 == 0 else psC
                    @B.tensor
                    def _(t, tc=tc, oc=oc, st0=st0):
                        t.wait_ge(env.s['AC'], acst)
                        if tc >= 2 or oc > 0:
                            t.wait_ge(env.s['AC'], env.c['AC'])
                        for i in range(IC):
                            env.inc(t.matmul(ps[:], wtrans[:, st0 + i * 128: st0 + (i + 1) * 128],
                                             bigA[:, i * 2048 + tc * 512: i * 2048 + (tc + 1) * 512],
                                             start=(i == 0), stop=(i == IC - 1)), 'PE', 1)
                    pemm = env.c['PE']
                    eb = evb if tc % 2 == 0 else evb2
                    @B.scalar
                    def _(sc):
                        sc.wait_ge(env.s['PE'], pemm)
                        sc.wait_ge(env.s['LG'], env.c['LG'])
                        env.inc(sc.copy(eb[:, 0:512], ps[:]), 'AC', 1)
                    acev2 = env.c['AC']
                    @B.gpsimd
                    def _(gp, oc=oc, tc=tc, eb=eb):
                        gp.wait_ge(env.s['AC'], acev2)
                        env.inc(gp.dma_start(out=sp_dst[oc * 128:(oc + 1) * 128, tc * 512:(tc + 1) * 512],
                                             in_=eb[:, 0:512]), 'LG', 16)

        l0_proj_fm(d["l0wq"], d["sp_q"])
        l0_proj_fm(d["l0wk"], d["sp_k"])

        for oc4 in range(4):
            for rt in range(4):
                ws = wstage[:, (rt % 2) * 2048:(rt % 2 + 1) * 2048]
                @B.sync
                def _(sy):
                    sy.wait_ge(env.s['DV'], env.c['DV'])
                    env.inc(sy.dma_start(out=ws, in_=d["l0wv"][(oc4 * 4 + rt) * 128:(oc4 * 4 + rt + 1) * 128, :]), 'LD', 16)
                ldw = env.c['LD']
                @B.vector
                def _(v):
                    v.wait_ge(env.s['LD'], ldw)
                    v.wait_ge(env.s['PE'], env.c['PE'])
                    env.inc(v.tensor_copy(stgb[:], ws), 'DV', 1)
                dvc = env.c['DV']
                transpose_blocks(lambda i: stgb[:, i * 128:(i + 1) * 128], IC, None,
                                 wait_dv=dvc,
                                 dst_batch=(wtrans, lambda i0, rt=rt: i0 * 512 + rt * 128, 512))
            acst = env.c['AC']
            for tt in range(TT):
                ps = psB if tt % 2 == 0 else psC
                @B.tensor
                def _(t, tt=tt, oc4=oc4):
                    t.wait_ge(env.s['AC'], acst)
                    if tt >= 2 or oc4 > 0:
                        t.wait_ge(env.s['AC'], env.c['AC'])
                    for i in range(IC):
                        env.inc(t.matmul(ps[:], bigA[:, i * 2048 + tt * 128: i * 2048 + tt * 128 + 128],
                                         wtrans[:, i * 512:(i + 1) * 512],
                                         start=(i == 0), stop=(i == IC - 1)), 'PE', 1)
                pemm = env.c['PE']
                eb = evb if tt % 2 == 0 else evb2
                @B.scalar
                def _(sc):
                    sc.wait_ge(env.s['PE'], pemm)
                    sc.wait_ge(env.s['LG'], env.c['LG'])
                    env.inc(sc.copy(eb[:, 0:512], ps[:]), 'AC', 1)
                acev2 = env.c['AC']
                @B.gpsimd
                def _(gp, tt=tt, oc4=oc4, eb=eb):
                    gp.wait_ge(env.s['AC'], acev2)
                    env.inc(gp.dma_start(out=d["sp_v"][tt * 128:(tt + 1) * 128, oc4 * 512:(oc4 + 1) * 512],
                                         in_=eb[:, 0:512]), 'LG', 16)

        env.barrier()

        qh = bigA[:, 0:2048]
        kh = bigA[:, 2048:4096]
        vh = bigA[:, 4096:6144]
        ptl = bigA[:, 6144:8192]
        pTs = bigA[:, 8192:16384]
        for h in range(NH):
            @B.sync
            def _(sy, h=h):
                if h >= 1:
                    sy.wait_ge(env.s['PE'], env.c['PE'])
                sy.wait_ge(env.s['LG'], env.c['LG'])
                env.inc(sy.dma_start(out=qh, in_=d["sp_q"][h * 128:(h + 1) * 128, :]), 'LD', 16)
                env.inc(sy.dma_start(out=kh, in_=d["sp_k"][h * 128:(h + 1) * 128, :]), 'LD', 16)
                env.inc(sy.dma_start(
                    out=vh.rearrange("p (a c) -> p a c", a=TT),
                    in_=d["sp_v"].rearrange("(a p) o -> p a o", p=128)[:, :, h * 128:(h + 1) * 128]), 'LD', 16)
            ldh = env.c['LD']

            for qg in range(4):
                for qi in range(4):
                    qt = qg * 4 + qi
                    @B.tensor
                    def _(t, qt=qt, h=h):
                        t.wait_ge(env.s['LD'], ldh)
                        if not (h == 0 and qt == 0):
                            t.wait_ge(env.s['DV'], env.c['DV'])
                        for kc in range(4):
                            env.inc(t.matmul(psA[:, kc * 512:(kc + 1) * 512],
                                             qh[:, qt * 128:(qt + 1) * 128],
                                             kh[:, kc * 512:(kc + 1) * 512],
                                             start=True, stop=True), 'PE', 1)
                    pesc = env.c['PE']

                    m0 = stats[:, 40:41]; nm = stats[:, 41:42]
                    zz = stats[:, 42:43]; zi = stats[:, 43:44]
                    @B.vector
                    def _(v):
                        v.wait_ge(env.s['PE'], pesc)
                        v.reduce_max(m0, psA[:], axis=AX.X)
                        v.drain()
                        env.inc(v.tensor_scalar_mul(nm, m0, -1.0 / math.sqrt(128.0)), 'DV', 1)
                        v.drain()
                    dvm = env.c['DV']

                    @B.scalar
                    def _(sc):
                        sc.wait_ge(env.s['DV'], dvm)
                        env.inc(sc.activation(ptl[:], psA[:], AF.Exp,
                                              bias=nm, scale=1.0 / math.sqrt(128.0),
                                              accum_out=zz), 'AC', 1)
                    acex = env.c['AC']

                    @B.vector
                    def _(v):
                        v.wait_ge(env.s['AC'], acex)
                        v.reciprocal(zi, zz)
                        v.drain()
                        env.inc(v.tensor_scalar_mul(ptl[:], ptl[:], zi), 'DV', 1)
                    dvp = env.c['DV']

                    transpose_blocks(lambda i: ptl[:, i * 128:(i + 1) * 128], 16, None,
                                     wait_dv=dvp,
                                     dst_batch=(bigA, lambda i0, qi=qi: 8192 + i0 * 512 + qi * 128, 512))
                acpt = env.c['AC']

                ps = psB if qg % 2 == 0 else psC
                @B.tensor
                def _(t, qg=qg, h=h):
                    t.wait_ge(env.s['AC'], acpt)
                    if qg >= 2 or h > 0:
                        t.wait_ge(env.s['AC'], env.c['AC'])
                    for kt in range(16):
                        env.inc(t.matmul(ps[:], vh[:, kt * 128:(kt + 1) * 128],
                                         pTs[:, kt * 512:(kt + 1) * 512],
                                         start=(kt == 0), stop=(kt == 15)), 'PE', 1)
                pepv = env.c['PE']

                eb = evb if qg % 2 == 0 else evb2
                @B.scalar
                def _(sc):
                    sc.wait_ge(env.s['PE'], pepv)
                    sc.wait_ge(env.s['LG'], env.c['LG'])
                    env.inc(sc.copy(eb[:, 0:512], ps[:]), 'AC', 1)
                acao = env.c['AC']
                @B.gpsimd
                def _(gp, h=h, qg=qg, eb=eb):
                    gp.wait_ge(env.s['AC'], acao)
                    env.inc(gp.dma_start(out=d["sp_ao"][h * 128:(h + 1) * 128, qg * 512:(qg + 1) * 512],
                                         in_=eb[:, 0:512]), 'LG', 16)

        env.barrier()

        for ob in range(16):
            buf = ob % 2
            ws = wstage[:, buf * 2048:(buf + 1) * 2048]
            @B.sync
            def _(sy, ob=ob):
                sy.wait_ge(env.s['DV'], env.c['DV'])
                env.inc(sy.dma_start(out=ws, in_=d["l0wo"][ob * 128:(ob + 1) * 128, :]), 'LD', 16)
            ldw = env.c['LD']
            @B.vector
            def _(v, ob=ob):
                v.wait_ge(env.s['LD'], ldw)
                v.wait_ge(env.s['PE'], env.c['PE'])
                env.inc(v.tensor_copy(stgb[:], ws), 'DV', 1)
            dvc = env.c['DV']
            transpose_blocks(lambda i: stgb[:, i * 128:(i + 1) * 128], IC, None,
                             wait_dv=dvc,
                             dst_batch=(bigA, lambda i0, ob=ob: i0 * 2048 + ob * 128, 2048))
        acwo = env.c['AC']

        for tt in range(TT):
            aosl = stgb
            @B.sync
            def _(sy, tt=tt):
                sy.wait_ge(env.s['PE'], env.c['PE'])
                sy.wait_ge(env.s['LG'], env.c['LG'])
                env.inc(sy.dma_start(
                    out=aosl.rearrange("p (a c) -> p a c", a=16),
                    in_=d["sp_ao"].rearrange("(a p) t -> p a t", p=128)[:, :, tt * 128:(tt + 1) * 128]), 'LD', 16)
            ldao = env.c['LD']
            for oc4 in range(4):
                ps = psB if oc4 % 2 == 0 else psC
                @B.tensor
                def _(t, tt=tt, oc4=oc4):
                    t.wait_ge(env.s['LD'], ldao)
                    t.wait_ge(env.s['AC'], acwo)
                    if oc4 >= 2 or tt > 0:
                        t.wait_ge(env.s['AC'], env.c['AC'])
                    for hh in range(16):
                        env.inc(t.matmul(ps[:], aosl[:, hh * 128:(hh + 1) * 128],
                                         bigA[:, hh * 2048 + oc4 * 512: hh * 2048 + (oc4 + 1) * 512],
                                         start=(hh == 0), stop=(hh == 15)), 'PE', 1)
                pemm = env.c['PE']
                eb = evb if oc4 % 2 == 0 else evb2
                @B.scalar
                def _(sc):
                    sc.wait_ge(env.s['PE'], pemm)
                    sc.wait_ge(env.s['LG'], env.c['LG'])
                    env.inc(sc.copy(eb[:, 0:512], ps[:]), 'AC', 1)
                acs2 = env.c['AC']
                @B.gpsimd
                def _(gp, tt=tt, oc4=oc4, eb=eb):
                    gp.wait_ge(env.s['AC'], acs2)
                    env.inc(gp.dma_start(out=d["sp_s"][tt * 128:(tt + 1) * 128, oc4 * 512:(oc4 + 1) * 512],
                                         in_=eb[:, 0:512]), 'LG', 16)

        env.barrier()
        if _DEBUG_SPILL[0]:
            @B.gpsimd
            def _(gp):
                env.inc(gp.dma_start(out=dbg_q, in_=d["sp_q"]), 'LG', 16)
                env.inc(gp.dma_start(out=dbg_ao, in_=d["sp_ao"]), 'LG', 16)
                env.inc(gp.dma_start(out=dbg_s, in_=d["sp_s"]), 'LG', 16)
                gp.wait_ge(env.s['LG'], env.c['LG'])
            env.barrier()

        @B.vector
        def _(v):
            env.inc(v.memset(tmaxs[:], 0.0), 'DV', 1)

        for jc in range(11):
            buf = jc % 2
            for gi, w_ap in ((0, d["wg"]), (1, d["wu"])):
                strip0 = (gi * 2 + buf) * 8192
                for rt in range(4):
                    ws = wstage[:, (rt % 2) * 2048:(rt % 2 + 1) * 2048]
                    @B.gpsimd
                    def _(gp, jc=jc, rt=rt, w_ap=w_ap):
                        gp.wait_ge(env.s['DV'], env.c['DV'])
                        env.inc(gp.indirect_dma_start(
                            ws, None, w_ap,
                            IndirectOffsetOnAxis(idxs[:, jc * 4 + rt: jc * 4 + rt + 1], 0)), 'LG', 16)
                    ldw = env.c['LG']
                    wmx = stats[:, 44:45]; s1 = stats[:, 45:46]; s2 = stats[:, 46:47]
                    @B.vector
                    def _(v):
                        v.wait_ge(env.s['LG'], ldw)
                        v.wait_ge(env.s['PE'], env.c['PE'])
                        v.reduce_max(wmx, ws, axis=AX.X, apply_absolute_value=True)
                        v.drain()
                        v.reciprocal(s1, wmx)
                        v.drain()
                        v.tensor_scalar_mul(s1, s1, 224.0)
                        v.drain()
                        v.tensor_scalar_mul(s2, wmx, 1.0 / 224.0)
                        v.drain()
                        v.tensor_scalar_mul(f8s[:, 0:2048], ws, s1)
                        v.drain()
                        env.inc(v.tensor_scalar_mul(stgb[:], f8s[:, 0:2048], s2), 'DV', 1)
                    dvq = env.c['DV']
                    transpose_blocks(lambda i: stgb[:, i * 128:(i + 1) * 128], IC, None,
                                     wait_dv=dvq,
                                     dst_batch=(bigA, lambda i0, strip0=strip0, rt=rt: strip0 + i0 * 512 + rt * 128, 512))
            acst = env.c['AC']

            for tt in range(TT):
                pg = psA[:, (tt % 2) * 1024:(tt % 2) * 1024 + 512]
                pu = psA[:, (tt % 2) * 1024 + 512:(tt % 2) * 1024 + 1024]
                @B.tensor
                def _(t, tt=tt, jc=jc, buf=buf):
                    t.wait_ge(env.s['AC'], acst)
                    if tt >= 2 or jc > 0:
                        t.wait_ge(env.s['DV'], env.c['DV'])
                    for i in range(IC):
                        t.matmul(pg, bigC[:, i * 2048 + tt * 128: i * 2048 + tt * 128 + 128],
                                 bigA[:, (0 * 2 + buf) * 8192 + i * 512:(0 * 2 + buf) * 8192 + (i + 1) * 512],
                                 start=(i == 0), stop=(i == IC - 1))
                        env.inc(t.matmul(pu, bigC[:, i * 2048 + tt * 128: i * 2048 + tt * 128 + 128],
                                         bigA[:, (1 * 2 + buf) * 8192 + i * 512:(1 * 2 + buf) * 8192 + (i + 1) * 512],
                                         start=(i == 0), stop=(i == IC - 1)), 'PE', 1)
                pemm = env.c['PE']
                gsb = sg0 if tt % 2 == 0 else sg1
                @B.scalar
                def _(sc, pg=pg):
                    sc.wait_ge(env.s['PE'], pemm)
                    env.inc(sc.activation(gsb[:], pg, AF.Silu, bias=0.0, scale=1.0), 'AC', 1)
                acsl = env.c['AC']
                tch = stats[:, 48:49]
                @B.vector
                def _(v, tt=tt, jc=jc, pu=pu, gsb=gsb):
                    v.wait_ge(env.s['AC'], acsl)
                    if tt >= 2 or jc > 0:
                        v.wait_ge(env.s['LG'], env.c['LG'])
                    v.tensor_tensor(gsb[:], gsb[:], pu, op=ALU.mult)
                    v.drain()
                    v.reduce_max(tch, gsb[:], axis=AX.X, apply_absolute_value=True)
                    v.drain()
                    env.inc(v.tensor_tensor(tmaxs[:, tt:tt + 1], tmaxs[:, tt:tt + 1], tch, op=ALU.max), 'DV', 1)
                    v.drain()
                dvt = env.c['DV']
                @B.gpsimd
                def _(gp, tt=tt, jc=jc, gsb=gsb):
                    gp.wait_ge(env.s['DV'], dvt)
                    env.inc(gp.dma_start(out=d["sp_T"][tt * 128:(tt + 1) * 128, jc * 512:(jc + 1) * 512],
                                         in_=gsb[:]), 'LG', 16)

        env.barrier()

        for tt in range(TT):
            Tw = wstage[:, 0:5632]
            @B.sync
            def _(sy, tt=tt):
                if tt >= 1:
                    sy.wait_ge(env.s['DV'], env.c['DV'])
                sy.wait_ge(env.s['LG'], env.c['LG'])
                env.inc(sy.dma_start(out=Tw, in_=d["sp_T"][tt * 128:(tt + 1) * 128, :]), 'LD', 16)
            ldt = env.c['LD']
            q1 = stats[:, 49:50]; q2 = stats[:, 50:51]
            @B.vector
            def _(v, tt=tt):
                v.wait_ge(env.s['LD'], ldt)
                if tt >= 1:
                    v.wait_ge(env.s['LG'], env.c['LG'])
                v.reciprocal(q1, tmaxs[:, tt:tt + 1])
                v.drain()
                v.tensor_scalar_mul(q1, q1, 224.0)
                v.drain()
                v.tensor_scalar_mul(q2, tmaxs[:, tt:tt + 1], 1.0 / 224.0)
                v.drain()
                v.tensor_scalar_mul(f8s[:], Tw, q1)
                v.drain()
                env.inc(v.tensor_scalar_mul(stg5[:], f8s[:], q2), 'DV', 1)
            dvq = env.c['DV']
            transpose_blocks(lambda i: stg5[:, i * 128:(i + 1) * 128], NJT, None,
                             wait_dv=dvq,
                             dst_batch=(bigA, lambda i0: i0 * 128, 128))
            acq = env.c['AC']
            @B.gpsimd
            def _(gp, tt=tt):
                gp.wait_ge(env.s['AC'], acq)
                env.inc(gp.dma_start(
                    out=d["sp_qtT"].rearrange("(a p) t -> p a t", p=128)[:, :, tt * 128:(tt + 1) * 128],
                    in_=bigA[:, 0:5632].rearrange("p (a c) -> p a c", a=NJT)), 'LG', 16)

        env.barrier()

        for hc in range(4):
            for rt in range(4):
                ws = wstage[:, 0:5632]
                @B.sync
                def _(sy, hc=hc, rt=rt):
                    sy.wait_ge(env.s['DV'], env.c['DV'])
                    env.inc(sy.dma_start(out=ws, in_=d["wd"][(hc * 4 + rt) * 128:(hc * 4 + rt + 1) * 128, :]), 'LD', 16)
                ldw = env.c['LD']
                wmx = stats[:, 51:52]; s1 = stats[:, 52:53]; s2 = stats[:, 53:54]
                @B.vector
                def _(v):
                    v.wait_ge(env.s['LD'], ldw)
                    v.wait_ge(env.s['PE'], env.c['PE'])
                    v.reduce_max(wmx, ws, axis=AX.X, apply_absolute_value=True)
                    v.drain()
                    v.reciprocal(s1, wmx)
                    v.drain()
                    v.tensor_scalar_mul(s1, s1, 224.0)
                    v.drain()
                    v.tensor_scalar_mul(s2, wmx, 1.0 / 224.0)
                    v.drain()
                    v.tensor_scalar_mul(f8s[:], ws, s1)
                    v.drain()
                    env.inc(v.tensor_scalar_mul(stg5[:], f8s[:], s2), 'DV', 1)
                dvq = env.c['DV']
                transpose_blocks(lambda i: stg5[:, i * 128:(i + 1) * 128], NJT, None,
                                 wait_dv=dvq,
                                 dst_batch=(bigA, lambda i0, rt=rt: i0 * 512 + rt * 128, 512))
            acst = env.c['AC']
            for tt in range(TT):
                @B.sync
                def _(sy, hc=hc, tt=tt):
                    sy.wait_ge(env.s['PE'], env.c['PE'])
                    sy.wait_ge(env.s['LG'], env.c['LG'])
                    env.inc(sy.dma_start(
                        out=stg5[:].rearrange("p (a c) -> p a c", a=NJT),
                        in_=d["sp_qtT"].rearrange("(a p) t -> p a t", p=128)[:, :, tt * 128:(tt + 1) * 128]), 'LD', 16)
                    env.inc(sy.dma_start(out=sx0[:], in_=x0_ap[tt * 128:(tt + 1) * 128, hc * 512:(hc + 1) * 512]), 'LD', 16)
                    env.inc(sy.dma_start(out=ssb[:], in_=d["sp_s"][tt * 128:(tt + 1) * 128, hc * 512:(hc + 1) * 512]), 'LD', 16)
                ldq = env.c['LD']
                ps = psB if tt % 2 == 0 else psC
                @B.tensor
                def _(t, hc=hc, tt=tt):
                    t.wait_ge(env.s['LD'], ldq)
                    t.wait_ge(env.s['AC'], acst)
                    if tt >= 2 or hc > 0:
                        t.wait_ge(env.s['DV'], env.c['DV'])
                    for i in range(NJT):
                        env.inc(t.matmul(ps[:], stg5[:, i * 128:(i + 1) * 128],
                                         bigA[:, i * 512:(i + 1) * 512],
                                         start=(i == 0), stop=(i == NJT - 1)), 'PE', 1)
                pemm = env.c['PE']
                gsb = sg0 if tt % 2 == 0 else sg1
                @B.scalar
                def _(sc, ps=ps, gsb=gsb):
                    sc.wait_ge(env.s['PE'], pemm)
                    env.inc(sc.activation(gsb[:], ps[:], AF.Silu, bias=0.0, scale=1.0), 'AC', 1)
                acg = env.c['AC']
                @B.vector
                def _(v, hc=hc, tt=tt, gsb=gsb):
                    v.wait_ge(env.s['AC'], acg)
                    v.wait_ge(env.s['LD'], ldq)
                    if tt >= 2 or hc > 0:
                        v.wait_ge(env.s['LG'], env.c['LG'])
                    v.tensor_tensor(gsb[:], gsb[:], ssb[:], op=ALU.mult)
                    v.drain()
                    env.inc(v.tensor_tensor(sx0[:], sx0[:], gsb[:], op=ALU.add), 'DV', 1)
                    v.drain()
                dvx = env.c['DV']
                @B.gpsimd
                def _(gp, hc=hc, tt=tt):
                    gp.wait_ge(env.s['DV'], dvx)
                    env.inc(gp.dma_start(out=d["sp_x1"][tt * 128:(tt + 1) * 128, hc * 512:(hc + 1) * 512],
                                         in_=sx0[:]), 'LG', 16)

        env.barrier()

        @B.sync
        def _(sy):
            env.inc(sy.dma_start(out=wstage[0:8, 0:2048], in_=d["wr"]), 'LD', 16)
        ldwr = env.c['LD']
        @B.vector
        def _(v):
            v.wait_ge(env.s['LD'], ldwr)
            v.tensor_copy(stgb[0:8, 0:2048], wstage[0:8, 0:2048])
            v.drain()
            env.inc(v.tensor_tensor(stg5[0:8, 0:2048], wstage[0:8, 0:2048], stgb[0:8, 0:2048],
                                    op=ALU.subtract), 'DV', 1)
            v.drain()
        dvwr = env.c['DV']
        @B.tensor
        def _(t):
            t.wait_ge(env.s['DV'], dvwr)
            t.wait_ge(env.s['AC'], env.marks['pt_free'])
            for i in range(IC):
                pt = ptA if i < 8 else ptB
                t.transpose(pt[:, (i % 8) * 128:(i % 8) * 128 + 8],
                            stgb[0:8, i * 128:(i + 1) * 128], ident[0:8, 0:8])
                env.inc(t.transpose(pt[:, (i % 8) * 128 + 8:(i % 8) * 128 + 16],
                                    stg5[0:8, i * 128:(i + 1) * 128], ident[0:8, 0:8]), 'PE', 1)
        pewr = env.c['PE']
        @B.scalar
        def _(sc):
            sc.wait_ge(env.s['PE'], pewr)
            for i in range(IC):
                pt = ptA if i < 8 else ptB
                env.inc(sc.copy(wr2t[:, i * 16:(i + 1) * 16],
                                pt[:, (i % 8) * 128:(i % 8) * 128 + 16]), 'AC', 1)
        env.marks['pt_free'] = env.c['AC']

        env.barrier()
        ln_quant_phase(d["sp_x1"], 1)
        env.barrier()

        cosfull = bigC[:, 16384:20480].bitcast(F32)
        sinfull = bigC[:, 20480:24576].bitcast(F32)
        @B.sync
        def _(sy):
            env.inc(sy.dma_start(out=cosfull.rearrange("p (a c) -> p a c", a=TT),
                                 in_=d["cosT"].rearrange("(a p) c -> p a c", p=128)), 'LD', 16)
            env.inc(sy.dma_start(out=sinfull.rearrange("p (a c) -> p a c", a=TT),
                                 in_=d["sinF"].rearrange("(a p) c -> p a c", p=128)), 'LD', 16)
        ldcs = env.c['LD']

        def rope_apply(v, ps, eb, t1, tt):
            cos4 = cosfull[:, tt * 128:(tt + 1) * 128].rearrange("p (o c) -> p o c", o=1).broadcast_to([128, 4, 128])
            v.tensor_tensor(t1[:].rearrange("p (h c) -> p h c", h=4), ps[:].rearrange("p (h c) -> p h c", h=4),
                            cos4, op=ALU.mult)
            pr = ps[:].rearrange("p (h t c) -> p h t c", h=4, t=2, c=64)
            ebr = eb[:, 0:512].rearrange("p (h t c) -> p h t c", h=4, t=2, c=64)
            sin4 = sinfull[:, tt * 128:(tt + 1) * 128].rearrange("p (o t c) -> p o t c", o=1, t=2)\
                .broadcast_to([128, 4, 2, 64])
            v.tensor_tensor(ebr[:, :, 0, :], pr[:, :, 1, :], sin4[:, :, 0, :], op=ALU.mult)
            v.tensor_tensor(ebr[:, :, 1, :], pr[:, :, 0, :], sin4[:, :, 1, :], op=ALU.mult)
            v.drain()
            inst = v.tensor_tensor(eb[:, 0:512], t1[:], eb[:, 0:512], op=ALU.add)
            v.drain()
            return inst

        for oc4 in range(4):
            buf = oc4 % 2
            for i in range(IC):
                ws = wstage[:, (i % 2) * 2048:(i % 2) * 2048 + 512]
                @B.sync
                def _(sy, i=i, oc4=oc4):
                    sy.wait_ge(env.s['DV'], env.c['DV'])
                    env.inc(sy.dma_start(out=ws, in_=d["l1wq"][i * 128:(i + 1) * 128, oc4 * 512:(oc4 + 1) * 512]), 'LD', 16)
                ldw = env.c['LD']
                @B.vector
                def _(v, i=i, buf=buf, oc4=oc4):
                    v.wait_ge(env.s['LD'], ldw)
                    if oc4 >= 2:
                        v.wait_ge(env.s['PE'], env.c['PE'])
                    env.inc(v.tensor_copy(wtrans[:, buf * 8192 + i * 512: buf * 8192 + (i + 1) * 512], ws), 'DV', 1)
            dvst = env.c['DV']
            for tt in range(TT):
                ps = psB if tt % 2 == 0 else psC
                @B.tensor
                def _(t, tt=tt, oc4=oc4, buf=buf):
                    t.wait_ge(env.s['DV'], dvst)
                    if tt >= 2 or oc4 > 0:
                        t.wait_ge(env.s['DV'], env.c['DV'])
                    for i in range(IC):
                        env.inc(t.matmul(ps[:], bigA[:, i * 2048 + tt * 128: i * 2048 + tt * 128 + 128],
                                         wtrans[:, buf * 8192 + i * 512: buf * 8192 + (i + 1) * 512],
                                         start=(i == 0), stop=(i == IC - 1)), 'PE', 1)
                pemm = env.c['PE']
                eb = evb if tt % 2 == 0 else evb2
                t1 = sg0 if tt % 2 == 0 else sg1
                @B.vector
                def _(v, tt=tt, ps=ps, eb=eb, t1=t1):
                    v.wait_ge(env.s['PE'], pemm)
                    v.wait_ge(env.s['LD'], ldcs)
                    v.wait_ge(env.s['LG'], env.c['LG'])
                    env.inc(rope_apply(v, ps, eb, t1, tt), 'DV', 1)
                dvr = env.c['DV']
                @B.gpsimd
                def _(gp, tt=tt, oc4=oc4, eb=eb):
                    gp.wait_ge(env.s['DV'], dvr)
                    env.inc(gp.dma_start(out=d["sp_q"][tt * 128:(tt + 1) * 128, oc4 * 512:(oc4 + 1) * 512],
                                         in_=eb[:, 0:512]), 'LG', 16)

        for which, w_ap in (("k", d["l1wk"]), ("v", d["l1wv"])):
            for i in range(IC):
                ws = wstage[:, (i % 2) * 2048:(i % 2) * 2048 + 512]
                @B.sync
                def _(sy, i=i, w_ap=w_ap):
                    sy.wait_ge(env.s['DV'], env.c['DV'])
                    env.inc(sy.dma_start(out=ws, in_=w_ap[i * 128:(i + 1) * 128, :]), 'LD', 16)
                ldw = env.c['LD']
                @B.vector
                def _(v, i=i):
                    v.wait_ge(env.s['LD'], ldw)
                    v.wait_ge(env.s['PE'], env.c['PE'])
                    env.inc(v.tensor_copy(wtrans[:, i * 512:(i + 1) * 512], ws), 'DV', 1)
            dvst = env.c['DV']
            for tt in range(TT):
                ps = psB if tt % 2 == 0 else psC
                @B.tensor
                def _(t, tt=tt):
                    t.wait_ge(env.s['DV'], dvst)
                    if tt >= 2:
                        t.wait_ge(env.s['DV'], env.c['DV'])
                        t.wait_ge(env.s['AC'], env.c['AC'])
                    for i in range(IC):
                        env.inc(t.matmul(ps[:], bigA[:, i * 2048 + tt * 128: i * 2048 + tt * 128 + 128],
                                         wtrans[:, i * 512:(i + 1) * 512],
                                         start=(i == 0), stop=(i == IC - 1)), 'PE', 1)
                pemm = env.c['PE']
                if which == "v":
                    @B.scalar
                    def _(sc, tt=tt, ps=ps):
                        sc.wait_ge(env.s['PE'], pemm)
                        env.inc(sc.copy(bigC[:, 8192 + tt * 512: 8192 + (tt + 1) * 512], ps[:]), 'AC', 1)
                else:
                    eb = evb if tt % 2 == 0 else evb2
                    t1 = sg0 if tt % 2 == 0 else sg1
                    @B.vector
                    def _(v, tt=tt, ps=ps, eb=eb, t1=t1):
                        v.wait_ge(env.s['PE'], pemm)
                        v.wait_ge(env.s['LD'], ldcs)
                        v.wait_ge(env.s['AC'], env.c['AC'])
                        env.inc(rope_apply(v, ps, eb, t1, tt), 'DV', 1)
                    dvr = env.c['DV']
                    @B.tensor
                    def _(t, tt=tt, eb=eb):
                        t.wait_ge(env.s['DV'], dvr)
                        t.wait_ge(env.s['AC'], env.marks['pt_free'])
                        for kv in range(4):
                            pt = ptA if kv < 2 else ptB
                            env.inc(t.transpose(pt[:, (kv % 2) * 128:(kv % 2 + 1) * 128],
                                                eb[:, kv * 128:(kv + 1) * 128], ident[:]), 'PE', 1)
                    pek = env.c['PE']
                    @B.scalar
                    def _(sc, tt=tt):
                        sc.wait_ge(env.s['PE'], pek)
                        for kv in range(4):
                            pt = ptA if kv < 2 else ptB
                            env.inc(sc.copy(bigC[:, kv * 2048 + tt * 128: kv * 2048 + tt * 128 + 128],
                                            pt[:, (kv % 2) * 128:(kv % 2 + 1) * 128]), 'AC', 1)
                    env.marks['pt_free'] = env.c['AC']

        env.barrier()

        qfm = bigC[:, 24576:26624]
        ptl1 = bigC[:, 26624:27392]
        ptl2 = bigC[:, 27392:28160]
        rT = bigC[:, 28160:28928]
        qtm = bigC[:, 28928:30976]
        for h in range(NH):
            kv = h // 4
            lamh = lamb[:, h:h + 1]
            @B.sync
            def _(sy, h=h):
                if h >= 1:
                    sy.wait_ge(env.s['PE'], env.c['PE'])
                sy.wait_ge(env.s['LG'], env.c['LG'])
                env.inc(sy.dma_start(
                    out=qtm.rearrange("p (a c) -> p a c", a=TT),
                    in_=d["sp_q"].rearrange("(a p) o -> p a o", p=128)[:, :, h * 128:(h + 1) * 128]), 'LD', 16)
            ldq = env.c['LD']
            @B.tensor
            def _(t):
                t.wait_ge(env.s['LD'], ldq)
            transpose_blocks(lambda i: qtm[:, i * 128:(i + 1) * 128], TT, None,
                             dst_batch=(bigC, lambda i0: 24576 + i0 * 128, 128))
            acq = env.c['AC']

            for qt in range(TT):
                j0 = max(0, qt - 4)
                nkt = qt - j0 + 1
                w = nkt * 128
                @B.tensor
                def _(t, qt=qt, h=h, j0=j0, nkt=nkt):
                    t.wait_ge(env.s['AC'], acq)
                    t.wait_ge(env.s['DV'], env.c['DV'])
                    for half, base in ((0, 0), (1, 1024)):
                        lhs = qfm[half * 64:half * 64 + 64, qt * 128:(qt + 1) * 128]
                        done = 0
                        while done < nkt:
                            take = min(4, nkt - done)
                            rhs = bigC[half * 64:half * 64 + 64,
                                       kv * 2048 + (j0 + done) * 128: kv * 2048 + (j0 + done + take) * 128]
                            env.inc(t.matmul(psA[:, base + done * 128: base + (done + take) * 128],
                                             lhs, rhs, start=True, stop=True,
                                             tile_position=(half * 64, 0)), 'PE', 1)
                            done += take
                pes = env.c['PE']
                m1 = stats[:, 54:55]; m2 = stats[:, 55:56]
                @B.vector
                def _(v, qt=qt, nkt=nkt, w=w):
                    v.wait_ge(env.s['PE'], pes)
                    for base in (0, 1024):
                        v.tensor_tensor(psA[:, base + (nkt - 1) * 128: base + nkt * 128],
                                        psA[:, base + (nkt - 1) * 128: base + nkt * 128],
                                        maskc[:], op=ALU.add)
                        if qt >= 4:
                            v.tensor_tensor(psA[:, base:base + 128],
                                            psA[:, base:base + 128],
                                            maska[:], op=ALU.add)
                    v.reduce_max(m1, psA[:, 0:w], axis=AX.X)
                    v.reduce_max(m2, psA[:, 1024:1024 + w], axis=AX.X)
                    v.tensor_scalar_mul(stats[:, 56:57], m1, -0.125)
                    env.inc(v.tensor_scalar_mul(stats[:, 57:58], m2, -0.125), 'DV', 1)
                dvmk = env.c['DV']
                z1 = stats[:, 58:59]; z2 = stats[:, 59:60]
                @B.scalar
                def _(sc, w=w):
                    sc.wait_ge(env.s['DV'], dvmk)
                    sc.activation(ptl1[:, 0:w], psA[:, 0:w], AF.Exp,
                                  bias=stats[:, 56:57], scale=0.125, accum_out=z1)
                    env.inc(sc.activation(ptl2[:, 0:w], psA[:, 1024:1024 + w], AF.Exp,
                                          bias=stats[:, 57:58], scale=0.125, accum_out=z2), 'AC', 1)
                acex = env.c['AC']
                @B.vector
                def _(v, w=w, lamh=lamh):
                    v.wait_ge(env.s['AC'], acex)
                    zi1 = stats[:, 60:61]; zi2 = stats[:, 61:62]
                    v.reciprocal(zi1, z1)
                    v.drain()
                    v.reciprocal(zi2, z2)
                    v.drain()
                    v.tensor_scalar_mul(ptl1[:, 0:w], ptl1[:, 0:w], zi1)
                    v.tensor_mul(zi2, zi2, lamh)
                    v.drain()
                    v.tensor_scalar_mul(zi2, zi2, -1.0)
                    v.drain()
                    v.scalar_tensor_tensor(ptl1[:, 0:w], ptl2[:, 0:w], zi2, ptl1[:, 0:w],
                                           op0=ALU.mult, op1=ALU.add)
                    v.drain()
                    ds = stats[:, 62:63]
                    v.tensor_scalar(ptl1[:, 0:w], ptl1[:, 0:w], 0.0, 0.0,
                                    op0=ALU.max, op1=ALU.add, accum_out=ds)
                    v.drain()
                    v.tensor_scalar_add(ds, ds, 1e-6)
                    v.drain()
                    ri = stats[:, 63:64]
                    v.reciprocal(ri, ds)
                    v.drain()
                    env.inc(v.tensor_scalar_mul(ptl1[:, 0:w], ptl1[:, 0:w], ri), 'DV', 1)
                    v.drain()
                dvr = env.c['DV']
                @B.tensor
                def _(t, nkt=nkt):
                    t.wait_ge(env.s['DV'], dvr)
                    t.wait_ge(env.s['AC'], env.marks['pt_free'])
                    for j in range(nkt):
                        pt = ptA if j < 4 else ptB
                        env.inc(t.transpose(pt[:, (j % 4) * 128:(j % 4 + 1) * 128],
                                            ptl1[:, j * 128:(j + 1) * 128], ident[:]), 'PE', 1)
                per = env.c['PE']
                @B.scalar
                def _(sc, nkt=nkt):
                    sc.wait_ge(env.s['PE'], per)
                    for j in range(nkt):
                        pt = ptA if j < 4 else ptB
                        env.inc(sc.copy(rT[:, j * 128:(j + 1) * 128],
                                        pt[:, (j % 4) * 128:(j % 4 + 1) * 128]), 'AC', 1)
                env.marks['pt_free'] = env.c['AC']
                acr = env.c['AC']
                ps = psB if qt % 2 == 0 else psC
                @B.tensor
                def _(t, qt=qt, h=h, j0=j0, nkt=nkt, kv=kv, ps=ps):
                    t.wait_ge(env.s['AC'], acr)
                    if qt >= 2 or h > 0:
                        t.wait_ge(env.s['AC'], env.c['AC'])
                    for j in range(nkt):
                        kt = j0 + j
                        env.inc(t.matmul(ps[:, 0:128],
                                         bigC[:, 8192 + kt * 512 + kv * 128: 8192 + kt * 512 + (kv + 1) * 128],
                                         rT[:, j * 128:(j + 1) * 128],
                                         start=(j == 0), stop=(j == nkt - 1)), 'PE', 1)
                pepv = env.c['PE']
                eb = evb if qt % 2 == 0 else evb2
                @B.scalar
                def _(sc, ps=ps, eb=eb):
                    sc.wait_ge(env.s['PE'], pepv)
                    sc.wait_ge(env.s['LG'], env.c['LG'])
                    env.inc(sc.copy(eb[:, 0:128], ps[:, 0:128]), 'AC', 1)
                acact = env.c['AC']
                @B.gpsimd
                def _(gp, h=h, qt=qt, eb=eb):
                    gp.wait_ge(env.s['AC'], acact)
                    env.inc(gp.dma_start(out=d["sp_ao"][h * 128:(h + 1) * 128, qt * 128:(qt + 1) * 128],
                                         in_=eb[:, 0:128]), 'LG', 16)

        env.barrier()

        for i in range(IC):
            ws = wstage[:, (i % 2) * 2048:(i % 2 + 1) * 2048]
            @B.sync
            def _(sy, i=i):
                if i >= 2:
                    sy.wait_ge(env.s['DV'], env.c['DV'])
                env.inc(sy.dma_start(out=ws, in_=d["l1wo"][i * 128:(i + 1) * 128, :]), 'LD', 16)
            ldw = env.c['LD']
            @B.vector
            def _(v, i=i):
                v.wait_ge(env.s['LD'], ldw)
                env.inc(v.tensor_copy(bigC[:, i * 2048:(i + 1) * 2048], ws), 'DV', 1)
        dvwo = env.c['DV']
        for tt in range(TT):
            aosl = stgb
            @B.sync
            def _(sy, tt=tt):
                sy.wait_ge(env.s['PE'], env.c['PE'])
                sy.wait_ge(env.s['LG'], env.c['LG'])
                env.inc(sy.dma_start(
                    out=aosl.rearrange("p (a c) -> p a c", a=16),
                    in_=d["sp_ao"].rearrange("(a p) t -> p a t", p=128)[:, :, tt * 128:(tt + 1) * 128]), 'LD', 16)
            ldao = env.c['LD']
            for oc4 in range(4):
                ps = psB if oc4 % 2 == 0 else psC
                @B.tensor
                def _(t, tt=tt, oc4=oc4, ps=ps):
                    t.wait_ge(env.s['LD'], ldao)
                    t.wait_ge(env.s['DV'], dvwo)
                    if oc4 >= 2 or tt > 0:
                        t.wait_ge(env.s['AC'], env.c['AC'])
                    for hh in range(16):
                        env.inc(t.matmul(ps[:], aosl[:, hh * 128:(hh + 1) * 128],
                                         bigC[:, hh * 2048 + oc4 * 512: hh * 2048 + (oc4 + 1) * 512],
                                         start=(hh == 0), stop=(hh == 15)), 'PE', 1)
                pemm = env.c['PE']
                eb = evb if oc4 % 2 == 0 else evb2
                @B.scalar
                def _(sc, ps=ps, eb=eb):
                    sc.wait_ge(env.s['PE'], pemm)
                    sc.wait_ge(env.s['LG'], env.c['LG'])
                    env.inc(sc.copy(eb[:, 0:512], ps[:]), 'AC', 1)
                acs2 = env.c['AC']
                @B.gpsimd
                def _(gp, tt=tt, oc4=oc4, eb=eb):
                    gp.wait_ge(env.s['AC'], acs2)
                    env.inc(gp.dma_start(out=d["sp_s"][tt * 128:(tt + 1) * 128, oc4 * 512:(oc4 + 1) * 512],
                                         in_=eb[:, 0:512]), 'LG', 16)

        env.barrier()

        @B.vector
        def _(v):
            env.inc(v.memset(bigC[:], 0.0), 'DV', 1)
            v.drain()
        for e in range(8):
            for fc in range(4):
                buf = fc % 2
                for rt in range(4):
                    ws = wstage[:, (rt % 2) * 2048:(rt % 2 + 1) * 2048]
                    @B.sync
                    def _(sy, e=e, fc=fc, rt=rt):
                        sy.wait_ge(env.s['DV'], env.c['DV'])
                        env.inc(sy.dma_start(out=ws, in_=d["wex"][e * 2048 + fc * 512 + rt * 128: e * 2048 + fc * 512 + (rt + 1) * 128, :]), 'LD', 16)
                    ldw = env.c['LD']
                    @B.vector
                    def _(v):
                        v.wait_ge(env.s['LD'], ldw)
                        v.wait_ge(env.s['PE'], env.c['PE'])
                        env.inc(v.tensor_copy(stgb[:], ws), 'DV', 1)
                    dvc = env.c['DV']
                    transpose_blocks(lambda i: stgb[:, i * 128:(i + 1) * 128], IC, None,
                                     wait_dv=dvc,
                                     dst_batch=(wtrans, lambda i0, buf=buf, rt=rt: buf * 8192 + i0 * 512 + rt * 128, 512))
                acst = env.c['AC']
                for tt in range(TT):
                    ps = psB if tt % 2 == 0 else psC
                    @B.tensor
                    def _(t, tt=tt, fc=fc, e=e, buf=buf, ps=ps):
                        t.wait_ge(env.s['AC'], acst)
                        if tt >= 2 or fc > 0 or e > 0:
                            t.wait_ge(env.s['DV'], env.c['DV'])
                        for i in range(IC):
                            env.inc(t.matmul(ps[:], bigA[:, i * 2048 + tt * 128: i * 2048 + tt * 128 + 128],
                                             wtrans[:, buf * 8192 + i * 512: buf * 8192 + (i + 1) * 512],
                                             start=(i == 0), stop=(i == IC - 1)), 'PE', 1)
                    pemm = env.c['PE']
                    @B.vector
                    def _(v, tt=tt, fc=fc, e=e, ps=ps):
                        v.wait_ge(env.s['PE'], pemm)
                        env.inc(v.scalar_tensor_tensor(
                            bigC[:, tt * 2048 + fc * 512: tt * 2048 + (fc + 1) * 512],
                            ps[:], selm[:, tt * 8 + e: tt * 8 + e + 1],
                            bigC[:, tt * 2048 + fc * 512: tt * 2048 + (fc + 1) * 512],
                            op0=ALU.mult, op1=ALU.add), 'DV', 1)

        env.barrier()

        for tt in range(TT):
            for oc4 in range(4):
                k = tt * 4 + oc4
                gsb = sg0 if k % 2 == 0 else sg1
                @B.sync
                def _(sy, tt=tt, oc4=oc4, k=k):
                    if k > 0:
                        sy.wait_ge(env.s['OT'], env.c['OT'])
                        sy.wait_ge(env.s['DV'], env.c['DV'])
                    sy.wait_ge(env.s['LG'], env.c['LG'])
                    env.inc(sy.dma_start(out=sx0[:], in_=d["sp_x1"][tt * 128:(tt + 1) * 128, oc4 * 512:(oc4 + 1) * 512]), 'LD', 16)
                    env.inc(sy.dma_start(out=ssb[:], in_=d["sp_s"][tt * 128:(tt + 1) * 128, oc4 * 512:(oc4 + 1) * 512]), 'LD', 16)
                ldxs = env.c['LD']
                @B.scalar
                def _(sc, tt=tt, oc4=oc4, gsb=gsb):
                    sc.wait_ge(env.s['LD'], ldxs)
                    sc.wait_ge(env.s['DV'], env.c['DV'])
                    env.inc(sc.activation(gsb[:], bigC[:, tt * 2048 + oc4 * 512: tt * 2048 + (oc4 + 1) * 512],
                                          AF.Silu, bias=0.0, scale=1.0), 'AC', 1)
                acg = env.c['AC']
                @B.vector
                def _(v, gsb=gsb):
                    v.wait_ge(env.s['AC'], acg)
                    v.wait_ge(env.s['LD'], ldxs)
                    v.tensor_tensor(gsb[:], gsb[:], ssb[:], op=ALU.mult)
                    v.drain()
                    env.inc(v.tensor_tensor(sx0[:], sx0[:], gsb[:], op=ALU.add), 'DV', 1)
                    v.drain()
                dvx = env.c['DV']
                @B.gpsimd
                def _(gp, tt=tt, oc4=oc4):
                    gp.wait_ge(env.s['DV'], dvx)
                    env.inc(gp.dma_start(out=out_ap[tt * 128:(tt + 1) * 128, oc4 * 512:(oc4 + 1) * 512],
                                         in_=sx0[:]), 'OT', 16)

        @B.gpsimd
        def _(gp):
            gp.wait_ge(env.s['OT'], env.c['OT'])

    return nc


_NC_CACHE = [None]


def kernel(**inputs):
    f32 = lambda a: np.ascontiguousarray(np.asarray(a, dtype=np.float32))
    x = f32(inputs["L0_attn_x"])
    hd = 128
    inv_freq = (1.0 / (10000.0 ** (np.arange(0, hd, 2, dtype=np.float32) / np.float32(hd)))).astype(np.float32)
    t = np.arange(S, dtype=np.float32)
    freqs = np.outer(t, inv_freq).astype(np.float32)
    emb = np.concatenate([freqs, freqs], axis=-1)
    cosT = np.ascontiguousarray(np.cos(emb).astype(np.float32))
    sinT = np.sin(emb).astype(np.float32)
    sinF = sinT.copy()
    sinF[:, :64] = -sinF[:, :64]
    sinF = np.ascontiguousarray(sinF)

    dn = np.ascontiguousarray(np.asarray(inputs["L0_ffn_down_reorder_index"], dtype=np.int32).reshape(IFF, 1))

    common = {
        "l0wq": f32(inputs["L0_attn_Wq"]), "l0wk": f32(inputs["L0_attn_Wk"]),
        "l0wv": f32(inputs["L0_attn_Wv"]), "l0wo": f32(inputs["L0_attn_Wo"]),
        "wg": f32(inputs["L0_ffn_w_gate"]), "wu": f32(inputs["L0_ffn_w_up"]),
        "wd": f32(inputs["L0_ffn_w_down"]), "dnidx": dn,
        "l1wq": f32(inputs["L1_attn_Wq"]), "l1wk": f32(inputs["L1_attn_Wk"]),
        "l1wv": f32(inputs["L1_attn_Wv"]), "l1wo": f32(inputs["L1_attn_Wo"]),
        "lam": f32(inputs["L1_attn_lam"]).reshape(1, NH),
        "wr": f32(inputs["L1_ffn_w_router"]),
        "wex": f32(inputs["L1_ffn_w_experts"]).reshape(8 * D, D),
        "cosT": cosT, "sinF": sinF,
    }
    if _NC_CACHE[0] is None:
        _NC_CACHE[0] = build_nc()
    nc = _NC_CACHE[0]
    in_maps = [dict(common, x0=np.ascontiguousarray(x[c])) for c in range(8)]
    res = run_bass_kernel_spmd(nc, in_maps, core_ids=list(range(8)))
    return np.stack([r["out"] for r in res.results], axis=0)


# version 07
# speedup vs baseline: 1.1807x; 1.1322x over previous
import math
import numpy as np
from contextlib import ExitStack

import concourse.bass as bass
import concourse.mybir as mybir
from concourse.bass_utils import run_bass_kernel_spmd
from concourse.bass import IndirectOffsetOnAxis

F32 = mybir.dt.float32
BF16 = mybir.dt.bfloat16
F8 = mybir.dt.float8e4
I32 = mybir.dt.int32
AF = mybir.ActivationFunctionType
ALU = mybir.AluOpType
AX = mybir.AxisListType

S = 2048
D = 2048
TT = 16
IC = 16
IFF = 5632
NJT = 44
NH = 16
HD = 128
NEG = -1.0e9


class Env:
    def __init__(self, nc, block, sems):
        self.nc = nc
        self.block = block
        self.s = sems
        self.c = {k: 0 for k in sems}

    def inc(self, inst, k, by=1):
        inst.then_inc(self.s[k], by)
        self.c[k] += by
        return self.c[k]

    def barrier(self):
        self.c['PH'] += 5
        tgt = self.c['PH']
        for attr in ('tensor', 'vector', 'scalar', 'gpsimd', 'sync'):
            def mk(eng, tgt=tgt):
                eng.nop().then_inc(self.s['PH'], 1)
                eng.wait_ge(self.s['PH'], tgt)
            getattr(self.block, attr)(mk)


_DEBUG_SPILL = [False]


def build_nc():
    nc = bass.Bass("TRN2", target_bir_lowering=False, debug=False)
    d = {}
    def inp(name, shape, dt=F32):
        d[name] = nc.dram_tensor(name, shape, dt, kind="ExternalInput").ap()
    inp("x0", [S, D])
    inp("l0wq", [D, D]); inp("l0wk", [D, D]); inp("l0wv", [D, D]); inp("l0wo", [D, D])
    inp("wg", [IFF, D]); inp("wu", [IFF, D]); inp("wd", [D, IFF])
    inp("dnidx", [IFF, 1], I32)
    inp("l1wq", [D, D]); inp("l1wk", [D, 512]); inp("l1wv", [D, 512]); inp("l1wo", [D, D])
    inp("lam", [1, NH]); inp("wr", [8, D]); inp("wex", [8 * D, D])
    inp("cosT", [S, HD]); inp("sinF", [S, HD])
    out_ap = nc.dram_tensor("out", [S, D], F32, kind="ExternalOutput").ap()
    def scr(name, shape, dt):
        if _DEBUG_SPILL[0]:
            d[name] = nc.dram_tensor(name, shape, dt, kind="ExternalOutput").ap()
        else:
            d[name] = nc.dram_tensor(name, shape, dt).ap()
    scr("sp_q", [D, S], BF16)
    scr("sp_k", [D, S], BF16)
    scr("sp_v", [S, D], BF16)
    scr("sp_ao", [D, S], BF16)
    scr("sp_s", [S, D], BF16)
    scr("sp_T", [S, IFF], F32)
    scr("sp_qtT", [IFF, S], BF16)
    scr("sp_x1", [S, D], F32)
    if _DEBUG_SPILL[0]:
        dbg_a = nc.dram_tensor("dbg_a", [128, 32768], BF16, kind="ExternalOutput").ap()
        dbg_c = nc.dram_tensor("dbg_c", [128, 32768], BF16, kind="ExternalOutput").ap()
        dbg_q = nc.dram_tensor("dbg_q", [D, S], BF16, kind="ExternalOutput").ap()
        dbg_ao = nc.dram_tensor("dbg_ao", [D, S], BF16, kind="ExternalOutput").ap()
        dbg_s = nc.dram_tensor("dbg_s", [S, D], BF16, kind="ExternalOutput").ap()

    with ExitStack() as ctx:
        def sb(n, s_, dt):
            return ctx.enter_context(nc.sbuf_tensor(n, s_, dt))
        bigA = sb("bigA", [128, 32768], BF16)
        bigC = sb("bigC", [128, 32768], BF16)
        wstage = sb("wstage", [128, 5632], F32)
        wtrans = sb("wtrans", [128, 16384], BF16)
        f8s = sb("f8s", [128, IFF], F8)
        evb = sb("evb", [128, 512], BF16)
        evb2 = sb("evb2", [128, 512], BF16)
        sg0 = sb("sg0", [128, 512], F32)
        sg1 = sb("sg1", [128, 512], F32)
        sx0 = sb("sx0", [128, 512], F32)
        ssb = sb("ssb", [128, 512], BF16)
        stats = sb("stats", [128, 64], F32)
        selm = sb("selm", [128, 128], F32)
        tmaxs = sb("tmaxs", [128, 16], F32)
        ident = sb("ident", [128, 128], BF16)
        maskc = sb("maskc", [128, 128], F32)
        maska = sb("maska", [128, 128], F32)
        lamb = sb("lamb", [128, NH], F32)
        onesc = sb("onesc", [1, 128], F32)
        lams = sb("lams", [1, NH], F32)
        idxs = sb("idxs", [128, NJT], I32)

        stgb2 = sb("stgb2", [128, 2048], BF16)
        stgb = wstage[:, 4096:5120].bitcast(BF16)
        stg5 = bigC[:, 0:5632]

        psA = ctx.enter_context(nc.psum_tensor("psA", [128, 2048], F32))
        psB = ctx.enter_context(nc.psum_tensor("psB", [128, 512], F32))
        psC = ctx.enter_context(nc.psum_tensor("psC", [128, 512], F32))
        ptA = ctx.enter_context(nc.psum_tensor("ptA", [128, 1024], BF16))
        ptB = ctx.enter_context(nc.psum_tensor("ptB", [128, 1024], BF16))

        sems = {}
        for name in ('LD', 'LG', 'PE', 'DV', 'AC', 'GP', 'PH', 'OT'):
            sems[name] = ctx.enter_context(nc.semaphore(name))
        block = ctx.enter_context(nc.Block())
        env = Env(nc, block, sems)
        B = env.block

        env.marks = {'pt_free': 0}

        def transpose_blocks(src_ap_fn, n128, dst_fn, wait_dv=None, wait_extra=None,
                             dst_batch=None, evac_eng='scalar'):
            for g0 in range(0, n128, 16):
                n = min(16, n128 - g0)
                @B.tensor
                def _(t, g0=g0, n=n):
                    if g0 == 0:
                        if wait_dv is not None:
                            t.wait_ge(env.s['DV'], wait_dv)
                        if wait_extra is not None:
                            wait_extra(t)
                    t.wait_ge(env.s['AC'], env.marks['pt_free'])
                    for j in range(n):
                        i = g0 + j
                        pt = ptA if j < 8 else ptB
                        env.inc(t.transpose(pt[:, (j % 8) * 128:(j % 8 + 1) * 128],
                                            src_ap_fn(i), ident[:]), 'PE', 1)
                pet = env.c['PE']
                eng_dec = getattr(B, evac_eng)
                @eng_dec
                def _(sc, g0=g0, n=n, pet=pet):
                    sc.wait_ge(env.s['PE'], pet)
                    if dst_batch is not None:
                        tgt, base_fn, step = dst_batch
                        W = tgt.shape[1]
                        for h0 in range(0, n, 8):
                            m = min(8, n - h0)
                            pt = ptA if h0 == 0 else ptB
                            b0 = base_fn(g0 + h0)
                            if step == 128:
                                env.inc(sc.copy(tgt[:, b0:b0 + m * 128],
                                                pt[:, 0:m * 128]), 'AC', 1)
                                continue
                            mm_ = m
                            if b0 + m * step > W:
                                mm_ = m - 1
                                env.inc(sc.copy(tgt[:, base_fn(g0 + h0 + mm_):base_fn(g0 + h0 + mm_) + 128],
                                                pt[:, mm_ * 128:(mm_ + 1) * 128]), 'AC', 1)
                            if mm_ == 1:
                                env.inc(sc.copy(tgt[:, b0:b0 + 128], pt[:, 0:128]), 'AC', 1)
                            elif mm_ > 0:
                                dst3 = tgt[:, b0:b0 + mm_ * step].rearrange(
                                    "p (a c) -> p a c", a=mm_)[:, :, 0:128]
                                src3 = pt[:, 0:mm_ * 128].rearrange("p (a c) -> p a c", a=mm_)
                                env.inc(sc.copy(dst3, src3), 'AC', 1)
                    else:
                        for j in range(n):
                            i = g0 + j
                            pt = ptA if j < 8 else ptB
                            env.inc(sc.copy(dst_fn(i), pt[:, (j % 8) * 128:(j % 8 + 1) * 128]), 'AC', 1)
                env.marks['pt_free'] = env.c['AC']
            return env.c['AC']

        @B.sync
        def _(sy):
            env.inc(sy.dma_start(out=idxs[:], in_=d["dnidx"].rearrange("(a b) c -> b (a c)", b=128),
                                 allow_slow_non_contiguous=True), 'LD', 16)
            env.inc(sy.dma_start(out=lams[:], in_=d["lam"]), 'LD', 16)
        ld_init = env.c['LD']

        @B.gpsimd
        def _(gp):
            gp.memset(ident[:], 1.0)
            gp.affine_select(ident[:], ident[:], [[-1, 128]], ALU.is_equal, 0.0,
                             base=0, channel_multiplier=1)
            gp.memset(maskc[:], 0.0)
            gp.affine_select(maskc[:], maskc[:], [[-1, 128]], ALU.is_ge, NEG,
                             base=0, channel_multiplier=1)
            gp.memset(maska[:], 0.0)
            env.inc(gp.affine_select(maska[:], maska[:], [[1, 128]], ALU.is_ge, NEG,
                                     base=-1, channel_multiplier=-1), 'GP', 1)

        @B.vector
        def _(v):
            v.memset(onesc[:], 1.0)
            env.inc(v.memset(stats[:, 5:6], 1e-5), 'DV', 1)


        @B.tensor
        def _(t):
            t.wait_ge(env.s['LD'], ld_init)
            t.wait_ge(env.s['DV'], env.c['DV'])
            env.inc(t.matmul(psB[:, 0:NH], onesc[:], lams[:], start=True, stop=True), 'PE', 1)

        @B.vector
        def _(v):
            v.wait_ge(env.s['PE'], env.c['PE'])
            env.inc(v.tensor_copy(lamb[:], psB[:, 0:NH]), 'DV', 1)

        env.barrier()

        x0_ap = d["x0"]
        wr2t = wtrans[:, 16128:16384]

        def ln_quant_phase(x_src, layer):
            for tt in range(TT):
                xw = wstage[:, 0:2048]
                xnf = wstage[:, 2048:4096]
                mu = stats[:, 0:1]; ss = stats[:, 1:2]; var = stats[:, 2:3]
                sd = stats[:, 3:4]; r0 = stats[:, 4:5]
                t1 = stats[:, 6:7]; t2 = stats[:, 7:8]
                rmax = stats[:, 8:9]; q1 = stats[:, 9:10]; q2 = stats[:, 10:11]

                @B.sync
                def _(sy, tt=tt):
                    if tt >= 1:
                        sy.wait_ge(env.s['DV'], env.c['DV'])
                        sy.wait_ge(env.s['AC'], env.c['AC'])
                    if layer == 1:
                        sy.wait_ge(env.s['LG'], env.c['LG'])
                    env.inc(sy.dma_start(out=xw, in_=x_src[tt * 128:(tt + 1) * 128, :]), 'LD', 16)
                ldx = env.c['LD']

                @B.vector
                def _(v):
                    v.wait_ge(env.s['LD'], ldx)
                    env.inc(v.reduce_sum(mu, xw, axis=AX.X), 'DV', 1)

                @B.scalar
                def _(sc):
                    sc.wait_ge(env.s['LD'], ldx)
                    env.inc(sc.activation(wtrans[:, 0:2048], xw, AF.Square, bias=0.0, scale=1.0,
                                          accum_out=ss), 'AC', 1)
                acs = env.c['AC']

                @B.vector
                def _(v):
                    v.wait_ge(env.s['AC'], acs)
                    v.tensor_scalar_mul(mu, mu, 1.0 / 2048.0)
                    v.drain()
                    v.tensor_scalar_mul(var, ss, 1.0 / 2048.0)
                    v.tensor_mul(t1, mu, mu)
                    v.drain()
                    env.inc(v.tensor_sub(var, var, t1), 'DV', 1)
                    v.drain()
                dvv = env.c['DV']

                @B.scalar
                def _(sc):
                    sc.wait_ge(env.s['DV'], dvv)
                    env.inc(sc.activation(sd, var, AF.Sqrt, bias=stats[:, 5:6], scale=1.0), 'AC', 1)
                acsd = env.c['AC']

                @B.vector
                def _(v):
                    v.wait_ge(env.s['AC'], acsd)
                    v.reciprocal(r0, sd)
                    v.drain()
                    v.tensor_scalar_add(t2, var, 1e-5)
                    v.drain()
                    for _i in range(2):
                        v.tensor_mul(t1, r0, r0)
                        v.drain()
                        v.tensor_mul(t1, t1, t2)
                        v.drain()
                        v.tensor_scalar(t1, t1, -0.5, 1.5, op0=ALU.mult, op1=ALU.add)
                        v.drain()
                        v.tensor_mul(r0, r0, t1)
                        v.drain()
                    v.tensor_scalar(xnf, xw, mu, r0, op0=ALU.subtract, op1=ALU.mult)
                    v.drain()
                    env.inc(v.tensor_copy(stgb[:], xnf), 'DV', 1)
                dvxn = env.c['DV']

                transpose_blocks(lambda i: stgb[:, i * 128:(i + 1) * 128], IC, None,
                                 wait_dv=dvxn,
                                 dst_batch=(bigA, lambda i0, tt=tt: i0 * 2048 + tt * 128, 2048))
                acev = env.c['AC']

                if layer == 0:
                    @B.vector
                    def _(v):
                        v.wait_ge(env.s['DV'], dvxn)
                        v.wait_ge(env.s['AC'], acev)
                        env.inc(v.reduce_max(rmax, xnf, axis=AX.X, apply_absolute_value=True), 'DV', 1)
                        v.drain()
                        v.reciprocal(q1, rmax)
                        v.drain()
                        v.tensor_scalar_mul(q1, q1, 224.0)
                        v.drain()
                        v.tensor_scalar_mul(q2, rmax, 1.0 / 224.0)
                        v.drain()
                        env.inc(v.tensor_scalar_mul(f8s[:, 0:2048], xnf, q1), 'DV', 1)
                        v.drain()
                        env.inc(v.tensor_scalar_mul(stgb[:], f8s[:, 0:2048], q2), 'DV', 1)
                    dvq = env.c['DV']
                    transpose_blocks(lambda i: stgb[:, i * 128:(i + 1) * 128], IC, None,
                                     wait_dv=dvq,
                                     dst_batch=(bigC, lambda i0, tt=tt: i0 * 2048 + tt * 128, 2048))
                else:
                    lob = stg5[:, 0:2048]
                    @B.vector
                    def _(v):
                        v.wait_ge(env.s['AC'], acev)
                        v.drain()
                        env.inc(v.tensor_tensor(lob, xnf, stgb[:], op=ALU.subtract), 'DV', 1)
                        v.drain()
                    dvlo = env.c['DV']
                    transpose_blocks(lambda i: lob[:, i * 128:(i + 1) * 128], IC, None,
                                     wait_dv=dvlo,
                                     dst_batch=(stg5, lambda i0: 2048 + i0 * 128, 128))
                    aclo = env.c['AC']

                    @B.tensor
                    def _(t, tt=tt):
                        t.wait_ge(env.s['AC'], aclo)
                        for i in range(IC):
                            t.matmul(psB[:, 0:16], bigA[:, i * 2048 + tt * 128: i * 2048 + tt * 128 + 128],
                                     wr2t[:, i * 16:(i + 1) * 16], start=(i == 0), stop=False)
                        for i in range(IC):
                            env.inc(t.matmul(psB[:, 0:16], stg5[:, 2048 + i * 128:2048 + (i + 1) * 128],
                                             wr2t[:, i * 16:(i + 1) * 16],
                                             start=False, stop=(i == IC - 1)), 'PE', 1)
                    perl = env.c['PE']

                    @B.vector
                    def _(v, tt=tt):
                        v.wait_ge(env.s['PE'], perl)
                        lg = stats[:, 16:24]
                        m8 = stats[:, 24:32]
                        v.tensor_copy(lg, psB[:, 0:8])
                        v.drain()
                        v.tensor_tensor(lg, lg, psB[:, 8:16], op=ALU.add)
                        v.drain()
                        v.max(m8, lg)
                        v.drain()
                        nt = stats[:, 32:33]
                        v.memset(nt, 1.0)
                        v.drain()
                        for e in range(8):
                            he = stats[:, 33:34]
                            v.tensor_tensor(he, lg[:, e:e + 1], m8[:, 0:1], op=ALU.is_equal)
                            v.drain()
                            v.tensor_mul(selm[:, tt * 8 + e: tt * 8 + e + 1], he, nt)
                            v.drain()
                            v.tensor_sub(nt, nt, selm[:, tt * 8 + e: tt * 8 + e + 1])
                            v.drain()
                        env.inc(v.engine_nop(), 'DV', 1)

        ln_quant_phase(x0_ap, 0)
        env.barrier()
        if _DEBUG_SPILL[0]:
            @B.gpsimd
            def _(gp):
                env.inc(gp.dma_start(out=dbg_a, in_=bigA[:]), 'LG', 16)
                env.inc(gp.dma_start(out=dbg_c, in_=bigC[:]), 'LG', 16)
                gp.wait_ge(env.s['LG'], env.c['LG'])
            env.barrier()

        def l0_proj_fm(w_ap, sp_dst):
            for oc in range(16):
                buf = oc % 2
                ws = wstage[:, buf * 2048:(buf + 1) * 2048]
                st0 = 8192 * buf
                @B.sync
                def _(sy, oc=oc):
                    sy.wait_ge(env.s['DV'], env.c['DV'])
                    env.inc(sy.dma_start(out=ws, in_=w_ap[oc * 128:(oc + 1) * 128, :]), 'LD', 16)
                ldw = env.c['LD']
                @B.vector
                def _(v, oc=oc):
                    v.wait_ge(env.s['LD'], ldw)
                    v.wait_ge(env.s['PE'], env.c['PE'])
                    env.inc(v.tensor_copy(stgb[:], ws), 'DV', 1)
                dvc = env.c['DV']
                def wx(t):
                    pass
                transpose_blocks(lambda i: stgb[:, i * 128:(i + 1) * 128], IC, None,
                                 wait_dv=dvc,
                                 dst_batch=(wtrans, lambda i0, st0=st0: st0 + i0 * 128, 128))
                acst = env.c['AC']
                for tc in range(4):
                    ps = psB if tc % 2 == 0 else psC
                    @B.tensor
                    def _(t, tc=tc, oc=oc, st0=st0):
                        t.wait_ge(env.s['AC'], acst)
                        if tc >= 2 or oc > 0:
                            t.wait_ge(env.s['AC'], env.c['AC'])
                        for i in range(IC):
                            env.inc(t.matmul(ps[:], wtrans[:, st0 + i * 128: st0 + (i + 1) * 128],
                                             bigA[:, i * 2048 + tc * 512: i * 2048 + (tc + 1) * 512],
                                             start=(i == 0), stop=(i == IC - 1)), 'PE', 1)
                    pemm = env.c['PE']
                    eb = evb if tc % 2 == 0 else evb2
                    @B.scalar
                    def _(sc):
                        sc.wait_ge(env.s['PE'], pemm)
                        sc.wait_ge(env.s['LG'], env.c['LG'])
                        env.inc(sc.copy(eb[:, 0:512], ps[:]), 'AC', 1)
                    acev2 = env.c['AC']
                    @B.gpsimd
                    def _(gp, oc=oc, tc=tc, eb=eb):
                        gp.wait_ge(env.s['AC'], acev2)
                        env.inc(gp.dma_start(out=sp_dst[oc * 128:(oc + 1) * 128, tc * 512:(tc + 1) * 512],
                                             in_=eb[:, 0:512]), 'LG', 16)

        l0_proj_fm(d["l0wq"], d["sp_q"])
        l0_proj_fm(d["l0wk"], d["sp_k"])

        for oc4 in range(4):
            for rt in range(4):
                ws = wstage[:, (rt % 2) * 2048:(rt % 2 + 1) * 2048]
                @B.sync
                def _(sy):
                    sy.wait_ge(env.s['DV'], env.c['DV'])
                    env.inc(sy.dma_start(out=ws, in_=d["l0wv"][(oc4 * 4 + rt) * 128:(oc4 * 4 + rt + 1) * 128, :]), 'LD', 16)
                ldw = env.c['LD']
                @B.vector
                def _(v):
                    v.wait_ge(env.s['LD'], ldw)
                    v.wait_ge(env.s['PE'], env.c['PE'])
                    env.inc(v.tensor_copy(stgb[:], ws), 'DV', 1)
                dvc = env.c['DV']
                transpose_blocks(lambda i: stgb[:, i * 128:(i + 1) * 128], IC, None,
                                 wait_dv=dvc,
                                 dst_batch=(wtrans, lambda i0, rt=rt: i0 * 512 + rt * 128, 512))
            acst = env.c['AC']
            for tt in range(TT):
                ps = psB if tt % 2 == 0 else psC
                @B.tensor
                def _(t, tt=tt, oc4=oc4):
                    t.wait_ge(env.s['AC'], acst)
                    if tt >= 2 or oc4 > 0:
                        t.wait_ge(env.s['AC'], env.c['AC'])
                    for i in range(IC):
                        env.inc(t.matmul(ps[:], bigA[:, i * 2048 + tt * 128: i * 2048 + tt * 128 + 128],
                                         wtrans[:, i * 512:(i + 1) * 512],
                                         start=(i == 0), stop=(i == IC - 1)), 'PE', 1)
                pemm = env.c['PE']
                eb = evb if tt % 2 == 0 else evb2
                @B.scalar
                def _(sc):
                    sc.wait_ge(env.s['PE'], pemm)
                    sc.wait_ge(env.s['LG'], env.c['LG'])
                    env.inc(sc.copy(eb[:, 0:512], ps[:]), 'AC', 1)
                acev2 = env.c['AC']
                @B.gpsimd
                def _(gp, tt=tt, oc4=oc4, eb=eb):
                    gp.wait_ge(env.s['AC'], acev2)
                    env.inc(gp.dma_start(out=d["sp_v"][tt * 128:(tt + 1) * 128, oc4 * 512:(oc4 + 1) * 512],
                                         in_=eb[:, 0:512]), 'LG', 16)

        env.barrier()

        ptl = bigA[:, 6144:8192]
        pTs = bigA[:, 8192:16384]
        qkv_base = [0, 16384]
        qkv_pe_mark = [0, 0]
        qkv_ld = {}
        pv_ac = [0, 0]
        eb_lg = [0, 0]
        last_dvm = [0]
        last_acex = [0]

        def l0_qkv_load(h):
            par = h % 2
            base = qkv_base[par]
            mark = qkv_pe_mark[par]
            @B.sync
            def _(sy, h=h, base=base, mark=mark):
                sy.wait_ge(env.s['PE'], mark)
                env.inc(sy.dma_start(out=bigA[:, base:base + 2048], in_=d["sp_q"][h * 128:(h + 1) * 128, :]), 'LD', 16)
                env.inc(sy.dma_start(out=bigA[:, base + 2048:base + 4096], in_=d["sp_k"][h * 128:(h + 1) * 128, :]), 'LD', 16)
                env.inc(sy.dma_start(
                    out=bigA[:, base + 4096:base + 6144].rearrange("p (a c) -> p a c", a=TT),
                    in_=d["sp_v"].rearrange("(a p) o -> p a o", p=128)[:, :, h * 128:(h + 1) * 128]), 'LD', 16)
            qkv_ld[h] = env.c['LD']

        def l0_scores(h, qt):
            base = qkv_base[h % 2]
            dvm0, acex0 = last_dvm[0], last_acex[0]
            @B.tensor
            def _(t, h=h, qt=qt, base=base, dvm0=dvm0, acex0=acex0):
                t.wait_ge(env.s['LD'], qkv_ld[h])
                if dvm0:
                    t.wait_ge(env.s['DV'], dvm0)
                if acex0:
                    t.wait_ge(env.s['AC'], acex0)
                for kc in range(4):
                    env.inc(t.matmul(psA[:, kc * 512:(kc + 1) * 512],
                                     bigA[:, base + qt * 128: base + (qt + 1) * 128],
                                     bigA[:, base + 2048 + kc * 512: base + 2048 + (kc + 1) * 512],
                                     start=True, stop=True), 'PE', 1)
            return env.c['PE']

        l0_qkv_load(0)
        l0_scores(0, 0)
        pe_sc = {(0, 0): env.c['PE']}
        for h in range(NH):
            if h + 1 < NH:
                l0_qkv_load(h + 1)
            base = qkv_base[h % 2]
            for qg in range(4):
                for qi in range(4):
                    qt = qg * 4 + qi
                    pesc = pe_sc.pop((h, qt))
                    m0 = stats[:, 40:41]; nm = stats[:, 41:42]
                    zz = stats[:, 42:43]; zi = stats[:, 43:44]
                    @B.vector
                    def _(v, pesc=pesc):
                        v.wait_ge(env.s['PE'], pesc)
                        v.reduce_max(m0, psA[:], axis=AX.X)
                        v.drain()
                        env.inc(v.tensor_scalar_mul(nm, m0, -1.0 / math.sqrt(128.0)), 'DV', 1)
                        v.drain()
                    dvm = env.c['DV']
                    last_dvm[0] = dvm

                    @B.scalar
                    def _(sc, dvm=dvm):
                        sc.wait_ge(env.s['DV'], dvm)
                        env.inc(sc.activation(ptl[:], psA[:], AF.Exp,
                                              bias=nm, scale=1.0 / math.sqrt(128.0),
                                              accum_out=zz), 'AC', 1)
                    acex = env.c['AC']
                    last_acex[0] = acex

                    if qt + 1 < 16:
                        pe_sc[(h, qt + 1)] = l0_scores(h, qt + 1)
                    elif h + 1 < NH:
                        pe_sc[(h + 1, 0)] = l0_scores(h + 1, 0)

                    @B.vector
                    def _(v, acex=acex):
                        v.wait_ge(env.s['AC'], acex)
                        v.reciprocal(zi, zz)
                        v.drain()
                        env.inc(v.tensor_scalar_mul(ptl[:], ptl[:], zi), 'DV', 1)
                    dvp = env.c['DV']

                    transpose_blocks(lambda i: ptl[:, i * 128:(i + 1) * 128], 16, None,
                                     wait_dv=dvp,
                                     dst_batch=(bigA, lambda i0, qi=qi: 8192 + i0 * 512 + qi * 128, 512))
                acpt = env.c['AC']

                ps = psB if qg % 2 == 0 else psC
                bp = pv_ac[qg % 2]
                @B.tensor
                def _(t, qg=qg, h=h, base=base, bp=bp, ps=ps):
                    t.wait_ge(env.s['AC'], acpt)
                    if bp:
                        t.wait_ge(env.s['AC'], bp)
                    for kt in range(16):
                        env.inc(t.matmul(ps[:], bigA[:, base + 4096 + kt * 128: base + 4096 + (kt + 1) * 128],
                                         pTs[:, kt * 512:(kt + 1) * 512],
                                         start=(kt == 0), stop=(kt == 15)), 'PE', 1)
                pepv = env.c['PE']

                eb = evb if qg % 2 == 0 else evb2
                eblg = eb_lg[qg % 2]
                @B.scalar
                def _(sc, pepv=pepv, eblg=eblg, eb=eb, ps=ps):
                    sc.wait_ge(env.s['PE'], pepv)
                    if eblg:
                        sc.wait_ge(env.s['LG'], eblg)
                    env.inc(sc.copy(eb[:, 0:512], ps[:]), 'AC', 1)
                acao = env.c['AC']
                pv_ac[qg % 2] = acao
                @B.gpsimd
                def _(gp, h=h, qg=qg, eb=eb, acao=acao):
                    gp.wait_ge(env.s['AC'], acao)
                    env.inc(gp.dma_start(out=d["sp_ao"][h * 128:(h + 1) * 128, qg * 512:(qg + 1) * 512],
                                         in_=eb[:, 0:512]), 'LG', 16)
                eb_lg[qg % 2] = env.c['LG']
            qkv_pe_mark[h % 2] = env.c['PE']

        env.barrier()

        for ob in range(16):
            buf = ob % 2
            ws = wstage[:, buf * 2048:(buf + 1) * 2048]
            @B.sync
            def _(sy, ob=ob):
                sy.wait_ge(env.s['DV'], env.c['DV'])
                env.inc(sy.dma_start(out=ws, in_=d["l0wo"][ob * 128:(ob + 1) * 128, :]), 'LD', 16)
            ldw = env.c['LD']
            @B.vector
            def _(v, ob=ob):
                v.wait_ge(env.s['LD'], ldw)
                v.wait_ge(env.s['PE'], env.c['PE'])
                env.inc(v.tensor_copy(stgb[:], ws), 'DV', 1)
            dvc = env.c['DV']
            transpose_blocks(lambda i: stgb[:, i * 128:(i + 1) * 128], IC, None,
                             wait_dv=dvc,
                             dst_batch=(bigA, lambda i0, ob=ob: i0 * 2048 + ob * 128, 2048))
        acwo = env.c['AC']

        for tt in range(TT):
            aosl = stgb
            @B.sync
            def _(sy, tt=tt):
                sy.wait_ge(env.s['PE'], env.c['PE'])
                sy.wait_ge(env.s['LG'], env.c['LG'])
                env.inc(sy.dma_start(
                    out=aosl.rearrange("p (a c) -> p a c", a=16),
                    in_=d["sp_ao"].rearrange("(a p) t -> p a t", p=128)[:, :, tt * 128:(tt + 1) * 128]), 'LD', 16)
            ldao = env.c['LD']
            for oc4 in range(4):
                ps = psB if oc4 % 2 == 0 else psC
                @B.tensor
                def _(t, tt=tt, oc4=oc4):
                    t.wait_ge(env.s['LD'], ldao)
                    t.wait_ge(env.s['AC'], acwo)
                    if oc4 >= 2 or tt > 0:
                        t.wait_ge(env.s['AC'], env.c['AC'])
                    for hh in range(16):
                        env.inc(t.matmul(ps[:], aosl[:, hh * 128:(hh + 1) * 128],
                                         bigA[:, hh * 2048 + oc4 * 512: hh * 2048 + (oc4 + 1) * 512],
                                         start=(hh == 0), stop=(hh == 15)), 'PE', 1)
                pemm = env.c['PE']
                eb = evb if oc4 % 2 == 0 else evb2
                @B.scalar
                def _(sc):
                    sc.wait_ge(env.s['PE'], pemm)
                    sc.wait_ge(env.s['LG'], env.c['LG'])
                    env.inc(sc.copy(eb[:, 0:512], ps[:]), 'AC', 1)
                acs2 = env.c['AC']
                @B.gpsimd
                def _(gp, tt=tt, oc4=oc4, eb=eb):
                    gp.wait_ge(env.s['AC'], acs2)
                    env.inc(gp.dma_start(out=d["sp_s"][tt * 128:(tt + 1) * 128, oc4 * 512:(oc4 + 1) * 512],
                                         in_=eb[:, 0:512]), 'LG', 16)

        env.barrier()
        if _DEBUG_SPILL[0]:
            @B.gpsimd
            def _(gp):
                env.inc(gp.dma_start(out=dbg_q, in_=d["sp_q"]), 'LG', 16)
                env.inc(gp.dma_start(out=dbg_ao, in_=d["sp_ao"]), 'LG', 16)
                env.inc(gp.dma_start(out=dbg_s, in_=d["sp_s"]), 'LG', 16)
                gp.wait_ge(env.s['LG'], env.c['LG'])
            env.barrier()

        @B.vector
        def _(v):
            env.inc(v.memset(tmaxs[:], 0.0), 'DV', 1)

        ffn_ws_dv = [0, 0]
        ffn_stg_pe = [0, 0]
        ffn_strip_ac = {}
        ffn_psum_dv = [0, 0]

        def ffn_prep_piece(jc, piece):
            gi, rt = divmod(piece, 4)
            w_ap = d["wg"] if gi == 0 else d["wu"]
            buf = jc % 2
            strip0 = (gi * 2 + buf) * 8192
            par = piece % 2
            ws = wstage[:, par * 2048:(par + 1) * 2048]
            sg = stgb if par == 0 else stgb2
            f8h = f8s[:, 0:2048] if par == 0 else f8s[:, 2048:4096]
            wsdv = ffn_ws_dv[par]
            @B.gpsimd
            def _(gp, jc=jc, rt=rt, w_ap=w_ap, wsdv=wsdv):
                gp.wait_ge(env.s['DV'], wsdv)
                env.inc(gp.indirect_dma_start(
                    ws, None, w_ap,
                    IndirectOffsetOnAxis(idxs[:, jc * 4 + rt: jc * 4 + rt + 1], 0)), 'LG', 16)
            ldw = env.c['LG']
            wmx = stats[:, 44 + par:45 + par]; s1 = stats[:, 46 + par:47 + par]
            stpe = ffn_stg_pe[par]
            @B.vector
            def _(v, ldw=ldw, stpe=stpe, sg=sg, f8h=f8h, wmx=wmx, s1=s1):
                v.wait_ge(env.s['LG'], ldw)
                v.wait_ge(env.s['PE'], stpe)
                v.reduce_max(wmx, ws, axis=AX.X, apply_absolute_value=True)
                v.drain()
                v.reciprocal(s1, wmx)
                v.drain()
                v.tensor_scalar_mul(s1, s1, 224.0)
                v.drain()
                v.tensor_scalar_mul(f8h, ws, s1)
                v.drain()
                v.tensor_scalar_mul(s1, wmx, 1.0 / 224.0)
                v.drain()
                env.inc(v.tensor_scalar_mul(sg[:], f8h, s1), 'DV', 1)
            ffn_ws_dv[par] = env.c['DV']
            dvq = env.c['DV']
            transpose_blocks(lambda i, sg=sg: sg[:, i * 128:(i + 1) * 128], IC, None,
                             wait_dv=dvq,
                             dst_batch=(bigA, lambda i0, strip0=strip0, rt=rt: strip0 + i0 * 512 + rt * 128, 512))
            ffn_stg_pe[par] = env.c['PE']
            if piece == 7:
                ffn_strip_ac[jc] = env.c['AC']

        for piece in range(8):
            ffn_prep_piece(0, piece)
        for jc in range(11):
            buf = jc % 2
            acst = ffn_strip_ac[jc]

            for tt in range(TT):
                if tt % 2 == 0 and jc + 1 < 11:
                    ffn_prep_piece(jc + 1, tt // 2)
                pg = psA[:, (tt % 2) * 1024:(tt % 2) * 1024 + 512]
                pu = psA[:, (tt % 2) * 1024 + 512:(tt % 2) * 1024 + 1024]
                bp = ffn_psum_dv[tt % 2]
                @B.tensor
                def _(t, tt=tt, jc=jc, buf=buf, bp=bp):
                    t.wait_ge(env.s['AC'], acst)
                    if bp:
                        t.wait_ge(env.s['DV'], bp)
                    for i in range(IC):
                        t.matmul(pg, bigC[:, i * 2048 + tt * 128: i * 2048 + tt * 128 + 128],
                                 bigA[:, (0 * 2 + buf) * 8192 + i * 512:(0 * 2 + buf) * 8192 + (i + 1) * 512],
                                 start=(i == 0), stop=(i == IC - 1))
                        env.inc(t.matmul(pu, bigC[:, i * 2048 + tt * 128: i * 2048 + tt * 128 + 128],
                                         bigA[:, (1 * 2 + buf) * 8192 + i * 512:(1 * 2 + buf) * 8192 + (i + 1) * 512],
                                         start=(i == 0), stop=(i == IC - 1)), 'PE', 1)
                pemm = env.c['PE']
                gsb = sg0 if tt % 2 == 0 else sg1
                @B.scalar
                def _(sc, pg=pg):
                    sc.wait_ge(env.s['PE'], pemm)
                    env.inc(sc.activation(gsb[:], pg, AF.Silu, bias=0.0, scale=1.0), 'AC', 1)
                acsl = env.c['AC']
                tch = stats[:, 48:49]
                @B.vector
                def _(v, tt=tt, jc=jc, pu=pu, gsb=gsb):
                    v.wait_ge(env.s['AC'], acsl)
                    if tt >= 2 or jc > 0:
                        v.wait_ge(env.s['LG'], env.c['LG'])
                    v.tensor_tensor(gsb[:], gsb[:], pu, op=ALU.mult)
                    v.drain()
                    v.reduce_max(tch, gsb[:], axis=AX.X, apply_absolute_value=True)
                    v.drain()
                    env.inc(v.tensor_tensor(tmaxs[:, tt:tt + 1], tmaxs[:, tt:tt + 1], tch, op=ALU.max), 'DV', 1)
                    v.drain()
                dvt = env.c['DV']
                ffn_psum_dv[tt % 2] = dvt
                @B.gpsimd
                def _(gp, tt=tt, jc=jc, gsb=gsb):
                    gp.wait_ge(env.s['DV'], dvt)
                    env.inc(gp.dma_start(out=d["sp_T"][tt * 128:(tt + 1) * 128, jc * 512:(jc + 1) * 512],
                                         in_=gsb[:]), 'LG', 16)

        env.barrier()

        for tt in range(TT):
            Tw = wstage[:, 0:5632]
            @B.sync
            def _(sy, tt=tt):
                if tt >= 1:
                    sy.wait_ge(env.s['DV'], env.c['DV'])
                sy.wait_ge(env.s['LG'], env.c['LG'])
                env.inc(sy.dma_start(out=Tw, in_=d["sp_T"][tt * 128:(tt + 1) * 128, :]), 'LD', 16)
            ldt = env.c['LD']
            q1 = stats[:, 49:50]; q2 = stats[:, 50:51]
            @B.vector
            def _(v, tt=tt):
                v.wait_ge(env.s['LD'], ldt)
                if tt >= 1:
                    v.wait_ge(env.s['LG'], env.c['LG'])
                v.reciprocal(q1, tmaxs[:, tt:tt + 1])
                v.drain()
                v.tensor_scalar_mul(q1, q1, 224.0)
                v.drain()
                v.tensor_scalar_mul(q2, tmaxs[:, tt:tt + 1], 1.0 / 224.0)
                v.drain()
                v.tensor_scalar_mul(f8s[:], Tw, q1)
                v.drain()
                env.inc(v.tensor_scalar_mul(stg5[:], f8s[:], q2), 'DV', 1)
            dvq = env.c['DV']
            transpose_blocks(lambda i: stg5[:, i * 128:(i + 1) * 128], NJT, None,
                             wait_dv=dvq,
                             dst_batch=(bigA, lambda i0: i0 * 128, 128))
            acq = env.c['AC']
            @B.gpsimd
            def _(gp, tt=tt):
                gp.wait_ge(env.s['AC'], acq)
                env.inc(gp.dma_start(
                    out=d["sp_qtT"].rearrange("(a p) t -> p a t", p=128)[:, :, tt * 128:(tt + 1) * 128],
                    in_=bigA[:, 0:5632].rearrange("p (a c) -> p a c", a=NJT)), 'LG', 16)

        env.barrier()

        for hc in range(4):
            for rt in range(4):
                ws = wstage[:, 0:5632]
                @B.sync
                def _(sy, hc=hc, rt=rt):
                    sy.wait_ge(env.s['DV'], env.c['DV'])
                    env.inc(sy.dma_start(out=ws, in_=d["wd"][(hc * 4 + rt) * 128:(hc * 4 + rt + 1) * 128, :]), 'LD', 16)
                ldw = env.c['LD']
                wmx = stats[:, 51:52]; s1 = stats[:, 52:53]; s2 = stats[:, 53:54]
                @B.vector
                def _(v):
                    v.wait_ge(env.s['LD'], ldw)
                    v.wait_ge(env.s['PE'], env.c['PE'])
                    v.reduce_max(wmx, ws, axis=AX.X, apply_absolute_value=True)
                    v.drain()
                    v.reciprocal(s1, wmx)
                    v.drain()
                    v.tensor_scalar_mul(s1, s1, 224.0)
                    v.drain()
                    v.tensor_scalar_mul(s2, wmx, 1.0 / 224.0)
                    v.drain()
                    v.tensor_scalar_mul(f8s[:], ws, s1)
                    v.drain()
                    env.inc(v.tensor_scalar_mul(stg5[:], f8s[:], s2), 'DV', 1)
                dvq = env.c['DV']
                transpose_blocks(lambda i: stg5[:, i * 128:(i + 1) * 128], NJT, None,
                                 wait_dv=dvq,
                                 dst_batch=(bigA, lambda i0, rt=rt: i0 * 512 + rt * 128, 512))
            acst = env.c['AC']
            for tt in range(TT):
                @B.sync
                def _(sy, hc=hc, tt=tt):
                    sy.wait_ge(env.s['PE'], env.c['PE'])
                    sy.wait_ge(env.s['LG'], env.c['LG'])
                    env.inc(sy.dma_start(
                        out=stg5[:].rearrange("p (a c) -> p a c", a=NJT),
                        in_=d["sp_qtT"].rearrange("(a p) t -> p a t", p=128)[:, :, tt * 128:(tt + 1) * 128]), 'LD', 16)
                    env.inc(sy.dma_start(out=sx0[:], in_=x0_ap[tt * 128:(tt + 1) * 128, hc * 512:(hc + 1) * 512]), 'LD', 16)
                    env.inc(sy.dma_start(out=ssb[:], in_=d["sp_s"][tt * 128:(tt + 1) * 128, hc * 512:(hc + 1) * 512]), 'LD', 16)
                ldq = env.c['LD']
                ps = psB if tt % 2 == 0 else psC
                @B.tensor
                def _(t, hc=hc, tt=tt):
                    t.wait_ge(env.s['LD'], ldq)
                    t.wait_ge(env.s['AC'], acst)
                    if tt >= 2 or hc > 0:
                        t.wait_ge(env.s['DV'], env.c['DV'])
                    for i in range(NJT):
                        env.inc(t.matmul(ps[:], stg5[:, i * 128:(i + 1) * 128],
                                         bigA[:, i * 512:(i + 1) * 512],
                                         start=(i == 0), stop=(i == NJT - 1)), 'PE', 1)
                pemm = env.c['PE']
                gsb = sg0 if tt % 2 == 0 else sg1
                @B.scalar
                def _(sc, ps=ps, gsb=gsb):
                    sc.wait_ge(env.s['PE'], pemm)
                    env.inc(sc.activation(gsb[:], ps[:], AF.Silu, bias=0.0, scale=1.0), 'AC', 1)
                acg = env.c['AC']
                @B.vector
                def _(v, hc=hc, tt=tt, gsb=gsb):
                    v.wait_ge(env.s['AC'], acg)
                    v.wait_ge(env.s['LD'], ldq)
                    if tt >= 2 or hc > 0:
                        v.wait_ge(env.s['LG'], env.c['LG'])
                    v.tensor_tensor(gsb[:], gsb[:], ssb[:], op=ALU.mult)
                    v.drain()
                    env.inc(v.tensor_tensor(sx0[:], sx0[:], gsb[:], op=ALU.add), 'DV', 1)
                    v.drain()
                dvx = env.c['DV']
                @B.gpsimd
                def _(gp, hc=hc, tt=tt):
                    gp.wait_ge(env.s['DV'], dvx)
                    env.inc(gp.dma_start(out=d["sp_x1"][tt * 128:(tt + 1) * 128, hc * 512:(hc + 1) * 512],
                                         in_=sx0[:]), 'LG', 16)

        env.barrier()

        @B.sync
        def _(sy):
            env.inc(sy.dma_start(out=wstage[0:8, 0:2048], in_=d["wr"]), 'LD', 16)
        ldwr = env.c['LD']
        @B.vector
        def _(v):
            v.wait_ge(env.s['LD'], ldwr)
            v.tensor_copy(stgb[0:8, 0:2048], wstage[0:8, 0:2048])
            v.drain()
            env.inc(v.tensor_tensor(stg5[0:8, 0:2048], wstage[0:8, 0:2048], stgb[0:8, 0:2048],
                                    op=ALU.subtract), 'DV', 1)
            v.drain()
        dvwr = env.c['DV']
        @B.tensor
        def _(t):
            t.wait_ge(env.s['DV'], dvwr)
            t.wait_ge(env.s['AC'], env.marks['pt_free'])
            for i in range(IC):
                pt = ptA if i < 8 else ptB
                t.transpose(pt[:, (i % 8) * 128:(i % 8) * 128 + 8],
                            stgb[0:8, i * 128:(i + 1) * 128], ident[0:8, 0:8])
                env.inc(t.transpose(pt[:, (i % 8) * 128 + 8:(i % 8) * 128 + 16],
                                    stg5[0:8, i * 128:(i + 1) * 128], ident[0:8, 0:8]), 'PE', 1)
        pewr = env.c['PE']
        @B.scalar
        def _(sc):
            sc.wait_ge(env.s['PE'], pewr)
            for i in range(IC):
                pt = ptA if i < 8 else ptB
                env.inc(sc.copy(wr2t[:, i * 16:(i + 1) * 16],
                                pt[:, (i % 8) * 128:(i % 8) * 128 + 16]), 'AC', 1)
        env.marks['pt_free'] = env.c['AC']

        env.barrier()
        ln_quant_phase(d["sp_x1"], 1)
        env.barrier()

        cosfull = bigC[:, 16384:20480].bitcast(F32)
        sinfull = bigC[:, 20480:24576].bitcast(F32)
        @B.sync
        def _(sy):
            env.inc(sy.dma_start(out=cosfull.rearrange("p (a c) -> p a c", a=TT),
                                 in_=d["cosT"].rearrange("(a p) c -> p a c", p=128)), 'LD', 16)
            env.inc(sy.dma_start(out=sinfull.rearrange("p (a c) -> p a c", a=TT),
                                 in_=d["sinF"].rearrange("(a p) c -> p a c", p=128)), 'LD', 16)
        ldcs = env.c['LD']

        def rope_apply(v, ps, eb, t1, tt):
            cos4 = cosfull[:, tt * 128:(tt + 1) * 128].rearrange("p (o c) -> p o c", o=1).broadcast_to([128, 4, 128])
            v.tensor_tensor(t1[:].rearrange("p (h c) -> p h c", h=4), ps[:].rearrange("p (h c) -> p h c", h=4),
                            cos4, op=ALU.mult)
            pr = ps[:].rearrange("p (h t c) -> p h t c", h=4, t=2, c=64)
            ebr = eb[:, 0:512].rearrange("p (h t c) -> p h t c", h=4, t=2, c=64)
            sin4 = sinfull[:, tt * 128:(tt + 1) * 128].rearrange("p (o t c) -> p o t c", o=1, t=2)\
                .broadcast_to([128, 4, 2, 64])
            v.tensor_tensor(ebr[:, :, 0, :], pr[:, :, 1, :], sin4[:, :, 0, :], op=ALU.mult)
            v.tensor_tensor(ebr[:, :, 1, :], pr[:, :, 0, :], sin4[:, :, 1, :], op=ALU.mult)
            v.drain()
            inst = v.tensor_tensor(eb[:, 0:512], t1[:], eb[:, 0:512], op=ALU.add)
            v.drain()
            return inst

        for oc4 in range(4):
            buf = oc4 % 2
            for i in range(IC):
                ws = wstage[:, (i % 2) * 2048:(i % 2) * 2048 + 512]
                @B.sync
                def _(sy, i=i, oc4=oc4):
                    sy.wait_ge(env.s['DV'], env.c['DV'])
                    env.inc(sy.dma_start(out=ws, in_=d["l1wq"][i * 128:(i + 1) * 128, oc4 * 512:(oc4 + 1) * 512]), 'LD', 16)
                ldw = env.c['LD']
                @B.vector
                def _(v, i=i, buf=buf, oc4=oc4):
                    v.wait_ge(env.s['LD'], ldw)
                    if oc4 >= 2:
                        v.wait_ge(env.s['PE'], env.c['PE'])
                    env.inc(v.tensor_copy(wtrans[:, buf * 8192 + i * 512: buf * 8192 + (i + 1) * 512], ws), 'DV', 1)
            dvst = env.c['DV']
            for tt in range(TT):
                ps = psB if tt % 2 == 0 else psC
                @B.tensor
                def _(t, tt=tt, oc4=oc4, buf=buf):
                    t.wait_ge(env.s['DV'], dvst)
                    if tt >= 2 or oc4 > 0:
                        t.wait_ge(env.s['DV'], env.c['DV'])
                    for i in range(IC):
                        env.inc(t.matmul(ps[:], bigA[:, i * 2048 + tt * 128: i * 2048 + tt * 128 + 128],
                                         wtrans[:, buf * 8192 + i * 512: buf * 8192 + (i + 1) * 512],
                                         start=(i == 0), stop=(i == IC - 1)), 'PE', 1)
                pemm = env.c['PE']
                eb = evb if tt % 2 == 0 else evb2
                t1 = sg0 if tt % 2 == 0 else sg1
                @B.vector
                def _(v, tt=tt, ps=ps, eb=eb, t1=t1):
                    v.wait_ge(env.s['PE'], pemm)
                    v.wait_ge(env.s['LD'], ldcs)
                    v.wait_ge(env.s['LG'], env.c['LG'])
                    env.inc(rope_apply(v, ps, eb, t1, tt), 'DV', 1)
                dvr = env.c['DV']
                @B.gpsimd
                def _(gp, tt=tt, oc4=oc4, eb=eb):
                    gp.wait_ge(env.s['DV'], dvr)
                    env.inc(gp.dma_start(out=d["sp_q"][tt * 128:(tt + 1) * 128, oc4 * 512:(oc4 + 1) * 512],
                                         in_=eb[:, 0:512]), 'LG', 16)

        for which, w_ap in (("k", d["l1wk"]), ("v", d["l1wv"])):
            for i in range(IC):
                ws = wstage[:, (i % 2) * 2048:(i % 2) * 2048 + 512]
                @B.sync
                def _(sy, i=i, w_ap=w_ap):
                    sy.wait_ge(env.s['DV'], env.c['DV'])
                    env.inc(sy.dma_start(out=ws, in_=w_ap[i * 128:(i + 1) * 128, :]), 'LD', 16)
                ldw = env.c['LD']
                @B.vector
                def _(v, i=i):
                    v.wait_ge(env.s['LD'], ldw)
                    v.wait_ge(env.s['PE'], env.c['PE'])
                    env.inc(v.tensor_copy(wtrans[:, i * 512:(i + 1) * 512], ws), 'DV', 1)
            dvst = env.c['DV']
            for tt in range(TT):
                ps = psB if tt % 2 == 0 else psC
                @B.tensor
                def _(t, tt=tt):
                    t.wait_ge(env.s['DV'], dvst)
                    if tt >= 2:
                        t.wait_ge(env.s['DV'], env.c['DV'])
                        t.wait_ge(env.s['AC'], env.c['AC'])
                    for i in range(IC):
                        env.inc(t.matmul(ps[:], bigA[:, i * 2048 + tt * 128: i * 2048 + tt * 128 + 128],
                                         wtrans[:, i * 512:(i + 1) * 512],
                                         start=(i == 0), stop=(i == IC - 1)), 'PE', 1)
                pemm = env.c['PE']
                if which == "v":
                    @B.scalar
                    def _(sc, tt=tt, ps=ps):
                        sc.wait_ge(env.s['PE'], pemm)
                        env.inc(sc.copy(bigC[:, 8192 + tt * 512: 8192 + (tt + 1) * 512], ps[:]), 'AC', 1)
                else:
                    eb = evb if tt % 2 == 0 else evb2
                    t1 = sg0 if tt % 2 == 0 else sg1
                    @B.vector
                    def _(v, tt=tt, ps=ps, eb=eb, t1=t1):
                        v.wait_ge(env.s['PE'], pemm)
                        v.wait_ge(env.s['LD'], ldcs)
                        v.wait_ge(env.s['AC'], env.c['AC'])
                        env.inc(rope_apply(v, ps, eb, t1, tt), 'DV', 1)
                    dvr = env.c['DV']
                    @B.tensor
                    def _(t, tt=tt, eb=eb):
                        t.wait_ge(env.s['DV'], dvr)
                        t.wait_ge(env.s['AC'], env.marks['pt_free'])
                        for kv in range(4):
                            pt = ptA if kv < 2 else ptB
                            env.inc(t.transpose(pt[:, (kv % 2) * 128:(kv % 2 + 1) * 128],
                                                eb[:, kv * 128:(kv + 1) * 128], ident[:]), 'PE', 1)
                    pek = env.c['PE']
                    @B.scalar
                    def _(sc, tt=tt):
                        sc.wait_ge(env.s['PE'], pek)
                        for kv in range(4):
                            pt = ptA if kv < 2 else ptB
                            env.inc(sc.copy(bigC[:, kv * 2048 + tt * 128: kv * 2048 + tt * 128 + 128],
                                            pt[:, (kv % 2) * 128:(kv % 2 + 1) * 128]), 'AC', 1)
                    env.marks['pt_free'] = env.c['AC']

        env.barrier()

        ptl1 = bigC[:, 28672:29440]
        ptl2 = bigC[:, 29440:30208]
        rT = bigC[:, 30208:30976]
        qfm_base = [24576, 26624]
        qfm_pe_mark = [0, 0]
        qfm_ld = {}
        qfm_ac = {}
        l1_pv_ac = [0, 0]
        l1_eb_lg = [0, 0]
        l1_last_dvmk = [0]
        l1_last_acex = [0]
        l1_pe_sc = {}

        def l1_q_load(h):
            par = h % 2
            base = qfm_base[par]
            mark = qfm_pe_mark[par]
            @B.sync
            def _(sy, h=h, base=base, mark=mark):
                sy.wait_ge(env.s['PE'], mark)
                env.inc(sy.dma_start(
                    out=bigC[:, base:base + 2048].rearrange("p (a c) -> p a c", a=TT),
                    in_=d["sp_q"].rearrange("(a p) o -> p a o", p=128)[:, :, h * 128:(h + 1) * 128]), 'LD', 16)
            qfm_ld[h] = env.c['LD']

        def l1_q_transpose(h):
            par = h % 2
            base = qfm_base[par]
            ldq = qfm_ld[h]
            def wx(t, ldq=ldq):
                t.wait_ge(env.s['LD'], ldq)
            transpose_blocks(lambda i, base=base: bigC[:, base + i * 128: base + (i + 1) * 128], TT, None,
                             wait_extra=wx,
                             dst_batch=(bigC, lambda i0, base=base: base + i0 * 128, 128))
            qfm_ac[h] = env.c['AC']

        def l1_scores(h, qt):
            kv = h // 4
            base = qfm_base[h % 2]
            j0 = max(0, qt - 4)
            nkt = qt - j0 + 1
            dvm0, acex0 = l1_last_dvmk[0], l1_last_acex[0]
            acq = qfm_ac[h]
            @B.tensor
            def _(t, h=h, qt=qt, base=base, j0=j0, nkt=nkt, kv=kv, dvm0=dvm0, acex0=acex0, acq=acq):
                t.wait_ge(env.s['AC'], acq)
                if dvm0:
                    t.wait_ge(env.s['DV'], dvm0)
                if acex0:
                    t.wait_ge(env.s['AC'], acex0)
                for half, bb in ((0, 0), (1, 1024)):
                    lhs = bigC[half * 64:half * 64 + 64, base + qt * 128: base + (qt + 1) * 128]
                    done = 0
                    while done < nkt:
                        take = min(4, nkt - done)
                        rhs = bigC[half * 64:half * 64 + 64,
                                   kv * 2048 + (j0 + done) * 128: kv * 2048 + (j0 + done + take) * 128]
                        env.inc(t.matmul(psA[:, bb + done * 128: bb + (done + take) * 128],
                                         lhs, rhs, start=True, stop=True,
                                         tile_position=(half * 64, 0)), 'PE', 1)
                        done += take
            return env.c['PE']

        l1_q_load(0)
        l1_q_transpose(0)
        l1_q_load(1)
        l1_pe_sc[(0, 0)] = l1_scores(0, 0)
        for h in range(NH):
            kv = h // 4
            lamh = lamb[:, h:h + 1]
            for qt in range(TT):
                j0 = max(0, qt - 4)
                nkt = qt - j0 + 1
                w = nkt * 128
                if qt == 4 and h + 1 < NH:
                    l1_q_transpose(h + 1)
                pes = l1_pe_sc.pop((h, qt))
                m1 = stats[:, 54:55]; m2 = stats[:, 55:56]
                @B.vector
                def _(v, qt=qt, nkt=nkt, w=w, pes=pes):
                    v.wait_ge(env.s['PE'], pes)
                    for bb in (0, 1024):
                        v.tensor_tensor(psA[:, bb + (nkt - 1) * 128: bb + nkt * 128],
                                        psA[:, bb + (nkt - 1) * 128: bb + nkt * 128],
                                        maskc[:], op=ALU.add)
                        if qt >= 4:
                            v.tensor_tensor(psA[:, bb:bb + 128],
                                            psA[:, bb:bb + 128],
                                            maska[:], op=ALU.add)
                    v.drain()
                    v.reduce_max(m1, psA[:, 0:w], axis=AX.X)
                    v.reduce_max(m2, psA[:, 1024:1024 + w], axis=AX.X)
                    v.drain()
                    v.tensor_scalar_mul(stats[:, 56:57], m1, -0.125)
                    env.inc(v.tensor_scalar_mul(stats[:, 57:58], m2, -0.125), 'DV', 1)
                    v.drain()
                dvmk = env.c['DV']
                l1_last_dvmk[0] = dvmk
                z1 = stats[:, 58:59]; z2 = stats[:, 59:60]
                @B.scalar
                def _(sc, w=w, dvmk=dvmk):
                    sc.wait_ge(env.s['DV'], dvmk)
                    sc.activation(ptl1[:, 0:w], psA[:, 0:w], AF.Exp,
                                  bias=stats[:, 56:57], scale=0.125, accum_out=z1)
                    env.inc(sc.activation(ptl2[:, 0:w], psA[:, 1024:1024 + w], AF.Exp,
                                          bias=stats[:, 57:58], scale=0.125, accum_out=z2), 'AC', 1)
                acex = env.c['AC']
                l1_last_acex[0] = acex

                if qt + 1 < TT:
                    l1_pe_sc[(h, qt + 1)] = l1_scores(h, qt + 1)
                elif h + 1 < NH:
                    l1_pe_sc[(h + 1, 0)] = l1_scores(h + 1, 0)

                @B.vector
                def _(v, w=w, lamh=lamh, acex=acex):
                    v.wait_ge(env.s['AC'], acex)
                    zi1 = stats[:, 60:61]; zi2 = stats[:, 61:62]
                    v.reciprocal(zi1, z1)
                    v.drain()
                    v.reciprocal(zi2, z2)
                    v.drain()
                    v.tensor_scalar_mul(ptl1[:, 0:w], ptl1[:, 0:w], zi1)
                    v.tensor_mul(zi2, zi2, lamh)
                    v.drain()
                    v.tensor_scalar_mul(zi2, zi2, -1.0)
                    v.drain()
                    v.scalar_tensor_tensor(ptl1[:, 0:w], ptl2[:, 0:w], zi2, ptl1[:, 0:w],
                                           op0=ALU.mult, op1=ALU.add)
                    v.drain()
                    ds = stats[:, 62:63]
                    v.tensor_scalar(ptl1[:, 0:w], ptl1[:, 0:w], 0.0, 0.0,
                                    op0=ALU.max, op1=ALU.add, accum_out=ds)
                    v.drain()
                    v.tensor_scalar_add(ds, ds, 1e-6)
                    v.drain()
                    ri = stats[:, 63:64]
                    v.reciprocal(ri, ds)
                    v.drain()
                    env.inc(v.tensor_scalar_mul(ptl1[:, 0:w], ptl1[:, 0:w], ri), 'DV', 1)
                    v.drain()
                dvr = env.c['DV']
                @B.tensor
                def _(t, nkt=nkt, dvr=dvr):
                    t.wait_ge(env.s['DV'], dvr)
                    t.wait_ge(env.s['AC'], env.marks['pt_free'])
                    for j in range(nkt):
                        pt = ptA if j < 4 else ptB
                        env.inc(t.transpose(pt[:, (j % 4) * 128:(j % 4 + 1) * 128],
                                            ptl1[:, j * 128:(j + 1) * 128], ident[:]), 'PE', 1)
                per = env.c['PE']
                @B.scalar
                def _(sc, nkt=nkt, per=per):
                    sc.wait_ge(env.s['PE'], per)
                    for j in range(nkt):
                        pt = ptA if j < 4 else ptB
                        env.inc(sc.copy(rT[:, j * 128:(j + 1) * 128],
                                        pt[:, (j % 4) * 128:(j % 4 + 1) * 128]), 'AC', 1)
                env.marks['pt_free'] = env.c['AC']
                acr = env.c['AC']
                ps = psB if qt % 2 == 0 else psC
                bp = l1_pv_ac[qt % 2]
                @B.tensor
                def _(t, qt=qt, h=h, j0=j0, nkt=nkt, kv=kv, ps=ps, acr=acr, bp=bp):
                    t.wait_ge(env.s['AC'], acr)
                    if bp:
                        t.wait_ge(env.s['AC'], bp)
                    for j in range(nkt):
                        kt = j0 + j
                        env.inc(t.matmul(ps[:, 0:128],
                                         bigC[:, 8192 + kt * 512 + kv * 128: 8192 + kt * 512 + (kv + 1) * 128],
                                         rT[:, j * 128:(j + 1) * 128],
                                         start=(j == 0), stop=(j == nkt - 1)), 'PE', 1)
                pepv = env.c['PE']
                eb = evb if qt % 2 == 0 else evb2
                eblg = l1_eb_lg[qt % 2]
                @B.scalar
                def _(sc, ps=ps, eb=eb, pepv=pepv, eblg=eblg):
                    sc.wait_ge(env.s['PE'], pepv)
                    if eblg:
                        sc.wait_ge(env.s['LG'], eblg)
                    env.inc(sc.copy(eb[:, 0:128], ps[:, 0:128]), 'AC', 1)
                acact = env.c['AC']
                l1_pv_ac[qt % 2] = acact
                @B.gpsimd
                def _(gp, h=h, qt=qt, eb=eb, acact=acact):
                    gp.wait_ge(env.s['AC'], acact)
                    env.inc(gp.dma_start(out=d["sp_ao"][h * 128:(h + 1) * 128, qt * 128:(qt + 1) * 128],
                                         in_=eb[:, 0:128]), 'LG', 16)
                l1_eb_lg[qt % 2] = env.c['LG']
            qfm_pe_mark[h % 2] = env.c['PE']
            if h + 2 < NH:
                l1_q_load(h + 2)

        env.barrier()

        for i in range(IC):
            ws = wstage[:, (i % 2) * 2048:(i % 2 + 1) * 2048]
            @B.sync
            def _(sy, i=i):
                if i >= 2:
                    sy.wait_ge(env.s['DV'], env.c['DV'])
                env.inc(sy.dma_start(out=ws, in_=d["l1wo"][i * 128:(i + 1) * 128, :]), 'LD', 16)
            ldw = env.c['LD']
            @B.vector
            def _(v, i=i):
                v.wait_ge(env.s['LD'], ldw)
                env.inc(v.tensor_copy(bigC[:, i * 2048:(i + 1) * 2048], ws), 'DV', 1)
        dvwo = env.c['DV']
        for tt in range(TT):
            aosl = stgb
            @B.sync
            def _(sy, tt=tt):
                sy.wait_ge(env.s['PE'], env.c['PE'])
                sy.wait_ge(env.s['LG'], env.c['LG'])
                env.inc(sy.dma_start(
                    out=aosl.rearrange("p (a c) -> p a c", a=16),
                    in_=d["sp_ao"].rearrange("(a p) t -> p a t", p=128)[:, :, tt * 128:(tt + 1) * 128]), 'LD', 16)
            ldao = env.c['LD']
            for oc4 in range(4):
                ps = psB if oc4 % 2 == 0 else psC
                @B.tensor
                def _(t, tt=tt, oc4=oc4, ps=ps):
                    t.wait_ge(env.s['LD'], ldao)
                    t.wait_ge(env.s['DV'], dvwo)
                    if oc4 >= 2 or tt > 0:
                        t.wait_ge(env.s['AC'], env.c['AC'])
                    for hh in range(16):
                        env.inc(t.matmul(ps[:], aosl[:, hh * 128:(hh + 1) * 128],
                                         bigC[:, hh * 2048 + oc4 * 512: hh * 2048 + (oc4 + 1) * 512],
                                         start=(hh == 0), stop=(hh == 15)), 'PE', 1)
                pemm = env.c['PE']
                eb = evb if oc4 % 2 == 0 else evb2
                @B.scalar
                def _(sc, ps=ps, eb=eb):
                    sc.wait_ge(env.s['PE'], pemm)
                    sc.wait_ge(env.s['LG'], env.c['LG'])
                    env.inc(sc.copy(eb[:, 0:512], ps[:]), 'AC', 1)
                acs2 = env.c['AC']
                @B.gpsimd
                def _(gp, tt=tt, oc4=oc4, eb=eb):
                    gp.wait_ge(env.s['AC'], acs2)
                    env.inc(gp.dma_start(out=d["sp_s"][tt * 128:(tt + 1) * 128, oc4 * 512:(oc4 + 1) * 512],
                                         in_=eb[:, 0:512]), 'LG', 16)

        env.barrier()

        @B.vector
        def _(v):
            env.inc(v.memset(bigC[:], 0.0), 'DV', 1)
            v.drain()
        moe_steps = [(e, fc) for e in range(8) for fc in range(4)]
        moe_strip_ac = {}
        moe_ws_dv = [0, 0]
        moe_stg_pe = [0, 0]
        moe_psum_dv = [0, 0]

        def moe_prep(si):
            e, fc = moe_steps[si]
            buf = si % 2
            for rt in range(4):
                par = rt % 2
                ws = wstage[:, par * 2048:(par + 1) * 2048]
                sg = stgb if par == 0 else stgb2
                wsdv = moe_ws_dv[par]
                @B.sync
                def _(sy, e=e, fc=fc, rt=rt, wsdv=wsdv):
                    sy.wait_ge(env.s['DV'], wsdv)
                    env.inc(sy.dma_start(out=ws, in_=d["wex"][e * 2048 + fc * 512 + rt * 128: e * 2048 + fc * 512 + (rt + 1) * 128, :]), 'LD', 16)
                ldw = env.c['LD']
                stpe = moe_stg_pe[par]
                @B.vector
                def _(v, ldw=ldw, stpe=stpe, sg=sg):
                    v.wait_ge(env.s['LD'], ldw)
                    v.wait_ge(env.s['PE'], stpe)
                    env.inc(v.tensor_copy(sg[:], ws), 'DV', 1)
                moe_ws_dv[par] = env.c['DV']
                dvc = env.c['DV']
                transpose_blocks(lambda i, sg=sg: sg[:, i * 128:(i + 1) * 128], IC, None,
                                 wait_dv=dvc,
                                 dst_batch=(wtrans, lambda i0, buf=buf, rt=rt: buf * 8192 + i0 * 512 + rt * 128, 512))
                moe_stg_pe[par] = env.c['PE']
            moe_strip_ac[si] = env.c['AC']

        def moe_prep_rt(si, rt):
            e, fc = moe_steps[si]
            buf = si % 2
            par = rt % 2
            ws = wstage[:, par * 2048:(par + 1) * 2048]
            sg = stgb if par == 0 else stgb2
            wsdv = moe_ws_dv[par]
            @B.sync
            def _(sy, e=e, fc=fc, rt=rt, wsdv=wsdv):
                sy.wait_ge(env.s['DV'], wsdv)
                env.inc(sy.dma_start(out=ws, in_=d["wex"][e * 2048 + fc * 512 + rt * 128: e * 2048 + fc * 512 + (rt + 1) * 128, :]), 'LD', 16)
            ldw = env.c['LD']
            stpe = moe_stg_pe[par]
            @B.vector
            def _(v, ldw=ldw, stpe=stpe, sg=sg):
                v.wait_ge(env.s['LD'], ldw)
                v.wait_ge(env.s['PE'], stpe)
                env.inc(v.tensor_copy(sg[:], ws), 'DV', 1)
            moe_ws_dv[par] = env.c['DV']
            dvc = env.c['DV']
            transpose_blocks(lambda i, sg=sg: sg[:, i * 128:(i + 1) * 128], IC, None,
                             wait_dv=dvc,
                             dst_batch=(wtrans, lambda i0, buf=buf, rt=rt: buf * 8192 + i0 * 512 + rt * 128, 512))
            moe_stg_pe[par] = env.c['PE']
            if rt == 3:
                moe_strip_ac[si] = env.c['AC']

        moe_prep(0)
        for si, (e, fc) in enumerate(moe_steps):
            buf = si % 2
            acst = moe_strip_ac[si]
            for tt in range(TT):
                if tt % 4 == 0 and si + 1 < len(moe_steps):
                    moe_prep_rt(si + 1, tt // 4)
                ps = psB if tt % 2 == 0 else psC
                bp = moe_psum_dv[tt % 2]
                @B.tensor
                def _(t, tt=tt, si=si, buf=buf, acst=acst, bp=bp):
                    t.wait_ge(env.s['AC'], acst)
                    if bp:
                        t.wait_ge(env.s['DV'], bp)
                    for i in range(IC):
                        env.inc(t.matmul(ps[:], bigA[:, i * 2048 + tt * 128: i * 2048 + tt * 128 + 128],
                                         wtrans[:, buf * 8192 + i * 512: buf * 8192 + (i + 1) * 512],
                                         start=(i == 0), stop=(i == IC - 1)), 'PE', 1)
                pemm = env.c['PE']
                @B.vector
                def _(v, tt=tt, fc=fc, e=e, ps=ps):
                    v.wait_ge(env.s['PE'], pemm)
                    env.inc(v.scalar_tensor_tensor(
                        bigC[:, tt * 2048 + fc * 512: tt * 2048 + (fc + 1) * 512],
                        ps[:], selm[:, tt * 8 + e: tt * 8 + e + 1],
                        bigC[:, tt * 2048 + fc * 512: tt * 2048 + (fc + 1) * 512],
                        op0=ALU.mult, op1=ALU.add), 'DV', 1)
                moe_psum_dv[tt % 2] = env.c['DV']

        env.barrier()

        for tt in range(TT):
            for oc4 in range(4):
                k = tt * 4 + oc4
                gsb = sg0 if k % 2 == 0 else sg1
                @B.sync
                def _(sy, tt=tt, oc4=oc4, k=k):
                    if k > 0:
                        sy.wait_ge(env.s['OT'], env.c['OT'])
                        sy.wait_ge(env.s['DV'], env.c['DV'])
                    sy.wait_ge(env.s['LG'], env.c['LG'])
                    env.inc(sy.dma_start(out=sx0[:], in_=d["sp_x1"][tt * 128:(tt + 1) * 128, oc4 * 512:(oc4 + 1) * 512]), 'LD', 16)
                    env.inc(sy.dma_start(out=ssb[:], in_=d["sp_s"][tt * 128:(tt + 1) * 128, oc4 * 512:(oc4 + 1) * 512]), 'LD', 16)
                ldxs = env.c['LD']
                @B.scalar
                def _(sc, tt=tt, oc4=oc4, gsb=gsb):
                    sc.wait_ge(env.s['LD'], ldxs)
                    sc.wait_ge(env.s['DV'], env.c['DV'])
                    env.inc(sc.activation(gsb[:], bigC[:, tt * 2048 + oc4 * 512: tt * 2048 + (oc4 + 1) * 512],
                                          AF.Silu, bias=0.0, scale=1.0), 'AC', 1)
                acg = env.c['AC']
                @B.vector
                def _(v, gsb=gsb):
                    v.wait_ge(env.s['AC'], acg)
                    v.wait_ge(env.s['LD'], ldxs)
                    v.tensor_tensor(gsb[:], gsb[:], ssb[:], op=ALU.mult)
                    v.drain()
                    env.inc(v.tensor_tensor(sx0[:], sx0[:], gsb[:], op=ALU.add), 'DV', 1)
                    v.drain()
                dvx = env.c['DV']
                @B.gpsimd
                def _(gp, tt=tt, oc4=oc4):
                    gp.wait_ge(env.s['DV'], dvx)
                    env.inc(gp.dma_start(out=out_ap[tt * 128:(tt + 1) * 128, oc4 * 512:(oc4 + 1) * 512],
                                         in_=sx0[:]), 'OT', 16)

        @B.gpsimd
        def _(gp):
            gp.wait_ge(env.s['OT'], env.c['OT'])

    return nc


_NC_CACHE = [None]


def kernel(**inputs):
    f32 = lambda a: np.ascontiguousarray(np.asarray(a, dtype=np.float32))
    x = f32(inputs["L0_attn_x"])
    hd = 128
    inv_freq = (1.0 / (10000.0 ** (np.arange(0, hd, 2, dtype=np.float32) / np.float32(hd)))).astype(np.float32)
    t = np.arange(S, dtype=np.float32)
    freqs = np.outer(t, inv_freq).astype(np.float32)
    emb = np.concatenate([freqs, freqs], axis=-1)
    cosT = np.ascontiguousarray(np.cos(emb).astype(np.float32))
    sinT = np.sin(emb).astype(np.float32)
    sinF = sinT.copy()
    sinF[:, :64] = -sinF[:, :64]
    sinF = np.ascontiguousarray(sinF)

    dn = np.ascontiguousarray(np.asarray(inputs["L0_ffn_down_reorder_index"], dtype=np.int32).reshape(IFF, 1))

    common = {
        "l0wq": f32(inputs["L0_attn_Wq"]), "l0wk": f32(inputs["L0_attn_Wk"]),
        "l0wv": f32(inputs["L0_attn_Wv"]), "l0wo": f32(inputs["L0_attn_Wo"]),
        "wg": f32(inputs["L0_ffn_w_gate"]), "wu": f32(inputs["L0_ffn_w_up"]),
        "wd": f32(inputs["L0_ffn_w_down"]), "dnidx": dn,
        "l1wq": f32(inputs["L1_attn_Wq"]), "l1wk": f32(inputs["L1_attn_Wk"]),
        "l1wv": f32(inputs["L1_attn_Wv"]), "l1wo": f32(inputs["L1_attn_Wo"]),
        "lam": f32(inputs["L1_attn_lam"]).reshape(1, NH),
        "wr": f32(inputs["L1_ffn_w_router"]),
        "wex": f32(inputs["L1_ffn_w_experts"]).reshape(8 * D, D),
        "cosT": cosT, "sinF": sinF,
    }
    if _NC_CACHE[0] is None:
        _NC_CACHE[0] = build_nc()
    nc = _NC_CACHE[0]
    in_maps = [dict(common, x0=np.ascontiguousarray(x[c])) for c in range(8)]
    res = run_bass_kernel_spmd(nc, in_maps, core_ids=list(range(8)))
    return np.stack([r["out"] for r in res.results], axis=0)


# version 09
# speedup vs baseline: 1.2260x; 1.0384x over previous
import math
import numpy as np
from contextlib import ExitStack

import concourse.bass as bass
import concourse.mybir as mybir
from concourse.bass_utils import run_bass_kernel_spmd
from concourse.bass import IndirectOffsetOnAxis

F32 = mybir.dt.float32
BF16 = mybir.dt.bfloat16
F8 = mybir.dt.float8e4
I32 = mybir.dt.int32
AF = mybir.ActivationFunctionType
ALU = mybir.AluOpType
AX = mybir.AxisListType

S = 2048
D = 2048
TT = 16
IC = 16
IFF = 5632
NJT = 44
NH = 16
HD = 128
NEG = -1.0e9


class Env:
    def __init__(self, nc, block, sems):
        self.nc = nc
        self.block = block
        self.s = sems
        self.c = {k: 0 for k in sems}

    def inc(self, inst, k, by=1):
        inst.then_inc(self.s[k], by)
        self.c[k] += by
        return self.c[k]

    def barrier(self):
        self.c['PH'] += 5
        tgt = self.c['PH']
        for attr in ('tensor', 'vector', 'scalar', 'gpsimd', 'sync'):
            def mk(eng, tgt=tgt):
                eng.nop().then_inc(self.s['PH'], 1)
                eng.wait_ge(self.s['PH'], tgt)
            getattr(self.block, attr)(mk)


_DEBUG_SPILL = [False]


def build_nc():
    nc = bass.Bass("TRN2", target_bir_lowering=False, debug=False)
    d = {}
    def inp(name, shape, dt=F32):
        d[name] = nc.dram_tensor(name, shape, dt, kind="ExternalInput").ap()
    inp("x0", [S, D])
    inp("l0wq", [D, D]); inp("l0wk", [D, D]); inp("l0wv", [D, D]); inp("l0wo", [D, D])
    inp("wg", [IFF, D]); inp("wu", [IFF, D]); inp("wd", [D, IFF])
    inp("dnidx", [IFF, 1], I32)
    inp("l1wq", [D, D]); inp("l1wk", [D, 512]); inp("l1wv", [D, 512]); inp("l1wo", [D, D])
    inp("lam", [1, NH]); inp("wr", [8, D]); inp("wex", [8 * D, D])
    inp("cosT", [S, HD]); inp("sinF", [S, HD])
    out_ap = nc.dram_tensor("out", [S, D], F32, kind="ExternalOutput").ap()
    def scr(name, shape, dt):
        if _DEBUG_SPILL[0]:
            d[name] = nc.dram_tensor(name, shape, dt, kind="ExternalOutput").ap()
        else:
            d[name] = nc.dram_tensor(name, shape, dt).ap()
    scr("sp_q", [D, S], BF16)
    scr("sp_k", [D, S], BF16)
    scr("sp_v", [S, D], BF16)
    scr("sp_ao", [D, S], BF16)
    scr("sp_s", [S, D], BF16)
    scr("sp_T", [S, IFF], F32)
    scr("sp_qtT", [IFF, S], BF16)
    scr("sp_x1", [S, D], F32)
    if _DEBUG_SPILL[0]:
        dbg_a = nc.dram_tensor("dbg_a", [128, 32768], BF16, kind="ExternalOutput").ap()
        dbg_c = nc.dram_tensor("dbg_c", [128, 32768], BF16, kind="ExternalOutput").ap()
        dbg_q = nc.dram_tensor("dbg_q", [D, S], BF16, kind="ExternalOutput").ap()
        dbg_ao = nc.dram_tensor("dbg_ao", [D, S], BF16, kind="ExternalOutput").ap()
        dbg_s = nc.dram_tensor("dbg_s", [S, D], BF16, kind="ExternalOutput").ap()

    with ExitStack() as ctx:
        def sb(n, s_, dt):
            return ctx.enter_context(nc.sbuf_tensor(n, s_, dt))
        bigA = sb("bigA", [128, 32768], BF16)
        bigC = sb("bigC", [128, 32768], BF16)
        wstage = sb("wstage", [128, 5632], F32)
        wtrans = sb("wtrans", [128, 16384], BF16)
        f8s = sb("f8s", [128, IFF], F8)
        evb = sb("evb", [128, 512], BF16)
        evb2 = sb("evb2", [128, 512], BF16)
        sg0 = sb("sg0", [128, 512], F32)
        sg1 = sb("sg1", [128, 512], F32)
        sx0 = sb("sx0", [128, 1024], F32)
        ssb = sb("ssb", [128, 1024], BF16)
        stats = sb("stats", [128, 64], F32)
        selm = sb("selm", [128, 128], F32)
        tmaxs = sb("tmaxs", [128, 16], F32)
        ident = sb("ident", [128, 128], BF16)
        maskc = sb("maskc", [128, 128], F32)
        maska = sb("maska", [128, 128], F32)
        lamb = sb("lamb", [128, NH], F32)
        onesc = sb("onesc", [1, 128], F32)
        lams = sb("lams", [1, NH], F32)
        idxs = sb("idxs", [128, NJT], I32)

        stgb2 = sb("stgb2", [128, 2048], BF16)
        stgb = wstage[:, 4096:5120].bitcast(BF16)
        stg5 = bigC[:, 0:5632]

        psA = ctx.enter_context(nc.psum_tensor("psA", [128, 2048], F32))
        psB = ctx.enter_context(nc.psum_tensor("psB", [128, 512], F32))
        psC = ctx.enter_context(nc.psum_tensor("psC", [128, 512], F32))
        ptA = ctx.enter_context(nc.psum_tensor("ptA", [128, 1024], BF16))
        ptB = ctx.enter_context(nc.psum_tensor("ptB", [128, 1024], BF16))

        sems = {}
        for name in ('LD', 'LG', 'PE', 'DV', 'AC', 'GP', 'PH', 'OT'):
            sems[name] = ctx.enter_context(nc.semaphore(name))
        block = ctx.enter_context(nc.Block())
        env = Env(nc, block, sems)
        B = env.block

        env.marks = {'pt_free': 0}

        def transpose_blocks(src_ap_fn, n128, dst_fn, wait_dv=None, wait_extra=None,
                             dst_batch=None, evac_eng='scalar'):
            for g0 in range(0, n128, 16):
                n = min(16, n128 - g0)
                @B.tensor
                def _(t, g0=g0, n=n):
                    if g0 == 0:
                        if wait_dv is not None:
                            t.wait_ge(env.s['DV'], wait_dv)
                        if wait_extra is not None:
                            wait_extra(t)
                    t.wait_ge(env.s['AC'], env.marks['pt_free'])
                    for j in range(n):
                        i = g0 + j
                        pt = ptA if j < 8 else ptB
                        env.inc(t.transpose(pt[:, (j % 8) * 128:(j % 8 + 1) * 128],
                                            src_ap_fn(i), ident[:]), 'PE', 1)
                pet = env.c['PE']
                eng_dec = getattr(B, evac_eng)
                @eng_dec
                def _(sc, g0=g0, n=n, pet=pet):
                    cp = sc.copy if evac_eng == 'scalar' else sc.tensor_copy
                    sc.wait_ge(env.s['PE'], pet)
                    if dst_batch is not None:
                        tgt, base_fn, step = dst_batch
                        W = tgt.shape[1]
                        for h0 in range(0, n, 8):
                            m = min(8, n - h0)
                            pt = ptA if h0 == 0 else ptB
                            b0 = base_fn(g0 + h0)
                            if step == 128:
                                env.inc(cp(tgt[:, b0:b0 + m * 128],
                                           pt[:, 0:m * 128]), 'AC', 1)
                                continue
                            mm_ = m
                            if b0 + m * step > W:
                                mm_ = m - 1
                                env.inc(cp(tgt[:, base_fn(g0 + h0 + mm_):base_fn(g0 + h0 + mm_) + 128],
                                           pt[:, mm_ * 128:(mm_ + 1) * 128]), 'AC', 1)
                            if mm_ == 1:
                                env.inc(cp(tgt[:, b0:b0 + 128], pt[:, 0:128]), 'AC', 1)
                            elif mm_ > 0:
                                dst3 = tgt[:, b0:b0 + mm_ * step].rearrange(
                                    "p (a c) -> p a c", a=mm_)[:, :, 0:128]
                                src3 = pt[:, 0:mm_ * 128].rearrange("p (a c) -> p a c", a=mm_)
                                env.inc(cp(dst3, src3), 'AC', 1)
                    else:
                        for j in range(n):
                            i = g0 + j
                            pt = ptA if j < 8 else ptB
                            env.inc(cp(dst_fn(i), pt[:, (j % 8) * 128:(j % 8 + 1) * 128]), 'AC', 1)
                env.marks['pt_free'] = env.c['AC']
            return env.c['AC']

        @B.sync
        def _(sy):
            env.inc(sy.dma_start(out=idxs[:], in_=d["dnidx"].rearrange("(a b) c -> b (a c)", b=128),
                                 allow_slow_non_contiguous=True), 'LD', 16)
            env.inc(sy.dma_start(out=lams[:], in_=d["lam"]), 'LD', 16)
        ld_init = env.c['LD']

        @B.gpsimd
        def _(gp):
            gp.memset(ident[:], 1.0)
            gp.affine_select(ident[:], ident[:], [[-1, 128]], ALU.is_equal, 0.0,
                             base=0, channel_multiplier=1)
            gp.memset(maskc[:], 0.0)
            gp.affine_select(maskc[:], maskc[:], [[-1, 128]], ALU.is_ge, NEG,
                             base=0, channel_multiplier=1)
            gp.memset(maska[:], 0.0)
            env.inc(gp.affine_select(maska[:], maska[:], [[1, 128]], ALU.is_ge, NEG,
                                     base=-1, channel_multiplier=-1), 'GP', 1)

        @B.vector
        def _(v):
            v.memset(onesc[:], 1.0)
            env.inc(v.memset(stats[:, 5:6], 1e-5), 'DV', 1)


        @B.tensor
        def _(t):
            t.wait_ge(env.s['LD'], ld_init)
            t.wait_ge(env.s['DV'], env.c['DV'])
            env.inc(t.matmul(psB[:, 0:NH], onesc[:], lams[:], start=True, stop=True), 'PE', 1)

        @B.vector
        def _(v):
            v.wait_ge(env.s['PE'], env.c['PE'])
            env.inc(v.tensor_copy(lamb[:], psB[:, 0:NH]), 'DV', 1)

        env.barrier()

        x0_ap = d["x0"]
        wr2t = wtrans[:, 16128:16384]

        def ln_quant_phase(x_src, layer):
            for tt in range(TT):
                xw = wstage[:, 0:2048]
                xnf = wstage[:, 2048:4096]
                mu = stats[:, 0:1]; ss = stats[:, 1:2]; var = stats[:, 2:3]
                sd = stats[:, 3:4]; r0 = stats[:, 4:5]
                t1 = stats[:, 6:7]; t2 = stats[:, 7:8]
                rmax = stats[:, 8:9]; q1 = stats[:, 9:10]; q2 = stats[:, 10:11]

                @B.sync
                def _(sy, tt=tt):
                    if tt >= 1:
                        sy.wait_ge(env.s['DV'], env.c['DV'])
                        sy.wait_ge(env.s['AC'], env.c['AC'])
                    if layer == 1:
                        sy.wait_ge(env.s['LG'], env.c['LG'])
                    env.inc(sy.dma_start(out=xw, in_=x_src[tt * 128:(tt + 1) * 128, :]), 'LD', 16)
                ldx = env.c['LD']

                @B.vector
                def _(v):
                    v.wait_ge(env.s['LD'], ldx)
                    env.inc(v.reduce_sum(mu, xw, axis=AX.X), 'DV', 1)

                @B.scalar
                def _(sc):
                    sc.wait_ge(env.s['LD'], ldx)
                    env.inc(sc.activation(wtrans[:, 0:2048], xw, AF.Square, bias=0.0, scale=1.0,
                                          accum_out=ss), 'AC', 1)
                acs = env.c['AC']

                @B.vector
                def _(v):
                    v.wait_ge(env.s['AC'], acs)
                    v.tensor_scalar_mul(mu, mu, 1.0 / 2048.0)
                    v.drain()
                    v.tensor_scalar_mul(var, ss, 1.0 / 2048.0)
                    v.tensor_mul(t1, mu, mu)
                    v.drain()
                    env.inc(v.tensor_sub(var, var, t1), 'DV', 1)
                    v.drain()
                dvv = env.c['DV']

                @B.scalar
                def _(sc):
                    sc.wait_ge(env.s['DV'], dvv)
                    env.inc(sc.activation(sd, var, AF.Sqrt, bias=stats[:, 5:6], scale=1.0), 'AC', 1)
                acsd = env.c['AC']

                @B.vector
                def _(v):
                    v.wait_ge(env.s['AC'], acsd)
                    v.reciprocal(r0, sd)
                    v.drain()
                    v.tensor_scalar_add(t2, var, 1e-5)
                    v.drain()
                    for _i in range(2):
                        v.tensor_mul(t1, r0, r0)
                        v.drain()
                        v.tensor_mul(t1, t1, t2)
                        v.drain()
                        v.tensor_scalar(t1, t1, -0.5, 1.5, op0=ALU.mult, op1=ALU.add)
                        v.drain()
                        v.tensor_mul(r0, r0, t1)
                        v.drain()
                    v.tensor_scalar(xnf, xw, mu, r0, op0=ALU.subtract, op1=ALU.mult)
                    v.drain()
                    env.inc(v.tensor_copy(stgb[:], xnf), 'DV', 1)
                dvxn = env.c['DV']

                transpose_blocks(lambda i: stgb[:, i * 128:(i + 1) * 128], IC, None,
                                 wait_dv=dvxn, evac_eng='vector',
                                 dst_batch=(bigA, lambda i0, tt=tt: i0 * 2048 + tt * 128, 2048))
                acev = env.c['AC']

                if layer == 0:
                    @B.vector
                    def _(v):
                        v.wait_ge(env.s['DV'], dvxn)
                        v.wait_ge(env.s['AC'], acev)
                        env.inc(v.reduce_max(rmax, xnf, axis=AX.X, apply_absolute_value=True), 'DV', 1)
                        v.drain()
                        v.reciprocal(q1, rmax)
                        v.drain()
                        v.tensor_scalar_mul(q1, q1, 224.0)
                        v.drain()
                        v.tensor_scalar_mul(q2, rmax, 1.0 / 224.0)
                        v.drain()
                        env.inc(v.tensor_scalar_mul(f8s[:, 0:2048], xnf, q1), 'DV', 1)
                        v.drain()
                        env.inc(v.tensor_scalar_mul(stgb[:], f8s[:, 0:2048], q2), 'DV', 1)
                    dvq = env.c['DV']
                    transpose_blocks(lambda i: stgb[:, i * 128:(i + 1) * 128], IC, None,
                                     wait_dv=dvq, evac_eng='vector',
                                     dst_batch=(bigC, lambda i0, tt=tt: i0 * 2048 + tt * 128, 2048))
                else:
                    lob = stg5[:, 0:2048]
                    @B.vector
                    def _(v):
                        v.wait_ge(env.s['AC'], acev)
                        v.drain()
                        env.inc(v.tensor_tensor(lob, xnf, stgb[:], op=ALU.subtract), 'DV', 1)
                        v.drain()
                    dvlo = env.c['DV']
                    transpose_blocks(lambda i: lob[:, i * 128:(i + 1) * 128], IC, None,
                                     wait_dv=dvlo, evac_eng='vector',
                                     dst_batch=(stg5, lambda i0: 2048 + i0 * 128, 128))
                    aclo = env.c['AC']

                    @B.tensor
                    def _(t, tt=tt):
                        t.wait_ge(env.s['AC'], aclo)
                        for i in range(IC):
                            t.matmul(psB[:, 0:16], bigA[:, i * 2048 + tt * 128: i * 2048 + tt * 128 + 128],
                                     wr2t[:, i * 16:(i + 1) * 16], start=(i == 0), stop=False)
                        for i in range(IC):
                            env.inc(t.matmul(psB[:, 0:16], stg5[:, 2048 + i * 128:2048 + (i + 1) * 128],
                                             wr2t[:, i * 16:(i + 1) * 16],
                                             start=False, stop=(i == IC - 1)), 'PE', 1)
                    perl = env.c['PE']

                    @B.vector
                    def _(v, tt=tt):
                        v.wait_ge(env.s['PE'], perl)
                        lg = stats[:, 16:24]
                        m8 = stats[:, 24:32]
                        v.tensor_copy(lg, psB[:, 0:8])
                        v.drain()
                        v.tensor_tensor(lg, lg, psB[:, 8:16], op=ALU.add)
                        v.drain()
                        v.max(m8, lg)
                        v.drain()
                        nt = stats[:, 32:33]
                        v.memset(nt, 1.0)
                        v.drain()
                        for e in range(8):
                            he = stats[:, 33:34]
                            v.tensor_tensor(he, lg[:, e:e + 1], m8[:, 0:1], op=ALU.is_equal)
                            v.drain()
                            v.tensor_mul(selm[:, tt * 8 + e: tt * 8 + e + 1], he, nt)
                            v.drain()
                            v.tensor_sub(nt, nt, selm[:, tt * 8 + e: tt * 8 + e + 1])
                            v.drain()
                        env.inc(v.engine_nop(), 'DV', 1)

        ln_quant_phase(x0_ap, 0)
        env.barrier()
        if _DEBUG_SPILL[0]:
            @B.gpsimd
            def _(gp):
                env.inc(gp.dma_start(out=dbg_a, in_=bigA[:]), 'LG', 16)
                env.inc(gp.dma_start(out=dbg_c, in_=bigC[:]), 'LG', 16)
                gp.wait_ge(env.s['LG'], env.c['LG'])
            env.barrier()

        def l0_proj_fm(w_ap, sp_dst):
            for oc in range(16):
                buf = oc % 2
                ws = wstage[:, buf * 2048:(buf + 1) * 2048]
                st0 = 8192 * buf
                @B.sync
                def _(sy, oc=oc):
                    sy.wait_ge(env.s['DV'], env.c['DV'])
                    env.inc(sy.dma_start(out=ws, in_=w_ap[oc * 128:(oc + 1) * 128, :]), 'LD', 16)
                ldw = env.c['LD']
                @B.vector
                def _(v, oc=oc):
                    v.wait_ge(env.s['LD'], ldw)
                    v.wait_ge(env.s['PE'], env.c['PE'])
                    env.inc(v.tensor_copy(stgb[:], ws), 'DV', 1)
                dvc = env.c['DV']
                def wx(t):
                    pass
                transpose_blocks(lambda i: stgb[:, i * 128:(i + 1) * 128], IC, None,
                                 wait_dv=dvc,
                                 dst_batch=(wtrans, lambda i0, st0=st0: st0 + i0 * 128, 128))
                acst = env.c['AC']
                for tc in range(4):
                    ps = psB if tc % 2 == 0 else psC
                    @B.tensor
                    def _(t, tc=tc, oc=oc, st0=st0):
                        t.wait_ge(env.s['AC'], acst)
                        if tc >= 2 or oc > 0:
                            t.wait_ge(env.s['AC'], env.c['AC'])
                        for i in range(IC):
                            env.inc(t.matmul(ps[:], wtrans[:, st0 + i * 128: st0 + (i + 1) * 128],
                                             bigA[:, i * 2048 + tc * 512: i * 2048 + (tc + 1) * 512],
                                             start=(i == 0), stop=(i == IC - 1)), 'PE', 1)
                    pemm = env.c['PE']
                    eb = evb if tc % 2 == 0 else evb2
                    @B.scalar
                    def _(sc):
                        sc.wait_ge(env.s['PE'], pemm)
                        sc.wait_ge(env.s['LG'], env.c['LG'])
                        env.inc(sc.copy(eb[:, 0:512], ps[:]), 'AC', 1)
                    acev2 = env.c['AC']
                    @B.gpsimd
                    def _(gp, oc=oc, tc=tc, eb=eb):
                        gp.wait_ge(env.s['AC'], acev2)
                        env.inc(gp.dma_start(out=sp_dst[oc * 128:(oc + 1) * 128, tc * 512:(tc + 1) * 512],
                                             in_=eb[:, 0:512]), 'LG', 16)

        l0_proj_fm(d["l0wq"], d["sp_q"])
        l0_proj_fm(d["l0wk"], d["sp_k"])

        for oc4 in range(4):
            for rt in range(4):
                ws = wstage[:, (rt % 2) * 2048:(rt % 2 + 1) * 2048]
                @B.sync
                def _(sy):
                    sy.wait_ge(env.s['DV'], env.c['DV'])
                    env.inc(sy.dma_start(out=ws, in_=d["l0wv"][(oc4 * 4 + rt) * 128:(oc4 * 4 + rt + 1) * 128, :]), 'LD', 16)
                ldw = env.c['LD']
                @B.vector
                def _(v):
                    v.wait_ge(env.s['LD'], ldw)
                    v.wait_ge(env.s['PE'], env.c['PE'])
                    env.inc(v.tensor_copy(stgb[:], ws), 'DV', 1)
                dvc = env.c['DV']
                transpose_blocks(lambda i: stgb[:, i * 128:(i + 1) * 128], IC, None,
                                 wait_dv=dvc,
                                 dst_batch=(wtrans, lambda i0, rt=rt: i0 * 512 + rt * 128, 512))
            acst = env.c['AC']
            for tt in range(TT):
                ps = psB if tt % 2 == 0 else psC
                @B.tensor
                def _(t, tt=tt, oc4=oc4):
                    t.wait_ge(env.s['AC'], acst)
                    if tt >= 2 or oc4 > 0:
                        t.wait_ge(env.s['AC'], env.c['AC'])
                    for i in range(IC):
                        env.inc(t.matmul(ps[:], bigA[:, i * 2048 + tt * 128: i * 2048 + tt * 128 + 128],
                                         wtrans[:, i * 512:(i + 1) * 512],
                                         start=(i == 0), stop=(i == IC - 1)), 'PE', 1)
                pemm = env.c['PE']
                eb = evb if tt % 2 == 0 else evb2
                @B.scalar
                def _(sc):
                    sc.wait_ge(env.s['PE'], pemm)
                    sc.wait_ge(env.s['LG'], env.c['LG'])
                    env.inc(sc.copy(eb[:, 0:512], ps[:]), 'AC', 1)
                acev2 = env.c['AC']
                @B.gpsimd
                def _(gp, tt=tt, oc4=oc4, eb=eb):
                    gp.wait_ge(env.s['AC'], acev2)
                    env.inc(gp.dma_start(out=d["sp_v"][tt * 128:(tt + 1) * 128, oc4 * 512:(oc4 + 1) * 512],
                                         in_=eb[:, 0:512]), 'LG', 16)

        env.barrier()

        ptl = bigA[:, 6144:8192]
        pTs = bigA[:, 8192:16384]
        qkv_base = [0, 16384]
        qkv_pe_mark = [0, 0]
        qkv_ld = {}
        pv_ac = [0, 0]
        eb_lg = [0, 0]
        last_dvm = [0]
        last_acex = [0]
        l0_last_trpe = [0]

        def l0_qkv_load(h):
            par = h % 2
            base = qkv_base[par]
            mark = qkv_pe_mark[par]
            @B.sync
            def _(sy, h=h, base=base, mark=mark):
                sy.wait_ge(env.s['PE'], mark)
                env.inc(sy.dma_start(out=bigA[:, base:base + 2048], in_=d["sp_q"][h * 128:(h + 1) * 128, :]), 'LD', 16)
                env.inc(sy.dma_start(out=bigA[:, base + 2048:base + 4096], in_=d["sp_k"][h * 128:(h + 1) * 128, :]), 'LD', 16)
                env.inc(sy.dma_start(
                    out=bigA[:, base + 4096:base + 6144].rearrange("p (a c) -> p a c", a=TT),
                    in_=d["sp_v"].rearrange("(a p) o -> p a o", p=128)[:, :, h * 128:(h + 1) * 128]), 'LD', 16)
            qkv_ld[h] = env.c['LD']

        def l0_scores(h, qt):
            base = qkv_base[h % 2]
            dvm0, acex0 = last_dvm[0], last_acex[0]
            @B.tensor
            def _(t, h=h, qt=qt, base=base, dvm0=dvm0, acex0=acex0):
                t.wait_ge(env.s['LD'], qkv_ld[h])
                if dvm0:
                    t.wait_ge(env.s['DV'], dvm0)
                if acex0:
                    t.wait_ge(env.s['AC'], acex0)
                for kc in range(4):
                    env.inc(t.matmul(psA[:, kc * 512:(kc + 1) * 512],
                                     bigA[:, base + qt * 128: base + (qt + 1) * 128],
                                     bigA[:, base + 2048 + kc * 512: base + 2048 + (kc + 1) * 512],
                                     start=True, stop=True), 'PE', 1)
            return env.c['PE']

        l0_qkv_load(0)
        l0_scores(0, 0)
        pe_sc = {(0, 0): env.c['PE']}
        for h in range(NH):
            if h + 1 < NH:
                l0_qkv_load(h + 1)
            base = qkv_base[h % 2]
            for qg in range(4):
                for qi in range(4):
                    qt = qg * 4 + qi
                    pesc = pe_sc.pop((h, qt))
                    m0 = stats[:, 40:41]; nm = stats[:, 41:42]
                    zz = stats[:, 42:43]; zi = stats[:, 43:44]
                    @B.vector
                    def _(v, pesc=pesc):
                        v.wait_ge(env.s['PE'], pesc)
                        v.reduce_max(m0, psA[:], axis=AX.X)
                        v.drain()
                        env.inc(v.tensor_scalar_mul(nm, m0, -1.0 / math.sqrt(128.0)), 'DV', 1)
                        v.drain()
                    dvm = env.c['DV']
                    last_dvm[0] = dvm

                    trpe = l0_last_trpe[0]
                    @B.scalar
                    def _(sc, dvm=dvm, trpe=trpe):
                        sc.wait_ge(env.s['DV'], dvm)
                        if trpe:
                            sc.wait_ge(env.s['PE'], trpe)
                        env.inc(sc.activation(ptl[:], psA[:], AF.Exp,
                                              bias=nm, scale=1.0 / math.sqrt(128.0),
                                              accum_out=zz), 'AC', 1)
                    acex = env.c['AC']
                    last_acex[0] = acex

                    if qt + 1 < 16:
                        pe_sc[(h, qt + 1)] = l0_scores(h, qt + 1)
                    elif h + 1 < NH:
                        pe_sc[(h + 1, 0)] = l0_scores(h + 1, 0)

                    @B.vector
                    def _(v, acex=acex):
                        v.wait_ge(env.s['AC'], acex)
                        v.reciprocal(zi, zz)
                        v.drain()
                        env.inc(v.tensor_scalar_mul(ptl[:], ptl[:], zi), 'DV', 1)
                    dvp = env.c['DV']

                    transpose_blocks(lambda i: ptl[:, i * 128:(i + 1) * 128], 16, None,
                                     wait_dv=dvp, evac_eng='vector',
                                     dst_batch=(bigA, lambda i0, qi=qi: 8192 + i0 * 512 + qi * 128, 512))
                    l0_last_trpe[0] = env.c['PE']
                acpt = env.c['AC']

                ps = psB if qg % 2 == 0 else psC
                bp = pv_ac[qg % 2]
                @B.tensor
                def _(t, qg=qg, h=h, base=base, bp=bp, ps=ps):
                    t.wait_ge(env.s['AC'], acpt)
                    if bp:
                        t.wait_ge(env.s['AC'], bp)
                    for kt in range(16):
                        env.inc(t.matmul(ps[:], bigA[:, base + 4096 + kt * 128: base + 4096 + (kt + 1) * 128],
                                         pTs[:, kt * 512:(kt + 1) * 512],
                                         start=(kt == 0), stop=(kt == 15)), 'PE', 1)
                pepv = env.c['PE']

                eb = evb if qg % 2 == 0 else evb2
                eblg = eb_lg[qg % 2]
                @B.vector
                def _(sc, pepv=pepv, eblg=eblg, eb=eb, ps=ps):
                    sc.wait_ge(env.s['PE'], pepv)
                    if eblg:
                        sc.wait_ge(env.s['LG'], eblg)
                    env.inc(sc.tensor_copy(eb[:, 0:512], ps[:]), 'AC', 1)
                acao = env.c['AC']
                pv_ac[qg % 2] = acao
                @B.gpsimd
                def _(gp, h=h, qg=qg, eb=eb, acao=acao):
                    gp.wait_ge(env.s['AC'], acao)
                    env.inc(gp.dma_start(out=d["sp_ao"][h * 128:(h + 1) * 128, qg * 512:(qg + 1) * 512],
                                         in_=eb[:, 0:512]), 'LG', 16)
                eb_lg[qg % 2] = env.c['LG']
            qkv_pe_mark[h % 2] = env.c['PE']

        env.barrier()

        for ob in range(16):
            buf = ob % 2
            ws = wstage[:, buf * 2048:(buf + 1) * 2048]
            @B.sync
            def _(sy, ob=ob):
                sy.wait_ge(env.s['DV'], env.c['DV'])
                env.inc(sy.dma_start(out=ws, in_=d["l0wo"][ob * 128:(ob + 1) * 128, :]), 'LD', 16)
            ldw = env.c['LD']
            @B.vector
            def _(v, ob=ob):
                v.wait_ge(env.s['LD'], ldw)
                v.wait_ge(env.s['PE'], env.c['PE'])
                env.inc(v.tensor_copy(stgb[:], ws), 'DV', 1)
            dvc = env.c['DV']
            transpose_blocks(lambda i: stgb[:, i * 128:(i + 1) * 128], IC, None,
                             wait_dv=dvc,
                             dst_batch=(bigA, lambda i0, ob=ob: i0 * 2048 + ob * 128, 2048))
        acwo = env.c['AC']

        for tt in range(TT):
            aosl = stgb
            @B.sync
            def _(sy, tt=tt):
                sy.wait_ge(env.s['PE'], env.c['PE'])
                sy.wait_ge(env.s['LG'], env.c['LG'])
                env.inc(sy.dma_start(
                    out=aosl.rearrange("p (a c) -> p a c", a=16),
                    in_=d["sp_ao"].rearrange("(a p) t -> p a t", p=128)[:, :, tt * 128:(tt + 1) * 128]), 'LD', 16)
            ldao = env.c['LD']
            for oc4 in range(4):
                ps = psB if oc4 % 2 == 0 else psC
                @B.tensor
                def _(t, tt=tt, oc4=oc4):
                    t.wait_ge(env.s['LD'], ldao)
                    t.wait_ge(env.s['AC'], acwo)
                    if oc4 >= 2 or tt > 0:
                        t.wait_ge(env.s['AC'], env.c['AC'])
                    for hh in range(16):
                        env.inc(t.matmul(ps[:], aosl[:, hh * 128:(hh + 1) * 128],
                                         bigA[:, hh * 2048 + oc4 * 512: hh * 2048 + (oc4 + 1) * 512],
                                         start=(hh == 0), stop=(hh == 15)), 'PE', 1)
                pemm = env.c['PE']
                eb = evb if oc4 % 2 == 0 else evb2
                @B.scalar
                def _(sc):
                    sc.wait_ge(env.s['PE'], pemm)
                    sc.wait_ge(env.s['LG'], env.c['LG'])
                    env.inc(sc.copy(eb[:, 0:512], ps[:]), 'AC', 1)
                acs2 = env.c['AC']
                @B.gpsimd
                def _(gp, tt=tt, oc4=oc4, eb=eb):
                    gp.wait_ge(env.s['AC'], acs2)
                    env.inc(gp.dma_start(out=d["sp_s"][tt * 128:(tt + 1) * 128, oc4 * 512:(oc4 + 1) * 512],
                                         in_=eb[:, 0:512]), 'LG', 16)

        env.barrier()
        if _DEBUG_SPILL[0]:
            @B.gpsimd
            def _(gp):
                env.inc(gp.dma_start(out=dbg_q, in_=d["sp_q"]), 'LG', 16)
                env.inc(gp.dma_start(out=dbg_ao, in_=d["sp_ao"]), 'LG', 16)
                env.inc(gp.dma_start(out=dbg_s, in_=d["sp_s"]), 'LG', 16)
                gp.wait_ge(env.s['LG'], env.c['LG'])
            env.barrier()

        @B.vector
        def _(v):
            env.inc(v.memset(tmaxs[:], 0.0), 'DV', 1)

        ffn_ws_dv = [0, 0]
        ffn_stg_pe = [0, 0]
        ffn_strip_ac = {}
        ffn_psum_dv = [0, 0]

        def ffn_prep_piece(jc, piece):
            gi, rt = divmod(piece, 4)
            w_ap = d["wg"] if gi == 0 else d["wu"]
            buf = jc % 2
            strip0 = (gi * 2 + buf) * 8192
            par = piece % 2
            ws = wstage[:, par * 2048:(par + 1) * 2048]
            sg = stgb if par == 0 else stgb2
            f8h = f8s[:, 0:2048] if par == 0 else f8s[:, 2048:4096]
            wsdv = ffn_ws_dv[par]
            @B.gpsimd
            def _(gp, jc=jc, rt=rt, w_ap=w_ap, wsdv=wsdv):
                gp.wait_ge(env.s['DV'], wsdv)
                env.inc(gp.indirect_dma_start(
                    ws, None, w_ap,
                    IndirectOffsetOnAxis(idxs[:, jc * 4 + rt: jc * 4 + rt + 1], 0)), 'LG', 16)
            ldw = env.c['LG']
            wmx = stats[:, 44 + par:45 + par]; s1 = stats[:, 46 + par:47 + par]
            stpe = ffn_stg_pe[par]
            @B.vector
            def _(v, ldw=ldw, stpe=stpe, sg=sg, f8h=f8h, wmx=wmx, s1=s1):
                v.wait_ge(env.s['LG'], ldw)
                v.wait_ge(env.s['PE'], stpe)
                v.reduce_max(wmx, ws, axis=AX.X, apply_absolute_value=True)
                v.drain()
                v.reciprocal(s1, wmx)
                v.drain()
                v.tensor_scalar_mul(s1, s1, 224.0)
                v.drain()
                v.tensor_scalar_mul(f8h, ws, s1)
                v.drain()
                v.tensor_scalar_mul(s1, wmx, 1.0 / 224.0)
                v.drain()
                env.inc(v.tensor_scalar_mul(sg[:], f8h, s1), 'DV', 1)
            ffn_ws_dv[par] = env.c['DV']
            dvq = env.c['DV']
            transpose_blocks(lambda i, sg=sg: sg[:, i * 128:(i + 1) * 128], IC, None,
                             wait_dv=dvq, evac_eng='vector',
                             dst_batch=(bigA, lambda i0, strip0=strip0, rt=rt: strip0 + i0 * 512 + rt * 128, 512))
            ffn_stg_pe[par] = env.c['PE']
            if piece == 7:
                ffn_strip_ac[jc] = env.c['AC']

        for piece in range(8):
            ffn_prep_piece(0, piece)
        for jc in range(11):
            buf = jc % 2
            acst = ffn_strip_ac[jc]

            for tt in range(TT):
                if tt % 2 == 0 and jc + 1 < 11:
                    ffn_prep_piece(jc + 1, tt // 2)
                pg = psA[:, (tt % 2) * 1024:(tt % 2) * 1024 + 512]
                pu = psA[:, (tt % 2) * 1024 + 512:(tt % 2) * 1024 + 1024]
                bp = ffn_psum_dv[tt % 2]
                @B.tensor
                def _(t, tt=tt, jc=jc, buf=buf, bp=bp):
                    t.wait_ge(env.s['AC'], acst)
                    if bp:
                        t.wait_ge(env.s['DV'], bp)
                    for i in range(IC):
                        t.matmul(pg, bigC[:, i * 2048 + tt * 128: i * 2048 + tt * 128 + 128],
                                 bigA[:, (0 * 2 + buf) * 8192 + i * 512:(0 * 2 + buf) * 8192 + (i + 1) * 512],
                                 start=(i == 0), stop=(i == IC - 1))
                        env.inc(t.matmul(pu, bigC[:, i * 2048 + tt * 128: i * 2048 + tt * 128 + 128],
                                         bigA[:, (1 * 2 + buf) * 8192 + i * 512:(1 * 2 + buf) * 8192 + (i + 1) * 512],
                                         start=(i == 0), stop=(i == IC - 1)), 'PE', 1)
                pemm = env.c['PE']
                gsb = sg0 if tt % 2 == 0 else sg1
                @B.scalar
                def _(sc, pg=pg):
                    sc.wait_ge(env.s['PE'], pemm)
                    env.inc(sc.activation(gsb[:], pg, AF.Silu, bias=0.0, scale=1.0), 'AC', 1)
                acsl = env.c['AC']
                tch = stats[:, 48:49]
                @B.vector
                def _(v, tt=tt, jc=jc, pu=pu, gsb=gsb):
                    v.wait_ge(env.s['AC'], acsl)
                    if tt >= 2 or jc > 0:
                        v.wait_ge(env.s['LG'], env.c['LG'])
                    v.tensor_tensor(gsb[:], gsb[:], pu, op=ALU.mult)
                    v.drain()
                    v.reduce_max(tch, gsb[:], axis=AX.X, apply_absolute_value=True)
                    v.drain()
                    env.inc(v.tensor_tensor(tmaxs[:, tt:tt + 1], tmaxs[:, tt:tt + 1], tch, op=ALU.max), 'DV', 1)
                    v.drain()
                dvt = env.c['DV']
                ffn_psum_dv[tt % 2] = dvt
                @B.gpsimd
                def _(gp, tt=tt, jc=jc, gsb=gsb):
                    gp.wait_ge(env.s['DV'], dvt)
                    env.inc(gp.dma_start(out=d["sp_T"][tt * 128:(tt + 1) * 128, jc * 512:(jc + 1) * 512],
                                         in_=gsb[:]), 'LG', 16)

        env.barrier()

        for tt in range(TT):
            Tw = wstage[:, 0:5632]
            @B.sync
            def _(sy, tt=tt):
                if tt >= 1:
                    sy.wait_ge(env.s['DV'], env.c['DV'])
                sy.wait_ge(env.s['LG'], env.c['LG'])
                env.inc(sy.dma_start(out=Tw, in_=d["sp_T"][tt * 128:(tt + 1) * 128, :]), 'LD', 16)
            ldt = env.c['LD']
            q1 = stats[:, 49:50]; q2 = stats[:, 50:51]
            @B.vector
            def _(v, tt=tt):
                v.wait_ge(env.s['LD'], ldt)
                if tt >= 1:
                    v.wait_ge(env.s['LG'], env.c['LG'])
                v.reciprocal(q1, tmaxs[:, tt:tt + 1])
                v.drain()
                v.tensor_scalar_mul(q1, q1, 224.0)
                v.drain()
                v.tensor_scalar_mul(q2, tmaxs[:, tt:tt + 1], 1.0 / 224.0)
                v.drain()
                v.tensor_scalar_mul(f8s[:], Tw, q1)
                v.drain()
                env.inc(v.tensor_scalar_mul(stg5[:], f8s[:], q2), 'DV', 1)
            dvq = env.c['DV']
            transpose_blocks(lambda i: stg5[:, i * 128:(i + 1) * 128], NJT, None,
                             wait_dv=dvq,
                             dst_batch=(bigA, lambda i0: i0 * 128, 128))
            acq = env.c['AC']
            @B.gpsimd
            def _(gp, tt=tt):
                gp.wait_ge(env.s['AC'], acq)
                env.inc(gp.dma_start(
                    out=d["sp_qtT"].rearrange("(a p) t -> p a t", p=128)[:, :, tt * 128:(tt + 1) * 128],
                    in_=bigA[:, 0:5632].rearrange("p (a c) -> p a c", a=NJT)), 'LG', 16)

        env.barrier()

        wd_ws_dv = [0, 0]
        for hp in range(2):
            for rt in range(8):
                par = rt % 2
                ws = wstage[:, 0:5632] if par == 0 else bigC[:, 11264:22528].bitcast(F32)
                wsdv = wd_ws_dv[par]
                @B.sync
                def _(sy, hp=hp, rt=rt, ws=ws, wsdv=wsdv):
                    sy.wait_ge(env.s['DV'], wsdv)
                    if rt == 0:
                        sy.wait_ge(env.s['PE'], env.c['PE'])
                        sy.wait_ge(env.s['LG'], env.c['LG'])
                    env.inc(sy.dma_start(out=ws, in_=d["wd"][(hp * 8 + rt) * 128:(hp * 8 + rt + 1) * 128, :]), 'LD', 16)
                ldw = env.c['LD']
                wmx = stats[:, 51:52]; s1 = stats[:, 52:53]
                @B.vector
                def _(v, ldw=ldw, ws=ws):
                    v.wait_ge(env.s['LD'], ldw)
                    v.wait_ge(env.s['PE'], env.c['PE'])
                    v.reduce_max(wmx, ws, axis=AX.X, apply_absolute_value=True)
                    v.drain()
                    v.reciprocal(s1, wmx)
                    v.drain()
                    v.tensor_scalar_mul(s1, s1, 224.0)
                    v.drain()
                    v.tensor_scalar_mul(f8s[:], ws, s1)
                    v.drain()
                    v.tensor_scalar_mul(s1, wmx, 1.0 / 224.0)
                    v.drain()
                    env.inc(v.tensor_scalar_mul(stg5[:], f8s[:], s1), 'DV', 1)
                wd_ws_dv[par] = env.c['DV']
                dvq = env.c['DV']
                transpose_blocks(lambda i: stg5[:, i * 128:(i + 1) * 128], 32, None,
                                 wait_dv=dvq, evac_eng='vector',
                                 dst_batch=(bigA, lambda i0, rt=rt: i0 * 1024 + rt * 128, 1024))
                transpose_blocks(lambda i: stg5[:, (32 + i) * 128:(33 + i) * 128], 12, None,
                                 evac_eng='vector',
                                 dst_batch=(wtrans, lambda i0, rt=rt: i0 * 1024 + rt * 128, 1024))
            acst = env.c['AC']
            p7_bp = [0, 0]
            for tt in range(TT):
                @B.sync
                def _(sy, hp=hp, tt=tt):
                    sy.wait_ge(env.s['PE'], env.c['PE'])
                    sy.wait_ge(env.s['LG'], env.c['LG'])
                    env.inc(sy.dma_start(
                        out=stg5[:].rearrange("p (a c) -> p a c", a=NJT),
                        in_=d["sp_qtT"].rearrange("(a p) t -> p a t", p=128)[:, :, tt * 128:(tt + 1) * 128]), 'LD', 16)
                    env.inc(sy.dma_start(out=sx0[:], in_=x0_ap[tt * 128:(tt + 1) * 128, hp * 1024:(hp + 1) * 1024]), 'LD', 16)
                    env.inc(sy.dma_start(out=ssb[:], in_=d["sp_s"][tt * 128:(tt + 1) * 128, hp * 1024:(hp + 1) * 1024]), 'LD', 16)
                ldq = env.c['LD']
                if tt % 2 == 0:
                    pa, pb = psB[:], psC[:]
                else:
                    pa, pb = psA[:, 0:512], psA[:, 512:1024]
                bp = p7_bp[tt % 2]
                @B.tensor
                def _(t, hp=hp, tt=tt, pa=pa, pb=pb, bp=bp):
                    t.wait_ge(env.s['LD'], ldq)
                    t.wait_ge(env.s['AC'], acst)
                    if bp:
                        t.wait_ge(env.s['AC'], bp)
                    for i in range(NJT):
                        base = bigA[:, i * 1024:(i + 1) * 1024] if i < 32 else wtrans[:, (i - 32) * 1024:(i - 31) * 1024]
                        t.matmul(pa, stg5[:, i * 128:(i + 1) * 128], base[:, 0:512],
                                 start=(i == 0), stop=(i == NJT - 1))
                        env.inc(t.matmul(pb, stg5[:, i * 128:(i + 1) * 128], base[:, 512:1024],
                                         start=(i == 0), stop=(i == NJT - 1)), 'PE', 1)
                pemm = env.c['PE']
                @B.scalar
                def _(sc, pa=pa, pb=pb):
                    sc.wait_ge(env.s['PE'], pemm)
                    sc.activation(sg0[:], pa, AF.Silu, bias=0.0, scale=1.0)
                    env.inc(sc.activation(sg1[:], pb, AF.Silu, bias=0.0, scale=1.0), 'AC', 1)
                acg = env.c['AC']
                p7_bp[tt % 2] = acg
                @B.vector
                def _(v, hp=hp, tt=tt, acg=acg, ldq=ldq):
                    v.wait_ge(env.s['AC'], acg)
                    v.wait_ge(env.s['LD'], ldq)
                    v.wait_ge(env.s['LG'], env.c['LG'])
                    v.tensor_tensor(sg0[:], sg0[:], ssb[:, 0:512], op=ALU.mult)
                    v.tensor_tensor(sg1[:], sg1[:], ssb[:, 512:1024], op=ALU.mult)
                    v.drain()
                    v.tensor_tensor(sx0[:, 0:512], sx0[:, 0:512], sg0[:], op=ALU.add)
                    env.inc(v.tensor_tensor(sx0[:, 512:1024], sx0[:, 512:1024], sg1[:], op=ALU.add), 'DV', 1)
                    v.drain()
                dvx = env.c['DV']
                @B.gpsimd
                def _(gp, hp=hp, tt=tt, dvx=dvx):
                    gp.wait_ge(env.s['DV'], dvx)
                    env.inc(gp.dma_start(out=d["sp_x1"][tt * 128:(tt + 1) * 128, hp * 1024:(hp + 1) * 1024],
                                         in_=sx0[:]), 'LG', 16)

        env.barrier()

        @B.sync
        def _(sy):
            env.inc(sy.dma_start(out=wstage[0:8, 0:2048], in_=d["wr"]), 'LD', 16)
        ldwr = env.c['LD']
        @B.vector
        def _(v):
            v.wait_ge(env.s['LD'], ldwr)
            v.tensor_copy(stgb[0:8, 0:2048], wstage[0:8, 0:2048])
            v.drain()
            env.inc(v.tensor_tensor(stg5[0:8, 0:2048], wstage[0:8, 0:2048], stgb[0:8, 0:2048],
                                    op=ALU.subtract), 'DV', 1)
            v.drain()
        dvwr = env.c['DV']
        @B.tensor
        def _(t):
            t.wait_ge(env.s['DV'], dvwr)
            t.wait_ge(env.s['AC'], env.marks['pt_free'])
            for i in range(IC):
                pt = ptA if i < 8 else ptB
                t.transpose(pt[:, (i % 8) * 128:(i % 8) * 128 + 8],
                            stgb[0:8, i * 128:(i + 1) * 128], ident[0:8, 0:8])
                env.inc(t.transpose(pt[:, (i % 8) * 128 + 8:(i % 8) * 128 + 16],
                                    stg5[0:8, i * 128:(i + 1) * 128], ident[0:8, 0:8]), 'PE', 1)
        pewr = env.c['PE']
        @B.scalar
        def _(sc):
            sc.wait_ge(env.s['PE'], pewr)
            for i in range(IC):
                pt = ptA if i < 8 else ptB
                env.inc(sc.copy(wr2t[:, i * 16:(i + 1) * 16],
                                pt[:, (i % 8) * 128:(i % 8) * 128 + 16]), 'AC', 1)
        env.marks['pt_free'] = env.c['AC']

        env.barrier()
        ln_quant_phase(d["sp_x1"], 1)
        env.barrier()

        cosfull = bigC[:, 16384:20480].bitcast(F32)
        sinfull = bigC[:, 20480:24576].bitcast(F32)
        @B.sync
        def _(sy):
            env.inc(sy.dma_start(out=cosfull.rearrange("p (a c) -> p a c", a=TT),
                                 in_=d["cosT"].rearrange("(a p) c -> p a c", p=128)), 'LD', 16)
            env.inc(sy.dma_start(out=sinfull.rearrange("p (a c) -> p a c", a=TT),
                                 in_=d["sinF"].rearrange("(a p) c -> p a c", p=128)), 'LD', 16)
        ldcs = env.c['LD']

        def rope_apply(v, ps, eb, t1, tt):
            cos4 = cosfull[:, tt * 128:(tt + 1) * 128].rearrange("p (o c) -> p o c", o=1).broadcast_to([128, 4, 128])
            v.tensor_tensor(t1[:].rearrange("p (h c) -> p h c", h=4), ps[:].rearrange("p (h c) -> p h c", h=4),
                            cos4, op=ALU.mult)
            pr = ps[:].rearrange("p (h t c) -> p h t c", h=4, t=2, c=64)
            ebr = eb[:, 0:512].rearrange("p (h t c) -> p h t c", h=4, t=2, c=64)
            sin4 = sinfull[:, tt * 128:(tt + 1) * 128].rearrange("p (o t c) -> p o t c", o=1, t=2)\
                .broadcast_to([128, 4, 2, 64])
            v.tensor_tensor(ebr[:, :, 0, :], pr[:, :, 1, :], sin4[:, :, 0, :], op=ALU.mult)
            v.tensor_tensor(ebr[:, :, 1, :], pr[:, :, 0, :], sin4[:, :, 1, :], op=ALU.mult)
            v.drain()
            inst = v.tensor_tensor(eb[:, 0:512], t1[:], eb[:, 0:512], op=ALU.add)
            v.drain()
            return inst

        for oc4 in range(4):
            buf = oc4 % 2
            for i in range(IC):
                ws = wstage[:, (i % 2) * 2048:(i % 2) * 2048 + 512]
                @B.sync
                def _(sy, i=i, oc4=oc4):
                    sy.wait_ge(env.s['DV'], env.c['DV'])
                    env.inc(sy.dma_start(out=ws, in_=d["l1wq"][i * 128:(i + 1) * 128, oc4 * 512:(oc4 + 1) * 512]), 'LD', 16)
                ldw = env.c['LD']
                @B.vector
                def _(v, i=i, buf=buf, oc4=oc4):
                    v.wait_ge(env.s['LD'], ldw)
                    if oc4 >= 2:
                        v.wait_ge(env.s['PE'], env.c['PE'])
                    env.inc(v.tensor_copy(wtrans[:, buf * 8192 + i * 512: buf * 8192 + (i + 1) * 512], ws), 'DV', 1)
            dvst = env.c['DV']
            for tt in range(TT):
                ps = psB if tt % 2 == 0 else psC
                @B.tensor
                def _(t, tt=tt, oc4=oc4, buf=buf):
                    t.wait_ge(env.s['DV'], dvst)
                    if tt >= 2 or oc4 > 0:
                        t.wait_ge(env.s['DV'], env.c['DV'])
                    for i in range(IC):
                        env.inc(t.matmul(ps[:], bigA[:, i * 2048 + tt * 128: i * 2048 + tt * 128 + 128],
                                         wtrans[:, buf * 8192 + i * 512: buf * 8192 + (i + 1) * 512],
                                         start=(i == 0), stop=(i == IC - 1)), 'PE', 1)
                pemm = env.c['PE']
                eb = evb if tt % 2 == 0 else evb2
                t1 = sg0 if tt % 2 == 0 else sg1
                @B.vector
                def _(v, tt=tt, ps=ps, eb=eb, t1=t1):
                    v.wait_ge(env.s['PE'], pemm)
                    v.wait_ge(env.s['LD'], ldcs)
                    v.wait_ge(env.s['LG'], env.c['LG'])
                    env.inc(rope_apply(v, ps, eb, t1, tt), 'DV', 1)
                dvr = env.c['DV']
                @B.gpsimd
                def _(gp, tt=tt, oc4=oc4, eb=eb):
                    gp.wait_ge(env.s['DV'], dvr)
                    env.inc(gp.dma_start(out=d["sp_q"][tt * 128:(tt + 1) * 128, oc4 * 512:(oc4 + 1) * 512],
                                         in_=eb[:, 0:512]), 'LG', 16)

        for which, w_ap in (("k", d["l1wk"]), ("v", d["l1wv"])):
            for i in range(IC):
                ws = wstage[:, (i % 2) * 2048:(i % 2) * 2048 + 512]
                @B.sync
                def _(sy, i=i, w_ap=w_ap):
                    sy.wait_ge(env.s['DV'], env.c['DV'])
                    env.inc(sy.dma_start(out=ws, in_=w_ap[i * 128:(i + 1) * 128, :]), 'LD', 16)
                ldw = env.c['LD']
                @B.vector
                def _(v, i=i):
                    v.wait_ge(env.s['LD'], ldw)
                    v.wait_ge(env.s['PE'], env.c['PE'])
                    env.inc(v.tensor_copy(wtrans[:, i * 512:(i + 1) * 512], ws), 'DV', 1)
            dvst = env.c['DV']
            for tt in range(TT):
                ps = psB if tt % 2 == 0 else psC
                @B.tensor
                def _(t, tt=tt):
                    t.wait_ge(env.s['DV'], dvst)
                    if tt >= 2:
                        t.wait_ge(env.s['DV'], env.c['DV'])
                        t.wait_ge(env.s['AC'], env.c['AC'])
                    for i in range(IC):
                        env.inc(t.matmul(ps[:], bigA[:, i * 2048 + tt * 128: i * 2048 + tt * 128 + 128],
                                         wtrans[:, i * 512:(i + 1) * 512],
                                         start=(i == 0), stop=(i == IC - 1)), 'PE', 1)
                pemm = env.c['PE']
                if which == "v":
                    @B.scalar
                    def _(sc, tt=tt, ps=ps):
                        sc.wait_ge(env.s['PE'], pemm)
                        env.inc(sc.copy(bigC[:, 8192 + tt * 512: 8192 + (tt + 1) * 512], ps[:]), 'AC', 1)
                else:
                    eb = evb if tt % 2 == 0 else evb2
                    t1 = sg0 if tt % 2 == 0 else sg1
                    @B.vector
                    def _(v, tt=tt, ps=ps, eb=eb, t1=t1):
                        v.wait_ge(env.s['PE'], pemm)
                        v.wait_ge(env.s['LD'], ldcs)
                        v.wait_ge(env.s['AC'], env.c['AC'])
                        env.inc(rope_apply(v, ps, eb, t1, tt), 'DV', 1)
                    dvr = env.c['DV']
                    @B.tensor
                    def _(t, tt=tt, eb=eb):
                        t.wait_ge(env.s['DV'], dvr)
                        t.wait_ge(env.s['AC'], env.marks['pt_free'])
                        for kv in range(4):
                            pt = ptA if kv < 2 else ptB
                            env.inc(t.transpose(pt[:, (kv % 2) * 128:(kv % 2 + 1) * 128],
                                                eb[:, kv * 128:(kv + 1) * 128], ident[:]), 'PE', 1)
                    pek = env.c['PE']
                    @B.scalar
                    def _(sc, tt=tt):
                        sc.wait_ge(env.s['PE'], pek)
                        for kv in range(4):
                            pt = ptA if kv < 2 else ptB
                            env.inc(sc.copy(bigC[:, kv * 2048 + tt * 128: kv * 2048 + tt * 128 + 128],
                                            pt[:, (kv % 2) * 128:(kv % 2 + 1) * 128]), 'AC', 1)
                    env.marks['pt_free'] = env.c['AC']

        env.barrier()

        ptl1 = bigC[:, 28672:29440]
        ptl2 = bigC[:, 29440:30208]
        rT = bigC[:, 30208:30976]
        qfm_base = [24576, 26624]
        qfm_pe_mark = [0, 0]
        qfm_ld = {}
        qfm_ac = {}
        l1_pv_ac = [0, 0]
        l1_eb_lg = [0, 0]
        l1_last_dvmk = [0]
        l1_last_acex = [0]
        l1_last_trpe = [0]
        l1_pe_sc = {}

        def l1_q_load(h):
            par = h % 2
            base = qfm_base[par]
            mark = qfm_pe_mark[par]
            @B.sync
            def _(sy, h=h, base=base, mark=mark):
                sy.wait_ge(env.s['PE'], mark)
                env.inc(sy.dma_start(
                    out=bigC[:, base:base + 2048].rearrange("p (a c) -> p a c", a=TT),
                    in_=d["sp_q"].rearrange("(a p) o -> p a o", p=128)[:, :, h * 128:(h + 1) * 128]), 'LD', 16)
            qfm_ld[h] = env.c['LD']

        def l1_q_transpose(h):
            par = h % 2
            base = qfm_base[par]
            ldq = qfm_ld[h]
            def wx(t, ldq=ldq):
                t.wait_ge(env.s['LD'], ldq)
            transpose_blocks(lambda i, base=base: bigC[:, base + i * 128: base + (i + 1) * 128], TT, None,
                             wait_extra=wx, evac_eng='vector',
                             dst_batch=(bigC, lambda i0, base=base: base + i0 * 128, 128))
            qfm_ac[h] = env.c['AC']

        def l1_scores(h, qt):
            kv = h // 4
            base = qfm_base[h % 2]
            j0 = max(0, qt - 4)
            nkt = qt - j0 + 1
            dvm0, acex0 = l1_last_dvmk[0], l1_last_acex[0]
            acq = qfm_ac[h]
            @B.tensor
            def _(t, h=h, qt=qt, base=base, j0=j0, nkt=nkt, kv=kv, dvm0=dvm0, acex0=acex0, acq=acq):
                t.wait_ge(env.s['AC'], acq)
                if dvm0:
                    t.wait_ge(env.s['DV'], dvm0)
                if acex0:
                    t.wait_ge(env.s['AC'], acex0)
                for half, bb in ((0, 0), (1, 1024)):
                    lhs = bigC[half * 64:half * 64 + 64, base + qt * 128: base + (qt + 1) * 128]
                    done = 0
                    while done < nkt:
                        take = min(4, nkt - done)
                        rhs = bigC[half * 64:half * 64 + 64,
                                   kv * 2048 + (j0 + done) * 128: kv * 2048 + (j0 + done + take) * 128]
                        env.inc(t.matmul(psA[:, bb + done * 128: bb + (done + take) * 128],
                                         lhs, rhs, start=True, stop=True,
                                         tile_position=(half * 64, 0)), 'PE', 1)
                        done += take
            return env.c['PE']

        l1_q_load(0)
        l1_q_transpose(0)
        l1_q_load(1)
        l1_pe_sc[(0, 0)] = l1_scores(0, 0)
        for h in range(NH):
            kv = h // 4
            lamh = lamb[:, h:h + 1]
            for qt in range(TT):
                j0 = max(0, qt - 4)
                nkt = qt - j0 + 1
                w = nkt * 128
                if qt == 4 and h + 1 < NH:
                    l1_q_transpose(h + 1)
                pes = l1_pe_sc.pop((h, qt))
                m1 = stats[:, 54:55]; m2 = stats[:, 55:56]
                @B.vector
                def _(v, qt=qt, nkt=nkt, w=w, pes=pes):
                    v.wait_ge(env.s['PE'], pes)
                    for bb in (0, 1024):
                        v.tensor_tensor(psA[:, bb + (nkt - 1) * 128: bb + nkt * 128],
                                        psA[:, bb + (nkt - 1) * 128: bb + nkt * 128],
                                        maskc[:], op=ALU.add)
                        if qt >= 4:
                            v.tensor_tensor(psA[:, bb:bb + 128],
                                            psA[:, bb:bb + 128],
                                            maska[:], op=ALU.add)
                    v.drain()
                    v.reduce_max(m1, psA[:, 0:w], axis=AX.X)
                    v.reduce_max(m2, psA[:, 1024:1024 + w], axis=AX.X)
                    v.drain()
                    v.tensor_scalar_mul(stats[:, 56:57], m1, -0.125)
                    env.inc(v.tensor_scalar_mul(stats[:, 57:58], m2, -0.125), 'DV', 1)
                    v.drain()
                dvmk = env.c['DV']
                l1_last_dvmk[0] = dvmk
                z1 = stats[:, 58:59]; z2 = stats[:, 59:60]
                trpe = l1_last_trpe[0]
                @B.scalar
                def _(sc, w=w, dvmk=dvmk, trpe=trpe):
                    sc.wait_ge(env.s['DV'], dvmk)
                    if trpe:
                        sc.wait_ge(env.s['PE'], trpe)
                    sc.activation(ptl1[:, 0:w], psA[:, 0:w], AF.Exp,
                                  bias=stats[:, 56:57], scale=0.125, accum_out=z1)
                    env.inc(sc.activation(ptl2[:, 0:w], psA[:, 1024:1024 + w], AF.Exp,
                                          bias=stats[:, 57:58], scale=0.125, accum_out=z2), 'AC', 1)
                acex = env.c['AC']
                l1_last_acex[0] = acex

                if qt + 1 < TT:
                    l1_pe_sc[(h, qt + 1)] = l1_scores(h, qt + 1)
                elif h + 1 < NH:
                    l1_pe_sc[(h + 1, 0)] = l1_scores(h + 1, 0)

                @B.vector
                def _(v, w=w, lamh=lamh, acex=acex):
                    v.wait_ge(env.s['AC'], acex)
                    zi1 = stats[:, 60:61]; zi2 = stats[:, 61:62]
                    v.reciprocal(zi1, z1)
                    v.drain()
                    v.reciprocal(zi2, z2)
                    v.drain()
                    v.tensor_scalar_mul(ptl1[:, 0:w], ptl1[:, 0:w], zi1)
                    v.tensor_mul(zi2, zi2, lamh)
                    v.drain()
                    v.tensor_scalar_mul(zi2, zi2, -1.0)
                    v.drain()
                    v.scalar_tensor_tensor(ptl1[:, 0:w], ptl2[:, 0:w], zi2, ptl1[:, 0:w],
                                           op0=ALU.mult, op1=ALU.add)
                    v.drain()
                    ds = stats[:, 62:63]
                    v.tensor_scalar(ptl1[:, 0:w], ptl1[:, 0:w], 0.0, 0.0,
                                    op0=ALU.max, op1=ALU.add, accum_out=ds)
                    v.drain()
                    v.tensor_scalar_add(ds, ds, 1e-6)
                    v.drain()
                    ri = stats[:, 63:64]
                    v.reciprocal(ri, ds)
                    v.drain()
                    env.inc(v.tensor_scalar_mul(ptl1[:, 0:w], ptl1[:, 0:w], ri), 'DV', 1)
                    v.drain()
                dvr = env.c['DV']
                @B.tensor
                def _(t, nkt=nkt, dvr=dvr):
                    t.wait_ge(env.s['DV'], dvr)
                    t.wait_ge(env.s['AC'], env.marks['pt_free'])
                    for j in range(nkt):
                        pt = ptA if j < 4 else ptB
                        env.inc(t.transpose(pt[:, (j % 4) * 128:(j % 4 + 1) * 128],
                                            ptl1[:, j * 128:(j + 1) * 128], ident[:]), 'PE', 1)
                per = env.c['PE']
                l1_last_trpe[0] = per
                @B.vector
                def _(sc, nkt=nkt, per=per):
                    sc.wait_ge(env.s['PE'], per)
                    for j in range(nkt):
                        pt = ptA if j < 4 else ptB
                        env.inc(sc.tensor_copy(rT[:, j * 128:(j + 1) * 128],
                                               pt[:, (j % 4) * 128:(j % 4 + 1) * 128]), 'AC', 1)
                env.marks['pt_free'] = env.c['AC']
                acr = env.c['AC']
                ps = psB if qt % 2 == 0 else psC
                bp = l1_pv_ac[qt % 2]
                @B.tensor
                def _(t, qt=qt, h=h, j0=j0, nkt=nkt, kv=kv, ps=ps, acr=acr, bp=bp):
                    t.wait_ge(env.s['AC'], acr)
                    if bp:
                        t.wait_ge(env.s['AC'], bp)
                    for j in range(nkt):
                        kt = j0 + j
                        env.inc(t.matmul(ps[:, 0:128],
                                         bigC[:, 8192 + kt * 512 + kv * 128: 8192 + kt * 512 + (kv + 1) * 128],
                                         rT[:, j * 128:(j + 1) * 128],
                                         start=(j == 0), stop=(j == nkt - 1)), 'PE', 1)
                pepv = env.c['PE']
                eb = evb if qt % 2 == 0 else evb2
                eblg = l1_eb_lg[qt % 2]
                @B.vector
                def _(sc, ps=ps, eb=eb, pepv=pepv, eblg=eblg):
                    sc.wait_ge(env.s['PE'], pepv)
                    if eblg:
                        sc.wait_ge(env.s['LG'], eblg)
                    env.inc(sc.tensor_copy(eb[:, 0:128], ps[:, 0:128]), 'AC', 1)
                acact = env.c['AC']
                l1_pv_ac[qt % 2] = acact
                @B.gpsimd
                def _(gp, h=h, qt=qt, eb=eb, acact=acact):
                    gp.wait_ge(env.s['AC'], acact)
                    env.inc(gp.dma_start(out=d["sp_ao"][h * 128:(h + 1) * 128, qt * 128:(qt + 1) * 128],
                                         in_=eb[:, 0:128]), 'LG', 16)
                l1_eb_lg[qt % 2] = env.c['LG']
            qfm_pe_mark[h % 2] = env.c['PE']
            if h + 2 < NH:
                l1_q_load(h + 2)

        env.barrier()

        for i in range(IC):
            ws = wstage[:, (i % 2) * 2048:(i % 2 + 1) * 2048]
            @B.sync
            def _(sy, i=i):
                if i >= 2:
                    sy.wait_ge(env.s['DV'], env.c['DV'])
                env.inc(sy.dma_start(out=ws, in_=d["l1wo"][i * 128:(i + 1) * 128, :]), 'LD', 16)
            ldw = env.c['LD']
            @B.vector
            def _(v, i=i):
                v.wait_ge(env.s['LD'], ldw)
                env.inc(v.tensor_copy(bigC[:, i * 2048:(i + 1) * 2048], ws), 'DV', 1)
        dvwo = env.c['DV']
        for tt in range(TT):
            aosl = stgb
            @B.sync
            def _(sy, tt=tt):
                sy.wait_ge(env.s['PE'], env.c['PE'])
                sy.wait_ge(env.s['LG'], env.c['LG'])
                env.inc(sy.dma_start(
                    out=aosl.rearrange("p (a c) -> p a c", a=16),
                    in_=d["sp_ao"].rearrange("(a p) t -> p a t", p=128)[:, :, tt * 128:(tt + 1) * 128]), 'LD', 16)
            ldao = env.c['LD']
            for oc4 in range(4):
                ps = psB if oc4 % 2 == 0 else psC
                @B.tensor
                def _(t, tt=tt, oc4=oc4, ps=ps):
                    t.wait_ge(env.s['LD'], ldao)
                    t.wait_ge(env.s['DV'], dvwo)
                    if oc4 >= 2 or tt > 0:
                        t.wait_ge(env.s['AC'], env.c['AC'])
                    for hh in range(16):
                        env.inc(t.matmul(ps[:], aosl[:, hh * 128:(hh + 1) * 128],
                                         bigC[:, hh * 2048 + oc4 * 512: hh * 2048 + (oc4 + 1) * 512],
                                         start=(hh == 0), stop=(hh == 15)), 'PE', 1)
                pemm = env.c['PE']
                eb = evb if oc4 % 2 == 0 else evb2
                @B.scalar
                def _(sc, ps=ps, eb=eb):
                    sc.wait_ge(env.s['PE'], pemm)
                    sc.wait_ge(env.s['LG'], env.c['LG'])
                    env.inc(sc.copy(eb[:, 0:512], ps[:]), 'AC', 1)
                acs2 = env.c['AC']
                @B.gpsimd
                def _(gp, tt=tt, oc4=oc4, eb=eb):
                    gp.wait_ge(env.s['AC'], acs2)
                    env.inc(gp.dma_start(out=d["sp_s"][tt * 128:(tt + 1) * 128, oc4 * 512:(oc4 + 1) * 512],
                                         in_=eb[:, 0:512]), 'LG', 16)

        env.barrier()

        @B.vector
        def _(v):
            env.inc(v.memset(bigC[:], 0.0), 'DV', 1)
            v.drain()
        moe_steps = [(e, fc) for e in range(8) for fc in range(4)]
        moe_strip_ac = {}
        moe_ws_dv = [0, 0]
        moe_stg_pe = [0, 0]
        moe_psum_dv = [0, 0]

        def moe_prep(si):
            e, fc = moe_steps[si]
            buf = si % 2
            for rt in range(4):
                par = rt % 2
                ws = wstage[:, par * 2048:(par + 1) * 2048]
                sg = stgb if par == 0 else stgb2
                wsdv = moe_ws_dv[par]
                @B.sync
                def _(sy, e=e, fc=fc, rt=rt, wsdv=wsdv):
                    sy.wait_ge(env.s['DV'], wsdv)
                    env.inc(sy.dma_start(out=ws, in_=d["wex"][e * 2048 + fc * 512 + rt * 128: e * 2048 + fc * 512 + (rt + 1) * 128, :]), 'LD', 16)
                ldw = env.c['LD']
                stpe = moe_stg_pe[par]
                @B.vector
                def _(v, ldw=ldw, stpe=stpe, sg=sg):
                    v.wait_ge(env.s['LD'], ldw)
                    v.wait_ge(env.s['PE'], stpe)
                    env.inc(v.tensor_copy(sg[:], ws), 'DV', 1)
                moe_ws_dv[par] = env.c['DV']
                dvc = env.c['DV']
                transpose_blocks(lambda i, sg=sg: sg[:, i * 128:(i + 1) * 128], IC, None,
                                 wait_dv=dvc,
                                 dst_batch=(wtrans, lambda i0, buf=buf, rt=rt: buf * 8192 + i0 * 512 + rt * 128, 512))
                moe_stg_pe[par] = env.c['PE']
            moe_strip_ac[si] = env.c['AC']

        def moe_prep_rt(si, rt):
            e, fc = moe_steps[si]
            buf = si % 2
            par = rt % 2
            ws = wstage[:, par * 2048:(par + 1) * 2048]
            sg = stgb if par == 0 else stgb2
            wsdv = moe_ws_dv[par]
            @B.sync
            def _(sy, e=e, fc=fc, rt=rt, wsdv=wsdv):
                sy.wait_ge(env.s['DV'], wsdv)
                env.inc(sy.dma_start(out=ws, in_=d["wex"][e * 2048 + fc * 512 + rt * 128: e * 2048 + fc * 512 + (rt + 1) * 128, :]), 'LD', 16)
            ldw = env.c['LD']
            stpe = moe_stg_pe[par]
            @B.vector
            def _(v, ldw=ldw, stpe=stpe, sg=sg):
                v.wait_ge(env.s['LD'], ldw)
                v.wait_ge(env.s['PE'], stpe)
                env.inc(v.tensor_copy(sg[:], ws), 'DV', 1)
            moe_ws_dv[par] = env.c['DV']
            dvc = env.c['DV']
            transpose_blocks(lambda i, sg=sg: sg[:, i * 128:(i + 1) * 128], IC, None,
                             wait_dv=dvc,
                             dst_batch=(wtrans, lambda i0, buf=buf, rt=rt: buf * 8192 + i0 * 512 + rt * 128, 512))
            moe_stg_pe[par] = env.c['PE']
            if rt == 3:
                moe_strip_ac[si] = env.c['AC']

        moe_prep(0)
        for si, (e, fc) in enumerate(moe_steps):
            buf = si % 2
            acst = moe_strip_ac[si]
            for tt in range(TT):
                if tt % 4 == 0 and si + 1 < len(moe_steps):
                    moe_prep_rt(si + 1, tt // 4)
                ps = psB if tt % 2 == 0 else psC
                bp = moe_psum_dv[tt % 2]
                @B.tensor
                def _(t, tt=tt, si=si, buf=buf, acst=acst, bp=bp):
                    t.wait_ge(env.s['AC'], acst)
                    if bp:
                        t.wait_ge(env.s['DV'], bp)
                    for i in range(IC):
                        env.inc(t.matmul(ps[:], bigA[:, i * 2048 + tt * 128: i * 2048 + tt * 128 + 128],
                                         wtrans[:, buf * 8192 + i * 512: buf * 8192 + (i + 1) * 512],
                                         start=(i == 0), stop=(i == IC - 1)), 'PE', 1)
                pemm = env.c['PE']
                @B.vector
                def _(v, tt=tt, fc=fc, e=e, ps=ps):
                    v.wait_ge(env.s['PE'], pemm)
                    env.inc(v.scalar_tensor_tensor(
                        bigC[:, tt * 2048 + fc * 512: tt * 2048 + (fc + 1) * 512],
                        ps[:], selm[:, tt * 8 + e: tt * 8 + e + 1],
                        bigC[:, tt * 2048 + fc * 512: tt * 2048 + (fc + 1) * 512],
                        op0=ALU.mult, op1=ALU.add), 'DV', 1)
                moe_psum_dv[tt % 2] = env.c['DV']

        env.barrier()

        for tt in range(TT):
            for oc4 in range(4):
                k = tt * 4 + oc4
                gsb = sg0 if k % 2 == 0 else sg1
                @B.sync
                def _(sy, tt=tt, oc4=oc4, k=k):
                    if k > 0:
                        sy.wait_ge(env.s['OT'], env.c['OT'])
                        sy.wait_ge(env.s['DV'], env.c['DV'])
                    sy.wait_ge(env.s['LG'], env.c['LG'])
                    env.inc(sy.dma_start(out=sx0[:, 0:512], in_=d["sp_x1"][tt * 128:(tt + 1) * 128, oc4 * 512:(oc4 + 1) * 512]), 'LD', 16)
                    env.inc(sy.dma_start(out=ssb[:, 0:512], in_=d["sp_s"][tt * 128:(tt + 1) * 128, oc4 * 512:(oc4 + 1) * 512]), 'LD', 16)
                ldxs = env.c['LD']
                @B.scalar
                def _(sc, tt=tt, oc4=oc4, gsb=gsb):
                    sc.wait_ge(env.s['LD'], ldxs)
                    sc.wait_ge(env.s['DV'], env.c['DV'])
                    env.inc(sc.activation(gsb[:], bigC[:, tt * 2048 + oc4 * 512: tt * 2048 + (oc4 + 1) * 512],
                                          AF.Silu, bias=0.0, scale=1.0), 'AC', 1)
                acg = env.c['AC']
                @B.vector
                def _(v, gsb=gsb):
                    v.wait_ge(env.s['AC'], acg)
                    v.wait_ge(env.s['LD'], ldxs)
                    v.tensor_tensor(gsb[:], gsb[:], ssb[:, 0:512], op=ALU.mult)
                    v.drain()
                    env.inc(v.tensor_tensor(sx0[:, 0:512], sx0[:, 0:512], gsb[:], op=ALU.add), 'DV', 1)
                    v.drain()
                dvx = env.c['DV']
                @B.gpsimd
                def _(gp, tt=tt, oc4=oc4):
                    gp.wait_ge(env.s['DV'], dvx)
                    env.inc(gp.dma_start(out=out_ap[tt * 128:(tt + 1) * 128, oc4 * 512:(oc4 + 1) * 512],
                                         in_=sx0[:, 0:512]), 'OT', 16)

        @B.gpsimd
        def _(gp):
            gp.wait_ge(env.s['OT'], env.c['OT'])

    return nc


_NC_CACHE = [None]


def kernel(**inputs):
    f32 = lambda a: np.ascontiguousarray(np.asarray(a, dtype=np.float32))
    x = f32(inputs["L0_attn_x"])
    hd = 128
    inv_freq = (1.0 / (10000.0 ** (np.arange(0, hd, 2, dtype=np.float32) / np.float32(hd)))).astype(np.float32)
    t = np.arange(S, dtype=np.float32)
    freqs = np.outer(t, inv_freq).astype(np.float32)
    emb = np.concatenate([freqs, freqs], axis=-1)
    cosT = np.ascontiguousarray(np.cos(emb).astype(np.float32))
    sinT = np.sin(emb).astype(np.float32)
    sinF = sinT.copy()
    sinF[:, :64] = -sinF[:, :64]
    sinF = np.ascontiguousarray(sinF)

    dn = np.ascontiguousarray(np.asarray(inputs["L0_ffn_down_reorder_index"], dtype=np.int32).reshape(IFF, 1))

    common = {
        "l0wq": f32(inputs["L0_attn_Wq"]), "l0wk": f32(inputs["L0_attn_Wk"]),
        "l0wv": f32(inputs["L0_attn_Wv"]), "l0wo": f32(inputs["L0_attn_Wo"]),
        "wg": f32(inputs["L0_ffn_w_gate"]), "wu": f32(inputs["L0_ffn_w_up"]),
        "wd": f32(inputs["L0_ffn_w_down"]), "dnidx": dn,
        "l1wq": f32(inputs["L1_attn_Wq"]), "l1wk": f32(inputs["L1_attn_Wk"]),
        "l1wv": f32(inputs["L1_attn_Wv"]), "l1wo": f32(inputs["L1_attn_Wo"]),
        "lam": f32(inputs["L1_attn_lam"]).reshape(1, NH),
        "wr": f32(inputs["L1_ffn_w_router"]),
        "wex": f32(inputs["L1_ffn_w_experts"]).reshape(8 * D, D),
        "cosT": cosT, "sinF": sinF,
    }
    if _NC_CACHE[0] is None:
        _NC_CACHE[0] = build_nc()
    nc = _NC_CACHE[0]
    in_maps = [dict(common, x0=np.ascontiguousarray(x[c])) for c in range(8)]
    res = run_bass_kernel_spmd(nc, in_maps, core_ids=list(range(8)))
    return np.stack([r["out"] for r in res.results], axis=0)


# version 11
# speedup vs baseline: 1.2309x; 1.0040x over previous
import math
import numpy as np
from contextlib import ExitStack

import concourse.bass as bass
import concourse.mybir as mybir
from concourse.bass_utils import run_bass_kernel_spmd
from concourse.bass import IndirectOffsetOnAxis

F32 = mybir.dt.float32
BF16 = mybir.dt.bfloat16
F8 = mybir.dt.float8e4
I32 = mybir.dt.int32
AF = mybir.ActivationFunctionType
ALU = mybir.AluOpType
AX = mybir.AxisListType

S = 2048
D = 2048
TT = 16
IC = 16
IFF = 5632
NJT = 44
NH = 16
HD = 128
NEG = -1.0e9


class Env:
    def __init__(self, nc, block, sems):
        self.nc = nc
        self.block = block
        self.s = sems
        self.c = {k: 0 for k in sems}

    def inc(self, inst, k, by=1):
        inst.then_inc(self.s[k], by)
        self.c[k] += by
        return self.c[k]

    def barrier(self):
        self.c['PH'] += 5
        tgt = self.c['PH']
        for attr in ('tensor', 'vector', 'scalar', 'gpsimd', 'sync'):
            def mk(eng, tgt=tgt):
                eng.nop().then_inc(self.s['PH'], 1)
                eng.wait_ge(self.s['PH'], tgt)
            getattr(self.block, attr)(mk)


_DEBUG_SPILL = [False]


def build_nc():
    nc = bass.Bass("TRN2", target_bir_lowering=False, debug=False)
    d = {}
    def inp(name, shape, dt=F32):
        d[name] = nc.dram_tensor(name, shape, dt, kind="ExternalInput").ap()
    inp("x0", [S, D])
    inp("l0wq", [D, D]); inp("l0wk", [D, D]); inp("l0wv", [D, D]); inp("l0wo", [D, D])
    inp("wg", [IFF, D]); inp("wu", [IFF, D]); inp("wd", [D, IFF])
    inp("dnidx", [IFF, 1], I32)
    inp("l1wq", [D, D]); inp("l1wk", [D, 512]); inp("l1wv", [D, 512]); inp("l1wo", [D, D])
    inp("lam", [1, NH]); inp("wr", [8, D]); inp("wex", [8 * D, D])
    inp("cosT", [S, HD]); inp("sinF", [S, HD])
    out_ap = nc.dram_tensor("out", [S, D], F32, kind="ExternalOutput").ap()
    def scr(name, shape, dt):
        if _DEBUG_SPILL[0]:
            d[name] = nc.dram_tensor(name, shape, dt, kind="ExternalOutput").ap()
        else:
            d[name] = nc.dram_tensor(name, shape, dt).ap()
    scr("sp_q", [D, S], BF16)
    scr("sp_k", [D, S], BF16)
    scr("sp_v", [S, D], BF16)
    scr("sp_ao", [D, S], BF16)
    scr("sp_s", [S, D], BF16)
    scr("sp_T", [S, IFF], F32)
    scr("sp_qtT", [IFF, S], BF16)
    scr("sp_x1", [S, D], F32)
    if _DEBUG_SPILL[0]:
        dbg_a = nc.dram_tensor("dbg_a", [128, 32768], BF16, kind="ExternalOutput").ap()
        dbg_c = nc.dram_tensor("dbg_c", [128, 32768], BF16, kind="ExternalOutput").ap()
        dbg_q = nc.dram_tensor("dbg_q", [D, S], BF16, kind="ExternalOutput").ap()
        dbg_ao = nc.dram_tensor("dbg_ao", [D, S], BF16, kind="ExternalOutput").ap()
        dbg_s = nc.dram_tensor("dbg_s", [S, D], BF16, kind="ExternalOutput").ap()

    with ExitStack() as ctx:
        def sb(n, s_, dt):
            return ctx.enter_context(nc.sbuf_tensor(n, s_, dt))
        bigA = sb("bigA", [128, 32768], BF16)
        bigC = sb("bigC", [128, 32768], BF16)
        wstage = sb("wstage", [128, 5632], F32)
        wtrans = sb("wtrans", [128, 16384], BF16)
        f8s = sb("f8s", [128, IFF], F8)
        evb = sb("evb", [128, 512], BF16)
        evb2 = sb("evb2", [128, 512], BF16)
        sg0 = sb("sg0", [128, 512], F32)
        sg1 = sb("sg1", [128, 512], F32)
        sx0 = sb("sx0", [128, 1024], F32)
        ssb = sb("ssb", [128, 1024], BF16)
        stats = sb("stats", [128, 64], F32)
        selm = sb("selm", [128, 128], F32)
        tmaxs = sb("tmaxs", [128, 16], F32)
        ident = sb("ident", [128, 128], BF16)
        maskc = sb("maskc", [128, 128], F32)
        maska = sb("maska", [128, 128], F32)
        lamb = sb("lamb", [128, NH], F32)
        onesc = sb("onesc", [1, 128], F32)
        lams = sb("lams", [1, NH], F32)
        idxs = sb("idxs", [128, NJT], I32)

        stgb2 = sb("stgb2", [128, 2048], BF16)
        stgb = wstage[:, 4096:5120].bitcast(BF16)
        stg5 = bigC[:, 0:5632]

        psA = ctx.enter_context(nc.psum_tensor("psA", [128, 2048], F32))
        psB = ctx.enter_context(nc.psum_tensor("psB", [128, 512], F32))
        psC = ctx.enter_context(nc.psum_tensor("psC", [128, 512], F32))
        ptA = ctx.enter_context(nc.psum_tensor("ptA", [128, 1024], BF16))
        ptB = ctx.enter_context(nc.psum_tensor("ptB", [128, 1024], BF16))

        sems = {}
        for name in ('LD', 'LG', 'PE', 'DV', 'AC', 'GP', 'PH', 'OT'):
            sems[name] = ctx.enter_context(nc.semaphore(name))
        block = ctx.enter_context(nc.Block())
        env = Env(nc, block, sems)
        B = env.block

        env.marks = {'pt_free': 0}

        def transpose_blocks(src_ap_fn, n128, dst_fn, wait_dv=None, wait_extra=None,
                             dst_batch=None, evac_eng='scalar'):
            for g0 in range(0, n128, 16):
                n = min(16, n128 - g0)
                @B.tensor
                def _(t, g0=g0, n=n):
                    if g0 == 0:
                        if wait_dv is not None:
                            t.wait_ge(env.s['DV'], wait_dv)
                        if wait_extra is not None:
                            wait_extra(t)
                    t.wait_ge(env.s['AC'], env.marks['pt_free'])
                    for j in range(n):
                        i = g0 + j
                        pt = ptA if j < 8 else ptB
                        env.inc(t.transpose(pt[:, (j % 8) * 128:(j % 8 + 1) * 128],
                                            src_ap_fn(i), ident[:]), 'PE', 1)
                pet = env.c['PE']
                eng_dec = getattr(B, evac_eng)
                @eng_dec
                def _(sc, g0=g0, n=n, pet=pet):
                    cp = sc.copy if evac_eng == 'scalar' else sc.tensor_copy
                    sc.wait_ge(env.s['PE'], pet)
                    if dst_batch is not None:
                        tgt, base_fn, step = dst_batch
                        W = tgt.shape[1]
                        for h0 in range(0, n, 8):
                            m = min(8, n - h0)
                            pt = ptA if h0 == 0 else ptB
                            b0 = base_fn(g0 + h0)
                            if step == 128:
                                env.inc(cp(tgt[:, b0:b0 + m * 128],
                                           pt[:, 0:m * 128]), 'AC', 1)
                                continue
                            mm_ = m
                            if b0 + m * step > W:
                                mm_ = m - 1
                                env.inc(cp(tgt[:, base_fn(g0 + h0 + mm_):base_fn(g0 + h0 + mm_) + 128],
                                           pt[:, mm_ * 128:(mm_ + 1) * 128]), 'AC', 1)
                            if mm_ == 1:
                                env.inc(cp(tgt[:, b0:b0 + 128], pt[:, 0:128]), 'AC', 1)
                            elif mm_ > 0:
                                dst3 = tgt[:, b0:b0 + mm_ * step].rearrange(
                                    "p (a c) -> p a c", a=mm_)[:, :, 0:128]
                                src3 = pt[:, 0:mm_ * 128].rearrange("p (a c) -> p a c", a=mm_)
                                env.inc(cp(dst3, src3), 'AC', 1)
                    else:
                        for j in range(n):
                            i = g0 + j
                            pt = ptA if j < 8 else ptB
                            env.inc(cp(dst_fn(i), pt[:, (j % 8) * 128:(j % 8 + 1) * 128]), 'AC', 1)
                env.marks['pt_free'] = env.c['AC']
            return env.c['AC']

        @B.sync
        def _(sy):
            env.inc(sy.dma_start(out=idxs[:], in_=d["dnidx"].rearrange("(a b) c -> b (a c)", b=128),
                                 allow_slow_non_contiguous=True), 'LD', 16)
            env.inc(sy.dma_start(out=lams[:], in_=d["lam"]), 'LD', 16)
        ld_init = env.c['LD']

        @B.gpsimd
        def _(gp):
            gp.memset(ident[:], 1.0)
            gp.affine_select(ident[:], ident[:], [[-1, 128]], ALU.is_equal, 0.0,
                             base=0, channel_multiplier=1)
            gp.memset(maskc[:], 0.0)
            gp.affine_select(maskc[:], maskc[:], [[-1, 128]], ALU.is_ge, NEG,
                             base=0, channel_multiplier=1)
            gp.memset(maska[:], 0.0)
            env.inc(gp.affine_select(maska[:], maska[:], [[1, 128]], ALU.is_ge, NEG,
                                     base=-1, channel_multiplier=-1), 'GP', 1)

        @B.vector
        def _(v):
            v.memset(onesc[:], 1.0)
            env.inc(v.memset(stats[:, 5:6], 1e-5), 'DV', 1)


        @B.tensor
        def _(t):
            t.wait_ge(env.s['LD'], ld_init)
            t.wait_ge(env.s['DV'], env.c['DV'])
            env.inc(t.matmul(psB[:, 0:NH], onesc[:], lams[:], start=True, stop=True), 'PE', 1)

        @B.vector
        def _(v):
            v.wait_ge(env.s['PE'], env.c['PE'])
            env.inc(v.tensor_copy(lamb[:], psB[:, 0:NH]), 'DV', 1)

        env.barrier()

        x0_ap = d["x0"]
        wr2t = wtrans[:, 16128:16384]

        def ln_quant_phase(x_src, layer):
            for tt in range(TT):
                xw = wstage[:, 0:2048]
                xnf = wstage[:, 2048:4096]
                mu = stats[:, 0:1]; ss = stats[:, 1:2]; var = stats[:, 2:3]
                sd = stats[:, 3:4]; r0 = stats[:, 4:5]
                t1 = stats[:, 6:7]; t2 = stats[:, 7:8]
                rmax = stats[:, 8:9]; q1 = stats[:, 9:10]; q2 = stats[:, 10:11]

                @B.sync
                def _(sy, tt=tt):
                    if tt >= 1:
                        sy.wait_ge(env.s['DV'], env.c['DV'])
                        sy.wait_ge(env.s['AC'], env.c['AC'])
                    if layer == 1:
                        sy.wait_ge(env.s['LG'], env.c['LG'])
                    env.inc(sy.dma_start(out=xw, in_=x_src[tt * 128:(tt + 1) * 128, :]), 'LD', 16)
                ldx = env.c['LD']

                @B.vector
                def _(v):
                    v.wait_ge(env.s['LD'], ldx)
                    env.inc(v.reduce_sum(mu, xw, axis=AX.X), 'DV', 1)

                @B.scalar
                def _(sc):
                    sc.wait_ge(env.s['LD'], ldx)
                    env.inc(sc.activation(wtrans[:, 0:2048], xw, AF.Square, bias=0.0, scale=1.0,
                                          accum_out=ss), 'AC', 1)
                acs = env.c['AC']

                @B.vector
                def _(v):
                    v.wait_ge(env.s['AC'], acs)
                    v.tensor_scalar_mul(mu, mu, 1.0 / 2048.0)
                    v.drain()
                    v.tensor_scalar_mul(var, ss, 1.0 / 2048.0)
                    v.tensor_mul(t1, mu, mu)
                    v.drain()
                    env.inc(v.tensor_sub(var, var, t1), 'DV', 1)
                    v.drain()
                dvv = env.c['DV']

                @B.scalar
                def _(sc):
                    sc.wait_ge(env.s['DV'], dvv)
                    env.inc(sc.activation(sd, var, AF.Sqrt, bias=stats[:, 5:6], scale=1.0), 'AC', 1)
                acsd = env.c['AC']

                @B.vector
                def _(v):
                    v.wait_ge(env.s['AC'], acsd)
                    v.reciprocal(r0, sd)
                    v.drain()
                    v.tensor_scalar_add(t2, var, 1e-5)
                    v.drain()
                    for _i in range(2):
                        v.tensor_mul(t1, r0, r0)
                        v.drain()
                        v.tensor_mul(t1, t1, t2)
                        v.drain()
                        v.tensor_scalar(t1, t1, -0.5, 1.5, op0=ALU.mult, op1=ALU.add)
                        v.drain()
                        v.tensor_mul(r0, r0, t1)
                        v.drain()
                    v.tensor_scalar(xnf, xw, mu, r0, op0=ALU.subtract, op1=ALU.mult)
                    v.drain()
                    env.inc(v.tensor_copy(stgb[:], xnf), 'DV', 1)
                dvxn = env.c['DV']

                transpose_blocks(lambda i: stgb[:, i * 128:(i + 1) * 128], IC, None,
                                 wait_dv=dvxn, evac_eng='vector',
                                 dst_batch=(bigA, lambda i0, tt=tt: i0 * 2048 + tt * 128, 2048))
                acev = env.c['AC']

                if layer == 0:
                    @B.vector
                    def _(v):
                        v.wait_ge(env.s['DV'], dvxn)
                        v.wait_ge(env.s['AC'], acev)
                        env.inc(v.reduce_max(rmax, xnf, axis=AX.X, apply_absolute_value=True), 'DV', 1)
                        v.drain()
                        v.reciprocal(q1, rmax)
                        v.drain()
                        v.tensor_scalar_mul(q1, q1, 224.0)
                        v.drain()
                        v.tensor_scalar_mul(q2, rmax, 1.0 / 224.0)
                        v.drain()
                        env.inc(v.tensor_scalar_mul(f8s[:, 0:2048], xnf, q1), 'DV', 1)
                        v.drain()
                        env.inc(v.tensor_scalar_mul(stgb[:], f8s[:, 0:2048], q2), 'DV', 1)
                    dvq = env.c['DV']
                    transpose_blocks(lambda i: stgb[:, i * 128:(i + 1) * 128], IC, None,
                                     wait_dv=dvq, evac_eng='vector',
                                     dst_batch=(bigC, lambda i0, tt=tt: i0 * 2048 + tt * 128, 2048))
                else:
                    lob = stg5[:, 0:2048]
                    @B.vector
                    def _(v):
                        v.wait_ge(env.s['AC'], acev)
                        v.drain()
                        env.inc(v.tensor_tensor(lob, xnf, stgb[:], op=ALU.subtract), 'DV', 1)
                        v.drain()
                    dvlo = env.c['DV']
                    transpose_blocks(lambda i: lob[:, i * 128:(i + 1) * 128], IC, None,
                                     wait_dv=dvlo, evac_eng='vector',
                                     dst_batch=(stg5, lambda i0: 2048 + i0 * 128, 128))
                    aclo = env.c['AC']

                    @B.tensor
                    def _(t, tt=tt):
                        t.wait_ge(env.s['AC'], aclo)
                        for i in range(IC):
                            t.matmul(psB[:, 0:16], bigA[:, i * 2048 + tt * 128: i * 2048 + tt * 128 + 128],
                                     wr2t[:, i * 16:(i + 1) * 16], start=(i == 0), stop=False)
                        for i in range(IC):
                            env.inc(t.matmul(psB[:, 0:16], stg5[:, 2048 + i * 128:2048 + (i + 1) * 128],
                                             wr2t[:, i * 16:(i + 1) * 16],
                                             start=False, stop=(i == IC - 1)), 'PE', 1)
                    perl = env.c['PE']

                    @B.vector
                    def _(v, tt=tt):
                        v.wait_ge(env.s['PE'], perl)
                        lg = stats[:, 16:24]
                        m8 = stats[:, 24:32]
                        v.tensor_copy(lg, psB[:, 0:8])
                        v.drain()
                        v.tensor_tensor(lg, lg, psB[:, 8:16], op=ALU.add)
                        v.drain()
                        v.max(m8, lg)
                        v.drain()
                        nt = stats[:, 32:33]
                        v.memset(nt, 1.0)
                        v.drain()
                        for e in range(8):
                            he = stats[:, 33:34]
                            v.tensor_tensor(he, lg[:, e:e + 1], m8[:, 0:1], op=ALU.is_equal)
                            v.drain()
                            v.tensor_mul(selm[:, tt * 8 + e: tt * 8 + e + 1], he, nt)
                            v.drain()
                            v.tensor_sub(nt, nt, selm[:, tt * 8 + e: tt * 8 + e + 1])
                            v.drain()
                        env.inc(v.engine_nop(), 'DV', 1)

        ln_quant_phase(x0_ap, 0)
        env.barrier()
        if _DEBUG_SPILL[0]:
            @B.gpsimd
            def _(gp):
                env.inc(gp.dma_start(out=dbg_a, in_=bigA[:]), 'LG', 16)
                env.inc(gp.dma_start(out=dbg_c, in_=bigC[:]), 'LG', 16)
                gp.wait_ge(env.s['LG'], env.c['LG'])
            env.barrier()

        p2_ws_dv = [0, 0]
        p2_stg_pe = [0, 0]
        p2_strip_ac = {}

        def l0_proj_fm(w_ap, sp_dst):
            def prep(oc):
                par = oc % 2
                ws = wstage[:, par * 2048:(par + 1) * 2048]
                sg = stgb if par == 0 else stgb2
                st0 = 8192 * par
                wsdv = p2_ws_dv[par]
                @B.sync
                def _(sy, oc=oc, wsdv=wsdv, ws=ws):
                    sy.wait_ge(env.s['DV'], wsdv)
                    env.inc(sy.dma_start(out=ws, in_=w_ap[oc * 128:(oc + 1) * 128, :]), 'LD', 16)
                ldw = env.c['LD']
                stpe = p2_stg_pe[par]
                @B.vector
                def _(v, ldw=ldw, stpe=stpe, sg=sg, ws=ws):
                    v.wait_ge(env.s['LD'], ldw)
                    v.wait_ge(env.s['PE'], stpe)
                    env.inc(v.tensor_copy(sg[:], ws), 'DV', 1)
                p2_ws_dv[par] = env.c['DV']
                dvc = env.c['DV']
                transpose_blocks(lambda i, sg=sg: sg[:, i * 128:(i + 1) * 128], IC, None,
                                 wait_dv=dvc,
                                 dst_batch=(wtrans, lambda i0, st0=st0: st0 + i0 * 128, 128))
                p2_stg_pe[par] = env.c['PE']
                p2_strip_ac[oc] = env.c['AC']

            accs = [psB[:], psC[:], psA[:, 0:512], psA[:, 512:1024]]
            ebs = [evb[:, 0:512], evb2[:, 0:512]]
            eb_marks = [0, 0]
            ps_mark = [0]
            prep(0)
            for oc in range(16):
                if oc + 1 < 16:
                    prep(oc + 1)
                acst = p2_strip_ac[oc]
                st0 = 8192 * (oc % 2)
                bp = ps_mark[0]
                @B.tensor
                def _(t, oc=oc, st0=st0, acst=acst, bp=bp):
                    t.wait_ge(env.s['AC'], acst)
                    if bp:
                        t.wait_ge(env.s['AC'], bp)
                    for i in range(IC):
                        for tc in range(4):
                            env.inc(t.matmul(accs[tc], wtrans[:, st0 + i * 128: st0 + (i + 1) * 128],
                                             bigA[:, i * 2048 + tc * 512: i * 2048 + (tc + 1) * 512],
                                             start=(i == 0), stop=(i == IC - 1)), 'PE', 1)
                pemm = env.c['PE']
                for tc in range(4):
                    eb = ebs[tc % 2]
                    eblg = eb_marks[tc % 2]
                    @B.scalar
                    def _(sc, tc=tc, eb=eb, eblg=eblg, pemm=pemm):
                        sc.wait_ge(env.s['PE'], pemm)
                        if eblg:
                            sc.wait_ge(env.s['LG'], eblg)
                        env.inc(sc.copy(eb, accs[tc]), 'AC', 1)
                    acev2 = env.c['AC']
                    @B.gpsimd
                    def _(gp, oc=oc, tc=tc, eb=eb, acev2=acev2):
                        gp.wait_ge(env.s['AC'], acev2)
                        env.inc(gp.dma_start(out=sp_dst[oc * 128:(oc + 1) * 128, tc * 512:(tc + 1) * 512],
                                             in_=eb), 'LG', 16)
                    eb_marks[tc % 2] = env.c['LG']
                ps_mark[0] = env.c['AC']

        l0_proj_fm(d["l0wq"], d["sp_q"])
        l0_proj_fm(d["l0wk"], d["sp_k"])

        for oc4 in range(4):
            for rt in range(4):
                ws = wstage[:, (rt % 2) * 2048:(rt % 2 + 1) * 2048]
                @B.sync
                def _(sy):
                    sy.wait_ge(env.s['DV'], env.c['DV'])
                    env.inc(sy.dma_start(out=ws, in_=d["l0wv"][(oc4 * 4 + rt) * 128:(oc4 * 4 + rt + 1) * 128, :]), 'LD', 16)
                ldw = env.c['LD']
                @B.vector
                def _(v):
                    v.wait_ge(env.s['LD'], ldw)
                    v.wait_ge(env.s['PE'], env.c['PE'])
                    env.inc(v.tensor_copy(stgb[:], ws), 'DV', 1)
                dvc = env.c['DV']
                transpose_blocks(lambda i: stgb[:, i * 128:(i + 1) * 128], IC, None,
                                 wait_dv=dvc,
                                 dst_batch=(wtrans, lambda i0, rt=rt: i0 * 512 + rt * 128, 512))
            acst = env.c['AC']
            for tt in range(TT):
                ps = psB if tt % 2 == 0 else psC
                @B.tensor
                def _(t, tt=tt, oc4=oc4):
                    t.wait_ge(env.s['AC'], acst)
                    if tt >= 2 or oc4 > 0:
                        t.wait_ge(env.s['AC'], env.c['AC'])
                    for i in range(IC):
                        env.inc(t.matmul(ps[:], bigA[:, i * 2048 + tt * 128: i * 2048 + tt * 128 + 128],
                                         wtrans[:, i * 512:(i + 1) * 512],
                                         start=(i == 0), stop=(i == IC - 1)), 'PE', 1)
                pemm = env.c['PE']
                eb = evb if tt % 2 == 0 else evb2
                @B.scalar
                def _(sc):
                    sc.wait_ge(env.s['PE'], pemm)
                    sc.wait_ge(env.s['LG'], env.c['LG'])
                    env.inc(sc.copy(eb[:, 0:512], ps[:]), 'AC', 1)
                acev2 = env.c['AC']
                @B.gpsimd
                def _(gp, tt=tt, oc4=oc4, eb=eb):
                    gp.wait_ge(env.s['AC'], acev2)
                    env.inc(gp.dma_start(out=d["sp_v"][tt * 128:(tt + 1) * 128, oc4 * 512:(oc4 + 1) * 512],
                                         in_=eb[:, 0:512]), 'LG', 16)

        env.barrier()

        ptl = bigA[:, 6144:8192]
        pTs = bigA[:, 8192:16384]
        qkv_base = [0, 16384]
        qkv_pe_mark = [0, 0]
        qkv_ld = {}
        pv_ac = [0, 0]
        eb_lg = [0, 0]
        last_dvm = [0]
        last_acex = [0]
        l0_last_trpe = [0]

        def l0_qkv_load(h):
            par = h % 2
            base = qkv_base[par]
            mark = qkv_pe_mark[par]
            @B.sync
            def _(sy, h=h, base=base, mark=mark):
                sy.wait_ge(env.s['PE'], mark)
                env.inc(sy.dma_start(out=bigA[:, base:base + 2048], in_=d["sp_q"][h * 128:(h + 1) * 128, :]), 'LD', 16)
                env.inc(sy.dma_start(out=bigA[:, base + 2048:base + 4096], in_=d["sp_k"][h * 128:(h + 1) * 128, :]), 'LD', 16)
                env.inc(sy.dma_start(
                    out=bigA[:, base + 4096:base + 6144].rearrange("p (a c) -> p a c", a=TT),
                    in_=d["sp_v"].rearrange("(a p) o -> p a o", p=128)[:, :, h * 128:(h + 1) * 128]), 'LD', 16)
            qkv_ld[h] = env.c['LD']

        def l0_scores(h, qt):
            base = qkv_base[h % 2]
            dvm0, acex0 = last_dvm[0], last_acex[0]
            @B.tensor
            def _(t, h=h, qt=qt, base=base, dvm0=dvm0, acex0=acex0):
                t.wait_ge(env.s['LD'], qkv_ld[h])
                if dvm0:
                    t.wait_ge(env.s['DV'], dvm0)
                if acex0:
                    t.wait_ge(env.s['AC'], acex0)
                for kc in range(4):
                    env.inc(t.matmul(psA[:, kc * 512:(kc + 1) * 512],
                                     bigA[:, base + qt * 128: base + (qt + 1) * 128],
                                     bigA[:, base + 2048 + kc * 512: base + 2048 + (kc + 1) * 512],
                                     start=True, stop=True), 'PE', 1)
            return env.c['PE']

        l0_qkv_load(0)
        l0_scores(0, 0)
        pe_sc = {(0, 0): env.c['PE']}
        for h in range(NH):
            if h + 1 < NH:
                l0_qkv_load(h + 1)
            base = qkv_base[h % 2]
            for qg in range(4):
                for qi in range(4):
                    qt = qg * 4 + qi
                    pesc = pe_sc.pop((h, qt))
                    m0 = stats[:, 40:41]; nm = stats[:, 41:42]
                    zz = stats[:, 42:43]; zi = stats[:, 43:44]
                    @B.vector
                    def _(v, pesc=pesc):
                        v.wait_ge(env.s['PE'], pesc)
                        v.reduce_max(m0, psA[:], axis=AX.X)
                        v.drain()
                        env.inc(v.tensor_scalar_mul(nm, m0, -1.0 / math.sqrt(128.0)), 'DV', 1)
                        v.drain()
                    dvm = env.c['DV']
                    last_dvm[0] = dvm

                    trpe = l0_last_trpe[0]
                    @B.scalar
                    def _(sc, dvm=dvm, trpe=trpe):
                        sc.wait_ge(env.s['DV'], dvm)
                        if trpe:
                            sc.wait_ge(env.s['PE'], trpe)
                        env.inc(sc.activation(ptl[:], psA[:], AF.Exp,
                                              bias=nm, scale=1.0 / math.sqrt(128.0),
                                              accum_out=zz), 'AC', 1)
                    acex = env.c['AC']
                    last_acex[0] = acex

                    if qt + 1 < 16:
                        pe_sc[(h, qt + 1)] = l0_scores(h, qt + 1)
                    elif h + 1 < NH:
                        pe_sc[(h + 1, 0)] = l0_scores(h + 1, 0)

                    @B.vector
                    def _(v, acex=acex):
                        v.wait_ge(env.s['AC'], acex)
                        v.reciprocal(zi, zz)
                        v.drain()
                        env.inc(v.tensor_scalar_mul(ptl[:], ptl[:], zi), 'DV', 1)
                    dvp = env.c['DV']

                    transpose_blocks(lambda i: ptl[:, i * 128:(i + 1) * 128], 16, None,
                                     wait_dv=dvp, evac_eng='vector',
                                     dst_batch=(bigA, lambda i0, qi=qi: 8192 + i0 * 512 + qi * 128, 512))
                    l0_last_trpe[0] = env.c['PE']
                acpt = env.c['AC']

                ps = psB if qg % 2 == 0 else psC
                bp = pv_ac[qg % 2]
                @B.tensor
                def _(t, qg=qg, h=h, base=base, bp=bp, ps=ps):
                    t.wait_ge(env.s['AC'], acpt)
                    if bp:
                        t.wait_ge(env.s['AC'], bp)
                    for kt in range(16):
                        env.inc(t.matmul(ps[:], bigA[:, base + 4096 + kt * 128: base + 4096 + (kt + 1) * 128],
                                         pTs[:, kt * 512:(kt + 1) * 512],
                                         start=(kt == 0), stop=(kt == 15)), 'PE', 1)
                pepv = env.c['PE']

                eb = evb if qg % 2 == 0 else evb2
                eblg = eb_lg[qg % 2]
                @B.vector
                def _(sc, pepv=pepv, eblg=eblg, eb=eb, ps=ps):
                    sc.wait_ge(env.s['PE'], pepv)
                    if eblg:
                        sc.wait_ge(env.s['LG'], eblg)
                    env.inc(sc.tensor_copy(eb[:, 0:512], ps[:]), 'AC', 1)
                acao = env.c['AC']
                pv_ac[qg % 2] = acao
                @B.gpsimd
                def _(gp, h=h, qg=qg, eb=eb, acao=acao):
                    gp.wait_ge(env.s['AC'], acao)
                    env.inc(gp.dma_start(out=d["sp_ao"][h * 128:(h + 1) * 128, qg * 512:(qg + 1) * 512],
                                         in_=eb[:, 0:512]), 'LG', 16)
                eb_lg[qg % 2] = env.c['LG']
            qkv_pe_mark[h % 2] = env.c['PE']

        env.barrier()

        for ob in range(16):
            buf = ob % 2
            ws = wstage[:, buf * 2048:(buf + 1) * 2048]
            @B.sync
            def _(sy, ob=ob):
                sy.wait_ge(env.s['DV'], env.c['DV'])
                env.inc(sy.dma_start(out=ws, in_=d["l0wo"][ob * 128:(ob + 1) * 128, :]), 'LD', 16)
            ldw = env.c['LD']
            @B.vector
            def _(v, ob=ob):
                v.wait_ge(env.s['LD'], ldw)
                v.wait_ge(env.s['PE'], env.c['PE'])
                env.inc(v.tensor_copy(stgb[:], ws), 'DV', 1)
            dvc = env.c['DV']
            transpose_blocks(lambda i: stgb[:, i * 128:(i + 1) * 128], IC, None,
                             wait_dv=dvc,
                             dst_batch=(bigA, lambda i0, ob=ob: i0 * 2048 + ob * 128, 2048))
        acwo = env.c['AC']

        for tt in range(TT):
            aosl = stgb
            @B.sync
            def _(sy, tt=tt):
                sy.wait_ge(env.s['PE'], env.c['PE'])
                sy.wait_ge(env.s['LG'], env.c['LG'])
                env.inc(sy.dma_start(
                    out=aosl.rearrange("p (a c) -> p a c", a=16),
                    in_=d["sp_ao"].rearrange("(a p) t -> p a t", p=128)[:, :, tt * 128:(tt + 1) * 128]), 'LD', 16)
            ldao = env.c['LD']
            for oc4 in range(4):
                ps = psB if oc4 % 2 == 0 else psC
                @B.tensor
                def _(t, tt=tt, oc4=oc4):
                    t.wait_ge(env.s['LD'], ldao)
                    t.wait_ge(env.s['AC'], acwo)
                    if oc4 >= 2 or tt > 0:
                        t.wait_ge(env.s['AC'], env.c['AC'])
                    for hh in range(16):
                        env.inc(t.matmul(ps[:], aosl[:, hh * 128:(hh + 1) * 128],
                                         bigA[:, hh * 2048 + oc4 * 512: hh * 2048 + (oc4 + 1) * 512],
                                         start=(hh == 0), stop=(hh == 15)), 'PE', 1)
                pemm = env.c['PE']
                eb = evb if oc4 % 2 == 0 else evb2
                @B.scalar
                def _(sc):
                    sc.wait_ge(env.s['PE'], pemm)
                    sc.wait_ge(env.s['LG'], env.c['LG'])
                    env.inc(sc.copy(eb[:, 0:512], ps[:]), 'AC', 1)
                acs2 = env.c['AC']
                @B.gpsimd
                def _(gp, tt=tt, oc4=oc4, eb=eb):
                    gp.wait_ge(env.s['AC'], acs2)
                    env.inc(gp.dma_start(out=d["sp_s"][tt * 128:(tt + 1) * 128, oc4 * 512:(oc4 + 1) * 512],
                                         in_=eb[:, 0:512]), 'LG', 16)

        env.barrier()
        if _DEBUG_SPILL[0]:
            @B.gpsimd
            def _(gp):
                env.inc(gp.dma_start(out=dbg_q, in_=d["sp_q"]), 'LG', 16)
                env.inc(gp.dma_start(out=dbg_ao, in_=d["sp_ao"]), 'LG', 16)
                env.inc(gp.dma_start(out=dbg_s, in_=d["sp_s"]), 'LG', 16)
                gp.wait_ge(env.s['LG'], env.c['LG'])
            env.barrier()

        @B.vector
        def _(v):
            env.inc(v.memset(tmaxs[:], 0.0), 'DV', 1)

        ffn_ws_dv = [0, 0]
        ffn_stg_pe = [0, 0]
        ffn_strip_ac = {}
        ffn_psum_dv = [0, 0]

        def ffn_prep_piece(jc, piece):
            gi, rt = divmod(piece, 4)
            w_ap = d["wg"] if gi == 0 else d["wu"]
            buf = jc % 2
            strip0 = (gi * 2 + buf) * 8192
            par = piece % 2
            ws = wstage[:, par * 2048:(par + 1) * 2048]
            sg = stgb if par == 0 else stgb2
            f8h = f8s[:, 0:2048] if par == 0 else f8s[:, 2048:4096]
            wsdv = ffn_ws_dv[par]
            @B.gpsimd
            def _(gp, jc=jc, rt=rt, w_ap=w_ap, wsdv=wsdv):
                gp.wait_ge(env.s['DV'], wsdv)
                env.inc(gp.indirect_dma_start(
                    ws, None, w_ap,
                    IndirectOffsetOnAxis(idxs[:, jc * 4 + rt: jc * 4 + rt + 1], 0)), 'LG', 16)
            ldw = env.c['LG']
            wmx = stats[:, 44 + par:45 + par]; s1 = stats[:, 46 + par:47 + par]
            stpe = ffn_stg_pe[par]
            @B.vector
            def _(v, ldw=ldw, stpe=stpe, sg=sg, f8h=f8h, wmx=wmx, s1=s1):
                v.wait_ge(env.s['LG'], ldw)
                v.wait_ge(env.s['PE'], stpe)
                v.reduce_max(wmx, ws, axis=AX.X, apply_absolute_value=True)
                v.drain()
                v.reciprocal(s1, wmx)
                v.drain()
                v.tensor_scalar_mul(s1, s1, 224.0)
                v.drain()
                v.tensor_scalar_mul(f8h, ws, s1)
                v.drain()
                v.tensor_scalar_mul(s1, wmx, 1.0 / 224.0)
                v.drain()
                env.inc(v.tensor_scalar_mul(sg[:], f8h, s1), 'DV', 1)
            ffn_ws_dv[par] = env.c['DV']
            dvq = env.c['DV']
            transpose_blocks(lambda i, sg=sg: sg[:, i * 128:(i + 1) * 128], IC, None,
                             wait_dv=dvq, evac_eng='vector',
                             dst_batch=(bigA, lambda i0, strip0=strip0, rt=rt: strip0 + i0 * 512 + rt * 128, 512))
            ffn_stg_pe[par] = env.c['PE']
            if piece == 7:
                ffn_strip_ac[jc] = env.c['AC']

        for piece in range(8):
            ffn_prep_piece(0, piece)
        for jc in range(11):
            buf = jc % 2
            acst = ffn_strip_ac[jc]

            for tt in range(TT):
                if tt % 2 == 0 and jc + 1 < 11:
                    ffn_prep_piece(jc + 1, tt // 2)
                pg = psA[:, (tt % 2) * 1024:(tt % 2) * 1024 + 512]
                pu = psA[:, (tt % 2) * 1024 + 512:(tt % 2) * 1024 + 1024]
                bp = ffn_psum_dv[tt % 2]
                @B.tensor
                def _(t, tt=tt, jc=jc, buf=buf, bp=bp):
                    t.wait_ge(env.s['AC'], acst)
                    if bp:
                        t.wait_ge(env.s['DV'], bp)
                    for i in range(IC):
                        t.matmul(pg, bigC[:, i * 2048 + tt * 128: i * 2048 + tt * 128 + 128],
                                 bigA[:, (0 * 2 + buf) * 8192 + i * 512:(0 * 2 + buf) * 8192 + (i + 1) * 512],
                                 start=(i == 0), stop=(i == IC - 1))
                        env.inc(t.matmul(pu, bigC[:, i * 2048 + tt * 128: i * 2048 + tt * 128 + 128],
                                         bigA[:, (1 * 2 + buf) * 8192 + i * 512:(1 * 2 + buf) * 8192 + (i + 1) * 512],
                                         start=(i == 0), stop=(i == IC - 1)), 'PE', 1)
                pemm = env.c['PE']
                gsb = sg0 if tt % 2 == 0 else sg1
                @B.scalar
                def _(sc, pg=pg):
                    sc.wait_ge(env.s['PE'], pemm)
                    env.inc(sc.activation(gsb[:], pg, AF.Silu, bias=0.0, scale=1.0), 'AC', 1)
                acsl = env.c['AC']
                tch = stats[:, 48:49]
                @B.vector
                def _(v, tt=tt, jc=jc, pu=pu, gsb=gsb):
                    v.wait_ge(env.s['AC'], acsl)
                    if tt >= 2 or jc > 0:
                        v.wait_ge(env.s['LG'], env.c['LG'])
                    v.tensor_tensor(gsb[:], gsb[:], pu, op=ALU.mult)
                    v.drain()
                    v.reduce_max(tch, gsb[:], axis=AX.X, apply_absolute_value=True)
                    v.drain()
                    env.inc(v.tensor_tensor(tmaxs[:, tt:tt + 1], tmaxs[:, tt:tt + 1], tch, op=ALU.max), 'DV', 1)
                    v.drain()
                dvt = env.c['DV']
                ffn_psum_dv[tt % 2] = dvt
                @B.gpsimd
                def _(gp, tt=tt, jc=jc, gsb=gsb):
                    gp.wait_ge(env.s['DV'], dvt)
                    env.inc(gp.dma_start(out=d["sp_T"][tt * 128:(tt + 1) * 128, jc * 512:(jc + 1) * 512],
                                         in_=gsb[:]), 'LG', 16)

        env.barrier()

        for tt in range(TT):
            Tw = wstage[:, 0:5632]
            @B.sync
            def _(sy, tt=tt):
                if tt >= 1:
                    sy.wait_ge(env.s['DV'], env.c['DV'])
                sy.wait_ge(env.s['LG'], env.c['LG'])
                env.inc(sy.dma_start(out=Tw, in_=d["sp_T"][tt * 128:(tt + 1) * 128, :]), 'LD', 16)
            ldt = env.c['LD']
            q1 = stats[:, 49:50]; q2 = stats[:, 50:51]
            @B.vector
            def _(v, tt=tt):
                v.wait_ge(env.s['LD'], ldt)
                if tt >= 1:
                    v.wait_ge(env.s['LG'], env.c['LG'])
                v.reciprocal(q1, tmaxs[:, tt:tt + 1])
                v.drain()
                v.tensor_scalar_mul(q1, q1, 224.0)
                v.drain()
                v.tensor_scalar_mul(q2, tmaxs[:, tt:tt + 1], 1.0 / 224.0)
                v.drain()
                v.tensor_scalar_mul(f8s[:], Tw, q1)
                v.drain()
                env.inc(v.tensor_scalar_mul(stg5[:], f8s[:], q2), 'DV', 1)
            dvq = env.c['DV']
            transpose_blocks(lambda i: stg5[:, i * 128:(i + 1) * 128], NJT, None,
                             wait_dv=dvq,
                             dst_batch=(bigA, lambda i0: i0 * 128, 128))
            acq = env.c['AC']
            @B.gpsimd
            def _(gp, tt=tt):
                gp.wait_ge(env.s['AC'], acq)
                env.inc(gp.dma_start(
                    out=d["sp_qtT"].rearrange("(a p) t -> p a t", p=128)[:, :, tt * 128:(tt + 1) * 128],
                    in_=bigA[:, 0:5632].rearrange("p (a c) -> p a c", a=NJT)), 'LG', 16)

        env.barrier()

        wd_ws_dv = [0, 0]
        for hp in range(2):
            for rt in range(8):
                par = rt % 2
                ws = wstage[:, 0:5632] if par == 0 else bigC[:, 11264:22528].bitcast(F32)
                wsdv = wd_ws_dv[par]
                @B.sync
                def _(sy, hp=hp, rt=rt, ws=ws, wsdv=wsdv):
                    sy.wait_ge(env.s['DV'], wsdv)
                    if rt == 0:
                        sy.wait_ge(env.s['PE'], env.c['PE'])
                        sy.wait_ge(env.s['LG'], env.c['LG'])
                    env.inc(sy.dma_start(out=ws, in_=d["wd"][(hp * 8 + rt) * 128:(hp * 8 + rt + 1) * 128, :]), 'LD', 16)
                ldw = env.c['LD']
                wmx = stats[:, 51:52]; s1 = stats[:, 52:53]
                @B.vector
                def _(v, ldw=ldw, ws=ws):
                    v.wait_ge(env.s['LD'], ldw)
                    v.wait_ge(env.s['PE'], env.c['PE'])
                    v.reduce_max(wmx, ws, axis=AX.X, apply_absolute_value=True)
                    v.drain()
                    v.reciprocal(s1, wmx)
                    v.drain()
                    v.tensor_scalar_mul(s1, s1, 224.0)
                    v.drain()
                    v.tensor_scalar_mul(f8s[:], ws, s1)
                    v.drain()
                    v.tensor_scalar_mul(s1, wmx, 1.0 / 224.0)
                    v.drain()
                    env.inc(v.tensor_scalar_mul(stg5[:], f8s[:], s1), 'DV', 1)
                wd_ws_dv[par] = env.c['DV']
                dvq = env.c['DV']
                transpose_blocks(lambda i: stg5[:, i * 128:(i + 1) * 128], 32, None,
                                 wait_dv=dvq, evac_eng='vector',
                                 dst_batch=(bigA, lambda i0, rt=rt: i0 * 1024 + rt * 128, 1024))
                transpose_blocks(lambda i: stg5[:, (32 + i) * 128:(33 + i) * 128], 12, None,
                                 evac_eng='vector',
                                 dst_batch=(wtrans, lambda i0, rt=rt: i0 * 1024 + rt * 128, 1024))
            acst = env.c['AC']
            p7_bp = [0, 0]
            for tt in range(TT):
                @B.sync
                def _(sy, hp=hp, tt=tt):
                    sy.wait_ge(env.s['PE'], env.c['PE'])
                    sy.wait_ge(env.s['LG'], env.c['LG'])
                    env.inc(sy.dma_start(
                        out=stg5[:].rearrange("p (a c) -> p a c", a=NJT),
                        in_=d["sp_qtT"].rearrange("(a p) t -> p a t", p=128)[:, :, tt * 128:(tt + 1) * 128]), 'LD', 16)
                    env.inc(sy.dma_start(out=sx0[:], in_=x0_ap[tt * 128:(tt + 1) * 128, hp * 1024:(hp + 1) * 1024]), 'LD', 16)
                    env.inc(sy.dma_start(out=ssb[:], in_=d["sp_s"][tt * 128:(tt + 1) * 128, hp * 1024:(hp + 1) * 1024]), 'LD', 16)
                ldq = env.c['LD']
                if tt % 2 == 0:
                    pa, pb = psB[:], psC[:]
                else:
                    pa, pb = psA[:, 0:512], psA[:, 512:1024]
                bp = p7_bp[tt % 2]
                @B.tensor
                def _(t, hp=hp, tt=tt, pa=pa, pb=pb, bp=bp):
                    t.wait_ge(env.s['LD'], ldq)
                    t.wait_ge(env.s['AC'], acst)
                    if bp:
                        t.wait_ge(env.s['AC'], bp)
                    for i in range(NJT):
                        base = bigA[:, i * 1024:(i + 1) * 1024] if i < 32 else wtrans[:, (i - 32) * 1024:(i - 31) * 1024]
                        t.matmul(pa, stg5[:, i * 128:(i + 1) * 128], base[:, 0:512],
                                 start=(i == 0), stop=(i == NJT - 1))
                        env.inc(t.matmul(pb, stg5[:, i * 128:(i + 1) * 128], base[:, 512:1024],
                                         start=(i == 0), stop=(i == NJT - 1)), 'PE', 1)
                pemm = env.c['PE']
                @B.scalar
                def _(sc, pa=pa, pb=pb):
                    sc.wait_ge(env.s['PE'], pemm)
                    sc.activation(sg0[:], pa, AF.Silu, bias=0.0, scale=1.0)
                    env.inc(sc.activation(sg1[:], pb, AF.Silu, bias=0.0, scale=1.0), 'AC', 1)
                acg = env.c['AC']
                p7_bp[tt % 2] = acg
                @B.vector
                def _(v, hp=hp, tt=tt, acg=acg, ldq=ldq):
                    v.wait_ge(env.s['AC'], acg)
                    v.wait_ge(env.s['LD'], ldq)
                    v.wait_ge(env.s['LG'], env.c['LG'])
                    v.tensor_tensor(sg0[:], sg0[:], ssb[:, 0:512], op=ALU.mult)
                    v.tensor_tensor(sg1[:], sg1[:], ssb[:, 512:1024], op=ALU.mult)
                    v.drain()
                    v.tensor_tensor(sx0[:, 0:512], sx0[:, 0:512], sg0[:], op=ALU.add)
                    env.inc(v.tensor_tensor(sx0[:, 512:1024], sx0[:, 512:1024], sg1[:], op=ALU.add), 'DV', 1)
                    v.drain()
                dvx = env.c['DV']
                @B.gpsimd
                def _(gp, hp=hp, tt=tt, dvx=dvx):
                    gp.wait_ge(env.s['DV'], dvx)
                    env.inc(gp.dma_start(out=d["sp_x1"][tt * 128:(tt + 1) * 128, hp * 1024:(hp + 1) * 1024],
                                         in_=sx0[:]), 'LG', 16)

        env.barrier()

        @B.sync
        def _(sy):
            env.inc(sy.dma_start(out=wstage[0:8, 0:2048], in_=d["wr"]), 'LD', 16)
        ldwr = env.c['LD']
        @B.vector
        def _(v):
            v.wait_ge(env.s['LD'], ldwr)
            v.tensor_copy(stgb[0:8, 0:2048], wstage[0:8, 0:2048])
            v.drain()
            env.inc(v.tensor_tensor(stg5[0:8, 0:2048], wstage[0:8, 0:2048], stgb[0:8, 0:2048],
                                    op=ALU.subtract), 'DV', 1)
            v.drain()
        dvwr = env.c['DV']
        @B.tensor
        def _(t):
            t.wait_ge(env.s['DV'], dvwr)
            t.wait_ge(env.s['AC'], env.marks['pt_free'])
            for i in range(IC):
                pt = ptA if i < 8 else ptB
                t.transpose(pt[:, (i % 8) * 128:(i % 8) * 128 + 8],
                            stgb[0:8, i * 128:(i + 1) * 128], ident[0:8, 0:8])
                env.inc(t.transpose(pt[:, (i % 8) * 128 + 8:(i % 8) * 128 + 16],
                                    stg5[0:8, i * 128:(i + 1) * 128], ident[0:8, 0:8]), 'PE', 1)
        pewr = env.c['PE']
        @B.scalar
        def _(sc):
            sc.wait_ge(env.s['PE'], pewr)
            for i in range(IC):
                pt = ptA if i < 8 else ptB
                env.inc(sc.copy(wr2t[:, i * 16:(i + 1) * 16],
                                pt[:, (i % 8) * 128:(i % 8) * 128 + 16]), 'AC', 1)
        env.marks['pt_free'] = env.c['AC']

        env.barrier()
        ln_quant_phase(d["sp_x1"], 1)
        env.barrier()

        cosfull = bigC[:, 16384:20480].bitcast(F32)
        sinfull = bigC[:, 20480:24576].bitcast(F32)
        @B.sync
        def _(sy):
            env.inc(sy.dma_start(out=cosfull.rearrange("p (a c) -> p a c", a=TT),
                                 in_=d["cosT"].rearrange("(a p) c -> p a c", p=128)), 'LD', 16)
            env.inc(sy.dma_start(out=sinfull.rearrange("p (a c) -> p a c", a=TT),
                                 in_=d["sinF"].rearrange("(a p) c -> p a c", p=128)), 'LD', 16)
        ldcs = env.c['LD']

        def rope_apply(v, ps, eb, t1, tt):
            cos4 = cosfull[:, tt * 128:(tt + 1) * 128].rearrange("p (o c) -> p o c", o=1).broadcast_to([128, 4, 128])
            v.tensor_tensor(t1[:].rearrange("p (h c) -> p h c", h=4), ps[:].rearrange("p (h c) -> p h c", h=4),
                            cos4, op=ALU.mult)
            pr = ps[:].rearrange("p (h t c) -> p h t c", h=4, t=2, c=64)
            ebr = eb[:, 0:512].rearrange("p (h t c) -> p h t c", h=4, t=2, c=64)
            sin4 = sinfull[:, tt * 128:(tt + 1) * 128].rearrange("p (o t c) -> p o t c", o=1, t=2)\
                .broadcast_to([128, 4, 2, 64])
            v.tensor_tensor(ebr[:, :, 0, :], pr[:, :, 1, :], sin4[:, :, 0, :], op=ALU.mult)
            v.tensor_tensor(ebr[:, :, 1, :], pr[:, :, 0, :], sin4[:, :, 1, :], op=ALU.mult)
            v.drain()
            inst = v.tensor_tensor(eb[:, 0:512], t1[:], eb[:, 0:512], op=ALU.add)
            v.drain()
            return inst

        for oc4 in range(4):
            buf = oc4 % 2
            for i in range(IC):
                ws = wstage[:, (i % 2) * 2048:(i % 2) * 2048 + 512]
                @B.sync
                def _(sy, i=i, oc4=oc4):
                    sy.wait_ge(env.s['DV'], env.c['DV'])
                    env.inc(sy.dma_start(out=ws, in_=d["l1wq"][i * 128:(i + 1) * 128, oc4 * 512:(oc4 + 1) * 512]), 'LD', 16)
                ldw = env.c['LD']
                @B.vector
                def _(v, i=i, buf=buf, oc4=oc4):
                    v.wait_ge(env.s['LD'], ldw)
                    if oc4 >= 2:
                        v.wait_ge(env.s['PE'], env.c['PE'])
                    env.inc(v.tensor_copy(wtrans[:, buf * 8192 + i * 512: buf * 8192 + (i + 1) * 512], ws), 'DV', 1)
            dvst = env.c['DV']
            for tt in range(TT):
                ps = psB if tt % 2 == 0 else psC
                @B.tensor
                def _(t, tt=tt, oc4=oc4, buf=buf):
                    t.wait_ge(env.s['DV'], dvst)
                    if tt >= 2 or oc4 > 0:
                        t.wait_ge(env.s['DV'], env.c['DV'])
                    for i in range(IC):
                        env.inc(t.matmul(ps[:], bigA[:, i * 2048 + tt * 128: i * 2048 + tt * 128 + 128],
                                         wtrans[:, buf * 8192 + i * 512: buf * 8192 + (i + 1) * 512],
                                         start=(i == 0), stop=(i == IC - 1)), 'PE', 1)
                pemm = env.c['PE']
                eb = evb if tt % 2 == 0 else evb2
                t1 = sg0 if tt % 2 == 0 else sg1
                @B.vector
                def _(v, tt=tt, ps=ps, eb=eb, t1=t1):
                    v.wait_ge(env.s['PE'], pemm)
                    v.wait_ge(env.s['LD'], ldcs)
                    v.wait_ge(env.s['LG'], env.c['LG'])
                    env.inc(rope_apply(v, ps, eb, t1, tt), 'DV', 1)
                dvr = env.c['DV']
                @B.gpsimd
                def _(gp, tt=tt, oc4=oc4, eb=eb):
                    gp.wait_ge(env.s['DV'], dvr)
                    env.inc(gp.dma_start(out=d["sp_q"][tt * 128:(tt + 1) * 128, oc4 * 512:(oc4 + 1) * 512],
                                         in_=eb[:, 0:512]), 'LG', 16)

        for which, w_ap in (("k", d["l1wk"]), ("v", d["l1wv"])):
            for i in range(IC):
                ws = wstage[:, (i % 2) * 2048:(i % 2) * 2048 + 512]
                @B.sync
                def _(sy, i=i, w_ap=w_ap):
                    sy.wait_ge(env.s['DV'], env.c['DV'])
                    env.inc(sy.dma_start(out=ws, in_=w_ap[i * 128:(i + 1) * 128, :]), 'LD', 16)
                ldw = env.c['LD']
                @B.vector
                def _(v, i=i):
                    v.wait_ge(env.s['LD'], ldw)
                    v.wait_ge(env.s['PE'], env.c['PE'])
                    env.inc(v.tensor_copy(wtrans[:, i * 512:(i + 1) * 512], ws), 'DV', 1)
            dvst = env.c['DV']
            for tt in range(TT):
                ps = psB if tt % 2 == 0 else psC
                @B.tensor
                def _(t, tt=tt):
                    t.wait_ge(env.s['DV'], dvst)
                    if tt >= 2:
                        t.wait_ge(env.s['DV'], env.c['DV'])
                        t.wait_ge(env.s['AC'], env.c['AC'])
                    for i in range(IC):
                        env.inc(t.matmul(ps[:], bigA[:, i * 2048 + tt * 128: i * 2048 + tt * 128 + 128],
                                         wtrans[:, i * 512:(i + 1) * 512],
                                         start=(i == 0), stop=(i == IC - 1)), 'PE', 1)
                pemm = env.c['PE']
                if which == "v":
                    @B.scalar
                    def _(sc, tt=tt, ps=ps):
                        sc.wait_ge(env.s['PE'], pemm)
                        env.inc(sc.copy(bigC[:, 8192 + tt * 512: 8192 + (tt + 1) * 512], ps[:]), 'AC', 1)
                else:
                    eb = evb if tt % 2 == 0 else evb2
                    t1 = sg0 if tt % 2 == 0 else sg1
                    @B.vector
                    def _(v, tt=tt, ps=ps, eb=eb, t1=t1):
                        v.wait_ge(env.s['PE'], pemm)
                        v.wait_ge(env.s['LD'], ldcs)
                        v.wait_ge(env.s['AC'], env.c['AC'])
                        env.inc(rope_apply(v, ps, eb, t1, tt), 'DV', 1)
                    dvr = env.c['DV']
                    @B.tensor
                    def _(t, tt=tt, eb=eb):
                        t.wait_ge(env.s['DV'], dvr)
                        t.wait_ge(env.s['AC'], env.marks['pt_free'])
                        for kv in range(4):
                            pt = ptA if kv < 2 else ptB
                            env.inc(t.transpose(pt[:, (kv % 2) * 128:(kv % 2 + 1) * 128],
                                                eb[:, kv * 128:(kv + 1) * 128], ident[:]), 'PE', 1)
                    pek = env.c['PE']
                    @B.scalar
                    def _(sc, tt=tt):
                        sc.wait_ge(env.s['PE'], pek)
                        for kv in range(4):
                            pt = ptA if kv < 2 else ptB
                            env.inc(sc.copy(bigC[:, kv * 2048 + tt * 128: kv * 2048 + tt * 128 + 128],
                                            pt[:, (kv % 2) * 128:(kv % 2 + 1) * 128]), 'AC', 1)
                    env.marks['pt_free'] = env.c['AC']

        env.barrier()

        ptl1 = bigC[:, 28672:29440]
        ptl2 = bigC[:, 29440:30208]
        rT = bigC[:, 30208:30976]
        qfm_base = [24576, 26624]
        qfm_pe_mark = [0, 0]
        qfm_ld = {}
        qfm_ac = {}
        l1_pv_ac = [0, 0]
        l1_eb_lg = [0, 0]
        l1_last_dvmk = [0]
        l1_last_acex = [0]
        l1_last_trpe = [0]
        l1_pe_sc = {}

        def l1_q_load(h):
            par = h % 2
            base = qfm_base[par]
            mark = qfm_pe_mark[par]
            @B.sync
            def _(sy, h=h, base=base, mark=mark):
                sy.wait_ge(env.s['PE'], mark)
                env.inc(sy.dma_start(
                    out=bigC[:, base:base + 2048].rearrange("p (a c) -> p a c", a=TT),
                    in_=d["sp_q"].rearrange("(a p) o -> p a o", p=128)[:, :, h * 128:(h + 1) * 128]), 'LD', 16)
            qfm_ld[h] = env.c['LD']

        def l1_q_transpose(h):
            par = h % 2
            base = qfm_base[par]
            ldq = qfm_ld[h]
            def wx(t, ldq=ldq):
                t.wait_ge(env.s['LD'], ldq)
            transpose_blocks(lambda i, base=base: bigC[:, base + i * 128: base + (i + 1) * 128], TT, None,
                             wait_extra=wx, evac_eng='vector',
                             dst_batch=(bigC, lambda i0, base=base: base + i0 * 128, 128))
            qfm_ac[h] = env.c['AC']

        def l1_scores(h, qt):
            kv = h // 4
            base = qfm_base[h % 2]
            j0 = max(0, qt - 4)
            nkt = qt - j0 + 1
            dvm0, acex0 = l1_last_dvmk[0], l1_last_acex[0]
            acq = qfm_ac[h]
            @B.tensor
            def _(t, h=h, qt=qt, base=base, j0=j0, nkt=nkt, kv=kv, dvm0=dvm0, acex0=acex0, acq=acq):
                t.wait_ge(env.s['AC'], acq)
                if dvm0:
                    t.wait_ge(env.s['DV'], dvm0)
                if acex0:
                    t.wait_ge(env.s['AC'], acex0)
                for half, bb in ((0, 0), (1, 1024)):
                    lhs = bigC[half * 64:half * 64 + 64, base + qt * 128: base + (qt + 1) * 128]
                    done = 0
                    while done < nkt:
                        take = min(4, nkt - done)
                        rhs = bigC[half * 64:half * 64 + 64,
                                   kv * 2048 + (j0 + done) * 128: kv * 2048 + (j0 + done + take) * 128]
                        env.inc(t.matmul(psA[:, bb + done * 128: bb + (done + take) * 128],
                                         lhs, rhs, start=True, stop=True,
                                         tile_position=(half * 64, 0)), 'PE', 1)
                        done += take
            return env.c['PE']

        l1_q_load(0)
        l1_q_transpose(0)
        l1_q_load(1)
        l1_pe_sc[(0, 0)] = l1_scores(0, 0)
        for h in range(NH):
            kv = h // 4
            lamh = lamb[:, h:h + 1]
            for qt in range(TT):
                j0 = max(0, qt - 4)
                nkt = qt - j0 + 1
                w = nkt * 128
                if qt == 4 and h + 1 < NH:
                    l1_q_transpose(h + 1)
                pes = l1_pe_sc.pop((h, qt))
                m1 = stats[:, 54:55]; m2 = stats[:, 55:56]
                @B.vector
                def _(v, qt=qt, nkt=nkt, w=w, pes=pes):
                    v.wait_ge(env.s['PE'], pes)
                    for bb in (0, 1024):
                        v.tensor_tensor(psA[:, bb + (nkt - 1) * 128: bb + nkt * 128],
                                        psA[:, bb + (nkt - 1) * 128: bb + nkt * 128],
                                        maskc[:], op=ALU.add)
                        if qt >= 4:
                            v.tensor_tensor(psA[:, bb:bb + 128],
                                            psA[:, bb:bb + 128],
                                            maska[:], op=ALU.add)
                    v.drain()
                    v.reduce_max(m1, psA[:, 0:w], axis=AX.X)
                    v.reduce_max(m2, psA[:, 1024:1024 + w], axis=AX.X)
                    v.drain()
                    v.tensor_scalar_mul(stats[:, 56:57], m1, -0.125)
                    env.inc(v.tensor_scalar_mul(stats[:, 57:58], m2, -0.125), 'DV', 1)
                    v.drain()
                dvmk = env.c['DV']
                l1_last_dvmk[0] = dvmk
                z1 = stats[:, 58:59]; z2 = stats[:, 59:60]
                trpe = l1_last_trpe[0]
                @B.scalar
                def _(sc, w=w, dvmk=dvmk, trpe=trpe):
                    sc.wait_ge(env.s['DV'], dvmk)
                    if trpe:
                        sc.wait_ge(env.s['PE'], trpe)
                    sc.activation(ptl1[:, 0:w], psA[:, 0:w], AF.Exp,
                                  bias=stats[:, 56:57], scale=0.125, accum_out=z1)
                    env.inc(sc.activation(ptl2[:, 0:w], psA[:, 1024:1024 + w], AF.Exp,
                                          bias=stats[:, 57:58], scale=0.125, accum_out=z2), 'AC', 1)
                acex = env.c['AC']
                l1_last_acex[0] = acex

                if qt + 1 < TT:
                    l1_pe_sc[(h, qt + 1)] = l1_scores(h, qt + 1)
                elif h + 1 < NH:
                    l1_pe_sc[(h + 1, 0)] = l1_scores(h + 1, 0)

                @B.vector
                def _(v, w=w, lamh=lamh, acex=acex):
                    v.wait_ge(env.s['AC'], acex)
                    zi1 = stats[:, 60:61]; zi2 = stats[:, 61:62]
                    v.reciprocal(zi1, z1)
                    v.drain()
                    v.reciprocal(zi2, z2)
                    v.drain()
                    v.tensor_scalar_mul(ptl1[:, 0:w], ptl1[:, 0:w], zi1)
                    v.tensor_mul(zi2, zi2, lamh)
                    v.drain()
                    v.tensor_scalar_mul(zi2, zi2, -1.0)
                    v.drain()
                    v.scalar_tensor_tensor(ptl1[:, 0:w], ptl2[:, 0:w], zi2, ptl1[:, 0:w],
                                           op0=ALU.mult, op1=ALU.add)
                    v.drain()
                    ds = stats[:, 62:63]
                    v.tensor_scalar(ptl1[:, 0:w], ptl1[:, 0:w], 0.0, 0.0,
                                    op0=ALU.max, op1=ALU.add, accum_out=ds)
                    v.drain()
                    v.tensor_scalar_add(ds, ds, 1e-6)
                    v.drain()
                    ri = stats[:, 63:64]
                    v.reciprocal(ri, ds)
                    v.drain()
                    env.inc(v.tensor_scalar_mul(ptl1[:, 0:w], ptl1[:, 0:w], ri), 'DV', 1)
                    v.drain()
                dvr = env.c['DV']
                @B.tensor
                def _(t, nkt=nkt, dvr=dvr):
                    t.wait_ge(env.s['DV'], dvr)
                    t.wait_ge(env.s['AC'], env.marks['pt_free'])
                    for j in range(nkt):
                        pt = ptA if j < 4 else ptB
                        env.inc(t.transpose(pt[:, (j % 4) * 128:(j % 4 + 1) * 128],
                                            ptl1[:, j * 128:(j + 1) * 128], ident[:]), 'PE', 1)
                per = env.c['PE']
                l1_last_trpe[0] = per
                @B.vector
                def _(sc, nkt=nkt, per=per):
                    sc.wait_ge(env.s['PE'], per)
                    for j in range(nkt):
                        pt = ptA if j < 4 else ptB
                        env.inc(sc.tensor_copy(rT[:, j * 128:(j + 1) * 128],
                                               pt[:, (j % 4) * 128:(j % 4 + 1) * 128]), 'AC', 1)
                env.marks['pt_free'] = env.c['AC']
                acr = env.c['AC']
                ps = psB if qt % 2 == 0 else psC
                bp = l1_pv_ac[qt % 2]
                @B.tensor
                def _(t, qt=qt, h=h, j0=j0, nkt=nkt, kv=kv, ps=ps, acr=acr, bp=bp):
                    t.wait_ge(env.s['AC'], acr)
                    if bp:
                        t.wait_ge(env.s['AC'], bp)
                    for j in range(nkt):
                        kt = j0 + j
                        env.inc(t.matmul(ps[:, 0:128],
                                         bigC[:, 8192 + kt * 512 + kv * 128: 8192 + kt * 512 + (kv + 1) * 128],
                                         rT[:, j * 128:(j + 1) * 128],
                                         start=(j == 0), stop=(j == nkt - 1)), 'PE', 1)
                pepv = env.c['PE']
                eb = evb if qt % 2 == 0 else evb2
                eblg = l1_eb_lg[qt % 2]
                @B.vector
                def _(sc, ps=ps, eb=eb, pepv=pepv, eblg=eblg):
                    sc.wait_ge(env.s['PE'], pepv)
                    if eblg:
                        sc.wait_ge(env.s['LG'], eblg)
                    env.inc(sc.tensor_copy(eb[:, 0:128], ps[:, 0:128]), 'AC', 1)
                acact = env.c['AC']
                l1_pv_ac[qt % 2] = acact
                @B.gpsimd
                def _(gp, h=h, qt=qt, eb=eb, acact=acact):
                    gp.wait_ge(env.s['AC'], acact)
                    env.inc(gp.dma_start(out=d["sp_ao"][h * 128:(h + 1) * 128, qt * 128:(qt + 1) * 128],
                                         in_=eb[:, 0:128]), 'LG', 16)
                l1_eb_lg[qt % 2] = env.c['LG']
            qfm_pe_mark[h % 2] = env.c['PE']
            if h + 2 < NH:
                l1_q_load(h + 2)

        env.barrier()

        for i in range(IC):
            ws = wstage[:, (i % 2) * 2048:(i % 2 + 1) * 2048]
            @B.sync
            def _(sy, i=i):
                if i >= 2:
                    sy.wait_ge(env.s['DV'], env.c['DV'])
                env.inc(sy.dma_start(out=ws, in_=d["l1wo"][i * 128:(i + 1) * 128, :]), 'LD', 16)
            ldw = env.c['LD']
            @B.vector
            def _(v, i=i):
                v.wait_ge(env.s['LD'], ldw)
                env.inc(v.tensor_copy(bigC[:, i * 2048:(i + 1) * 2048], ws), 'DV', 1)
        dvwo = env.c['DV']
        for tt in range(TT):
            aosl = stgb
            @B.sync
            def _(sy, tt=tt):
                sy.wait_ge(env.s['PE'], env.c['PE'])
                sy.wait_ge(env.s['LG'], env.c['LG'])
                env.inc(sy.dma_start(
                    out=aosl.rearrange("p (a c) -> p a c", a=16),
                    in_=d["sp_ao"].rearrange("(a p) t -> p a t", p=128)[:, :, tt * 128:(tt + 1) * 128]), 'LD', 16)
            ldao = env.c['LD']
            for oc4 in range(4):
                ps = psB if oc4 % 2 == 0 else psC
                @B.tensor
                def _(t, tt=tt, oc4=oc4, ps=ps):
                    t.wait_ge(env.s['LD'], ldao)
                    t.wait_ge(env.s['DV'], dvwo)
                    if oc4 >= 2 or tt > 0:
                        t.wait_ge(env.s['AC'], env.c['AC'])
                    for hh in range(16):
                        env.inc(t.matmul(ps[:], aosl[:, hh * 128:(hh + 1) * 128],
                                         bigC[:, hh * 2048 + oc4 * 512: hh * 2048 + (oc4 + 1) * 512],
                                         start=(hh == 0), stop=(hh == 15)), 'PE', 1)
                pemm = env.c['PE']
                eb = evb if oc4 % 2 == 0 else evb2
                @B.scalar
                def _(sc, ps=ps, eb=eb):
                    sc.wait_ge(env.s['PE'], pemm)
                    sc.wait_ge(env.s['LG'], env.c['LG'])
                    env.inc(sc.copy(eb[:, 0:512], ps[:]), 'AC', 1)
                acs2 = env.c['AC']
                @B.gpsimd
                def _(gp, tt=tt, oc4=oc4, eb=eb):
                    gp.wait_ge(env.s['AC'], acs2)
                    env.inc(gp.dma_start(out=d["sp_s"][tt * 128:(tt + 1) * 128, oc4 * 512:(oc4 + 1) * 512],
                                         in_=eb[:, 0:512]), 'LG', 16)

        env.barrier()

        @B.vector
        def _(v):
            env.inc(v.memset(bigC[:], 0.0), 'DV', 1)
            v.drain()
        moe_steps = [(e, fc) for e in range(8) for fc in range(4)]
        moe_strip_ac = {}
        moe_ws_dv = [0, 0]
        moe_stg_pe = [0, 0]
        moe_psum_dv = [0, 0]

        def moe_prep(si):
            e, fc = moe_steps[si]
            buf = si % 2
            for rt in range(4):
                par = rt % 2
                ws = wstage[:, par * 2048:(par + 1) * 2048]
                sg = stgb if par == 0 else stgb2
                wsdv = moe_ws_dv[par]
                @B.sync
                def _(sy, e=e, fc=fc, rt=rt, wsdv=wsdv):
                    sy.wait_ge(env.s['DV'], wsdv)
                    env.inc(sy.dma_start(out=ws, in_=d["wex"][e * 2048 + fc * 512 + rt * 128: e * 2048 + fc * 512 + (rt + 1) * 128, :]), 'LD', 16)
                ldw = env.c['LD']
                stpe = moe_stg_pe[par]
                @B.vector
                def _(v, ldw=ldw, stpe=stpe, sg=sg):
                    v.wait_ge(env.s['LD'], ldw)
                    v.wait_ge(env.s['PE'], stpe)
                    env.inc(v.tensor_copy(sg[:], ws), 'DV', 1)
                moe_ws_dv[par] = env.c['DV']
                dvc = env.c['DV']
                transpose_blocks(lambda i, sg=sg: sg[:, i * 128:(i + 1) * 128], IC, None,
                                 wait_dv=dvc,
                                 dst_batch=(wtrans, lambda i0, buf=buf, rt=rt: buf * 8192 + i0 * 512 + rt * 128, 512))
                moe_stg_pe[par] = env.c['PE']
            moe_strip_ac[si] = env.c['AC']

        def moe_prep_rt(si, rt):
            e, fc = moe_steps[si]
            buf = si % 2
            par = rt % 2
            ws = wstage[:, par * 2048:(par + 1) * 2048]
            sg = stgb if par == 0 else stgb2
            wsdv = moe_ws_dv[par]
            @B.sync
            def _(sy, e=e, fc=fc, rt=rt, wsdv=wsdv):
                sy.wait_ge(env.s['DV'], wsdv)
                env.inc(sy.dma_start(out=ws, in_=d["wex"][e * 2048 + fc * 512 + rt * 128: e * 2048 + fc * 512 + (rt + 1) * 128, :]), 'LD', 16)
            ldw = env.c['LD']
            stpe = moe_stg_pe[par]
            @B.vector
            def _(v, ldw=ldw, stpe=stpe, sg=sg):
                v.wait_ge(env.s['LD'], ldw)
                v.wait_ge(env.s['PE'], stpe)
                env.inc(v.tensor_copy(sg[:], ws), 'DV', 1)
            moe_ws_dv[par] = env.c['DV']
            dvc = env.c['DV']
            transpose_blocks(lambda i, sg=sg: sg[:, i * 128:(i + 1) * 128], IC, None,
                             wait_dv=dvc,
                             dst_batch=(wtrans, lambda i0, buf=buf, rt=rt: buf * 8192 + i0 * 512 + rt * 128, 512))
            moe_stg_pe[par] = env.c['PE']
            if rt == 3:
                moe_strip_ac[si] = env.c['AC']

        moe_prep(0)
        for si, (e, fc) in enumerate(moe_steps):
            buf = si % 2
            acst = moe_strip_ac[si]
            for tt in range(TT):
                if tt % 4 == 0 and si + 1 < len(moe_steps):
                    moe_prep_rt(si + 1, tt // 4)
                ps = psB if tt % 2 == 0 else psC
                bp = moe_psum_dv[tt % 2]
                @B.tensor
                def _(t, tt=tt, si=si, buf=buf, acst=acst, bp=bp):
                    t.wait_ge(env.s['AC'], acst)
                    if bp:
                        t.wait_ge(env.s['DV'], bp)
                    for i in range(IC):
                        env.inc(t.matmul(ps[:], bigA[:, i * 2048 + tt * 128: i * 2048 + tt * 128 + 128],
                                         wtrans[:, buf * 8192 + i * 512: buf * 8192 + (i + 1) * 512],
                                         start=(i == 0), stop=(i == IC - 1)), 'PE', 1)
                pemm = env.c['PE']
                @B.vector
                def _(v, tt=tt, fc=fc, e=e, ps=ps):
                    v.wait_ge(env.s['PE'], pemm)
                    env.inc(v.scalar_tensor_tensor(
                        bigC[:, tt * 2048 + fc * 512: tt * 2048 + (fc + 1) * 512],
                        ps[:], selm[:, tt * 8 + e: tt * 8 + e + 1],
                        bigC[:, tt * 2048 + fc * 512: tt * 2048 + (fc + 1) * 512],
                        op0=ALU.mult, op1=ALU.add), 'DV', 1)
                moe_psum_dv[tt % 2] = env.c['DV']

        env.barrier()

        for tt in range(TT):
            for oc4 in range(4):
                k = tt * 4 + oc4
                gsb = sg0 if k % 2 == 0 else sg1
                @B.sync
                def _(sy, tt=tt, oc4=oc4, k=k):
                    if k > 0:
                        sy.wait_ge(env.s['OT'], env.c['OT'])
                        sy.wait_ge(env.s['DV'], env.c['DV'])
                    sy.wait_ge(env.s['LG'], env.c['LG'])
                    env.inc(sy.dma_start(out=sx0[:, 0:512], in_=d["sp_x1"][tt * 128:(tt + 1) * 128, oc4 * 512:(oc4 + 1) * 512]), 'LD', 16)
                    env.inc(sy.dma_start(out=ssb[:, 0:512], in_=d["sp_s"][tt * 128:(tt + 1) * 128, oc4 * 512:(oc4 + 1) * 512]), 'LD', 16)
                ldxs = env.c['LD']
                @B.scalar
                def _(sc, tt=tt, oc4=oc4, gsb=gsb):
                    sc.wait_ge(env.s['LD'], ldxs)
                    sc.wait_ge(env.s['DV'], env.c['DV'])
                    env.inc(sc.activation(gsb[:], bigC[:, tt * 2048 + oc4 * 512: tt * 2048 + (oc4 + 1) * 512],
                                          AF.Silu, bias=0.0, scale=1.0), 'AC', 1)
                acg = env.c['AC']
                @B.vector
                def _(v, gsb=gsb):
                    v.wait_ge(env.s['AC'], acg)
                    v.wait_ge(env.s['LD'], ldxs)
                    v.tensor_tensor(gsb[:], gsb[:], ssb[:, 0:512], op=ALU.mult)
                    v.drain()
                    env.inc(v.tensor_tensor(sx0[:, 0:512], sx0[:, 0:512], gsb[:], op=ALU.add), 'DV', 1)
                    v.drain()
                dvx = env.c['DV']
                @B.gpsimd
                def _(gp, tt=tt, oc4=oc4):
                    gp.wait_ge(env.s['DV'], dvx)
                    env.inc(gp.dma_start(out=out_ap[tt * 128:(tt + 1) * 128, oc4 * 512:(oc4 + 1) * 512],
                                         in_=sx0[:, 0:512]), 'OT', 16)

        @B.gpsimd
        def _(gp):
            gp.wait_ge(env.s['OT'], env.c['OT'])

    return nc


_NC_CACHE = [None]


def kernel(**inputs):
    f32 = lambda a: np.ascontiguousarray(np.asarray(a, dtype=np.float32))
    x = f32(inputs["L0_attn_x"])
    hd = 128
    inv_freq = (1.0 / (10000.0 ** (np.arange(0, hd, 2, dtype=np.float32) / np.float32(hd)))).astype(np.float32)
    t = np.arange(S, dtype=np.float32)
    freqs = np.outer(t, inv_freq).astype(np.float32)
    emb = np.concatenate([freqs, freqs], axis=-1)
    cosT = np.ascontiguousarray(np.cos(emb).astype(np.float32))
    sinT = np.sin(emb).astype(np.float32)
    sinF = sinT.copy()
    sinF[:, :64] = -sinF[:, :64]
    sinF = np.ascontiguousarray(sinF)

    dn = np.ascontiguousarray(np.asarray(inputs["L0_ffn_down_reorder_index"], dtype=np.int32).reshape(IFF, 1))

    common = {
        "l0wq": f32(inputs["L0_attn_Wq"]), "l0wk": f32(inputs["L0_attn_Wk"]),
        "l0wv": f32(inputs["L0_attn_Wv"]), "l0wo": f32(inputs["L0_attn_Wo"]),
        "wg": f32(inputs["L0_ffn_w_gate"]), "wu": f32(inputs["L0_ffn_w_up"]),
        "wd": f32(inputs["L0_ffn_w_down"]), "dnidx": dn,
        "l1wq": f32(inputs["L1_attn_Wq"]), "l1wk": f32(inputs["L1_attn_Wk"]),
        "l1wv": f32(inputs["L1_attn_Wv"]), "l1wo": f32(inputs["L1_attn_Wo"]),
        "lam": f32(inputs["L1_attn_lam"]).reshape(1, NH),
        "wr": f32(inputs["L1_ffn_w_router"]),
        "wex": f32(inputs["L1_ffn_w_experts"]).reshape(8 * D, D),
        "cosT": cosT, "sinF": sinF,
    }
    if _NC_CACHE[0] is None:
        _NC_CACHE[0] = build_nc()
    nc = _NC_CACHE[0]
    in_maps = [dict(common, x0=np.ascontiguousarray(x[c])) for c in range(8)]
    res = run_bass_kernel_spmd(nc, in_maps, core_ids=list(range(8)))
    return np.stack([r["out"] for r in res.results], axis=0)
